# version 10
# baseline (speedup 1.0000x reference)
import numpy as np
import concourse.bass as bass
import concourse.mybir as mybir
import bass_rust as _br
from concourse.bass_utils import run_bass_kernel_spmd
from concourse.aot_env import allow_ici_all_to_all

F32 = mybir.dt.float32
BF16 = mybir.dt.bfloat16
AF = mybir.ActivationFunctionType
AL = mybir.AluOpType
AX = mybir.AxisListType.X

NCORES = 8
D = 2048
E = 4096
ES = E // NCORES
R = 128
NS = 16
L = 8192
DFF = 8192
NCH = 8
BLK = 128
CH_T = NCORES * BLK
DSH = D // NCORES
FSH = DFF // NCORES
MAGIC = 12582912.0
RG = [list(range(NCORES))]

DEBUG = False
STAGE = 5


def build_program():
    nc = bass.Bass("TRN2", target_bir_lowering=False, debug=False)
    ti = {}

    def din(name, shape, dtype=F32):
        ti[name] = nc.dram_tensor(name, list(shape), dtype, kind="ExternalInput")
        return ti[name]

    din("resid", (NCH, BLK, D))
    din("cvec", (128, 16))
    din("wmod_sl", (D, 1536))
    din("norm_w", (1, D))
    din("conv_w", (128, 16))
    din("conv_b", (128, 4))
    din("A_log", (128, 64))
    din("W_D", (128, 4))
    din("b_d2", (128, 4))
    din("W_d1_sl", (R, ES))
    din("W_d2_sl", (ES, R))
    din("W_Bm_sl", (NS, ES))
    din("W_Cm_sl", (NS, ES))
    din("W_in_sl", (E, DSH))
    din("W_skip_sl", (E, DSH))
    din("W_out_sl", (D, ES))
    din("w1_sl", (DFF, DSH))
    din("w2_sl", (DFF, DSH))
    din("w3_sl", (D, FSH))
    out_t = nc.dram_tensor("out", [NCH, BLK, D], F32, kind="ExternalOutput")
    dbg = {}
    if DEBUG:
        if STAGE >= 1:
            dbg["xsilu"] = nc.dram_tensor("d_xsilu", [4, 128, L], F32, kind="ExternalOutput")
        if STAGE >= 2:
            dbg["dt"] = nc.dram_tensor("d_dt", [4, 128, L], F32, kind="ExternalOutput")
        if STAGE >= 3:
            dbg["y"] = nc.dram_tensor("d_y", [4, 128, L], F32, kind="ExternalOutput")
        if STAGE >= 4:
            dbg["x2"] = nc.dram_tensor("d_x2", [NCH, BLK, D], F32, kind="ExternalOutput")

    dram = {}
    shared_out = {"ago_wi", "ago_ws", "ago_wo", "ago_w1", "ago_w2", "ago_w3",
                  "ago_mod", "aro_ws", "aro_p"}
    for nm, shape, dt_ in (
        ("agi_wi", [DSH, E], BF16), ("ago_wi", [D, E], BF16),
        ("agi_ws", [DSH, E], BF16), ("ago_ws", [D, E], BF16),
        ("agi_wo", [ES, D], BF16), ("ago_wo", [E, D], BF16),
        ("agi_w1", [DSH, DFF], BF16), ("ago_w1", [D, DFF], BF16),
        ("agi_w2", [DSH, DFF], BF16), ("ago_w2", [D, DFF], BF16),
        ("agi_w3", [FSH, D], BF16), ("ago_w3", [DFF, D], BF16),
        ("agi_mod", [1, 1536], F32), ("ago_mod", [1, 12288], F32),
        ("ari_ws", [128, 4], F32), ("aro_ws", [128, 4], F32),
        ("modv_d", [6, D], F32),
        ("a2a_fi", [NCH, NCORES, ES, BLK], BF16),
        ("a2a_fo", [NCH, NCORES, ES, BLK], BF16),
        ("ari_p", [NCH, 160, CH_T], F32), ("aro_p", [NCH, 160, CH_T], F32),
        ("bct_d", [NCH, 32, CH_T], BF16),
        ("a2a_bi", [NCH, NCORES, ES, BLK], BF16),
        ("a2a_bo", [NCH, NCORES, ES, BLK], BF16),
        ("h_sp", [NCH, BLK, D], BF16),
        ("skip_d", [32, 128, CH_T], BF16),
    ):
        dram[nm] = nc.dram_tensor(
            nm, shape, dt_,
            addr_space="Shared" if nm in shared_out else "Local")
    agi_wi, ago_wi = dram["agi_wi"], dram["ago_wi"]
    agi_ws, ago_ws = dram["agi_ws"], dram["ago_ws"]
    agi_wo, ago_wo = dram["agi_wo"], dram["ago_wo"]
    agi_w1, ago_w1 = dram["agi_w1"], dram["ago_w1"]
    agi_w2, ago_w2 = dram["agi_w2"], dram["ago_w2"]
    agi_w3, ago_w3 = dram["agi_w3"], dram["ago_w3"]
    agi_mod, ago_mod = dram["agi_mod"], dram["ago_mod"]
    ari_ws, aro_ws = dram["ari_ws"], dram["aro_ws"]
    modv_d = dram["modv_d"]
    a2a_fi, a2a_fo = dram["a2a_fi"], dram["a2a_fo"]
    ari_p, aro_p = dram["ari_p"], dram["aro_p"]
    bct_d = dram["bct_d"]
    a2a_bi, a2a_bo = dram["a2a_bi"], dram["a2a_bo"]
    h_sp, skip_d = dram["h_sp"], dram["skip_d"]

    groups = {}

    def alloc(group, name, shape, dtype):
        cm = nc.sbuf_tensor(name, list(shape), dtype)
        h = cm.__enter__()
        groups.setdefault(group, []).append(cm)
        return h

    def free_group(group):
        for cm in reversed(groups.pop(group, [])):
            cm.__exit__(None, None, None)

    psum_cms = []

    def psum(name, shape, dtype=F32):
        cm = nc.psum_tensor(name, list(shape), dtype)
        psum_cms.append(cm)
        return cm.__enter__()

    sems = {}
    cnt = {}
    sem_cms = []

    def sem(name):
        cm = nc.semaphore(name)
        sem_cms.append(cm)
        sems[name] = cm.__enter__()
        cnt[name] = 0

    def inc(instr, name, k=1):
        instr.then_inc(sems[name], k)
        cnt[name] += k
        return instr

    marks = {}

    DMA_SEMS = ("gld", "sld", "syn", "gst", "outd", "dbgd", "sst")

    def mark(name, *semnames):
        marks[name] = {s: cnt[s] for s in semnames}
        for s in semnames:
            if s in DMA_SEMS:
                dma_barrier[s] = cnt[s]

    def waitm(eng, *names):
        for name in names:
            for s, n in marks[name].items():
                if n > 0:
                    eng.wait_ge(sems[s], n)

    def wait_cur(eng, semname):
        if cnt[semname] > 0:
            eng.wait_ge(sems[semname], cnt[semname])

    dma_barrier = {}
    dma_applied = {}

    def _dma_guard(eng, semname):
        b = dma_barrier.get(semname, 0)
        if b > dma_applied.get(semname, 0):
            eng.wait_ge(sems[semname], b)
            dma_applied[semname] = b

    def DMA(eng, semname, out, in_):
        _dma_guard(eng, semname)
        return inc(eng.dma_start(out=out, in_=in_), semname, 16)

    def DMAT(eng, semname, out, in_):
        _dma_guard(eng, semname)
        return inc(eng.dma_start_transpose(out, in_), semname, 16)

    for s in ("gld", "sld", "syn", "cc", "pe", "dv", "sc", "gst", "gp", "outd",
              "dbgd", "sst"):
        sem(s)

    bank_free = [None] * 8

    def pe_take_bank(t, i):
        if bank_free[i]:
            s, n = bank_free[i]
            if n > 0:
                t.wait_ge(sems[s], n)

    def bank_release(i, semname):
        bank_free[i] = (semname, cnt[semname])

    P = "persist"
    cw_s = alloc(P, "cw_s", [128, 16], F32)
    cb_s = alloc(P, "cb_s", [128, 4], F32)
    wd_s = alloc(P, "wd_s", [128, 4], F32)
    bd2_s = alloc(P, "bd2_s", [128, 4], F32)
    a_neg = alloc(P, "a_neg", [128, 64], F32)
    wsc = alloc(P, "wsc", [128, 8], F32)
    wd1T = alloc(P, "wd1T", [128, 4, 128], BF16)
    wd2T = alloc(P, "wd2T", [128, 4, 128], BF16)
    wbmT = alloc(P, "wbmT", [128, 4, NS], BF16)
    wcmT = alloc(P, "wcmT", [128, 4, NS], BF16)
    cst = alloc(P, "cst", [128, 8], F32)
    carry_h = alloc(P, "carry_h", [128, 64], F32)
    carry_x = alloc(P, "carry_x", [128, 4, 4], BF16)
    tiny = alloc(P, "tiny", [128, 8], F32)

    SK = "skgrp"
    sk_st = alloc(SK, "sk_st", [128, 2, 512], BF16)
    FL = "frontlong"
    wq_f = alloc(FL, "wq_f", [128, 16, 512], BF16)
    xnT = alloc(FL, "xnT", [128, 16, NCH * BLK], BF16)
    EA = "earlyA"
    modf = alloc(EA, "modf", [128, 2, D], BF16)
    normw_bc = alloc(EA, "normw_bc", [128, D], BF16)
    xt_f = alloc(EA, "xt_f", [128, 2, D], F32)
    hb_f = alloc(EA, "hb_f", [128, 2, D], BF16)
    xn_f = alloc(EA, "xn_f", [128, 2, D], BF16)
    st_f = alloc(EA, "st_f", [128, 64], F32)
    scr_sc = alloc(EA, "scr_sc", [128, 1024], BF16)
    stgf = alloc(EA, "stgf", [128, 2, 4, 2, 512], BF16)

    pb = [psum(f"pb{i}", [128, 512], F32) for i in range(8)]

    blk_cm = nc.Block()
    block = blk_cm.__enter__()

    @block.vector
    def _(v):
        v.memset(cst[:, 0:1], 1e-5)
        v.memset(cst[:, 1:2], 1e-6)
        v.memset(cst[:, 2:3], MAGIC)
        v.memset(cst[:, 3:4], 1.0 / D)
        v.memset(cst[:, 4:5], 1.0 / DFF)
        v.memset(cst[:, 5:6], -1.0)
        v.memset(cst[:, 6:7], 1.0)
        v.memset(carry_h[:, :], 0.0)
        a = v.memset(tiny[:, :], 0.0)
        inc(a, "dv", 1)
    mark("cst", "dv")

    PR = "prep"
    pw = alloc(PR, "pw", [128, 64, DSH], BF16)
    pt_s = alloc(PR, "pt_s", [128, 2, 8192], BF16)
    wm_s = alloc(PR, "wm_s", [128, 16, 128], F32)
    psc_f = alloc(PR, "psc_f", [1, 1536], F32)
    silc = alloc(PR, "silc", [128, 16], F32)
    pabs = alloc(PR, "pabs", [128, 8], F32)
    ones_f = alloc(PR, "ones_f", [128, 128], F32)
    pqs = alloc(PR, "pqs", [128, 1024], F32)

    @block.gpsimd
    def _(g):
        DMA(g, "gld", pw[:, 0:32, :],
            ti["W_in_sl"].ap().rearrange("(c p) m -> p c m", p=128))
        DMA(g, "gld", pw[:, 32:64, :],
            ti["W_skip_sl"].ap().rearrange("(c p) m -> p c m", p=128))
    mark("pw_fr", "gld")

    @block.sync
    def _(s):
        waitm(s, "pw_fr")
        for c in range(64):
            half, cc = divmod(c, 32)
            tgt = pt_s[:, :, half * 4096 + cc * 128: half * 4096 + (cc + 1) * 128]
            DMAT(s, "syn", tgt, pw[:, c, :])
        s.wait_ge(sems["syn"], cnt["syn"])
        DMA(s, "syn", agi_wi.ap().rearrange("(q p) e -> p q e", p=128),
            pt_s[:, :, 0:4096])
        DMA(s, "syn", agi_ws.ap().rearrange("(q p) e -> p q e", p=128),
            pt_s[:, :, 4096:8192])
    mark("agi_fr", "syn")

    @block.gpsimd
    def _(g):
        waitm(g, "agi_fr")
        c0 = g.collective_compute("AllGather", AL.bypass, replica_groups=RG,
                                  ins=[agi_wi.ap().opt()], outs=[ago_wi.ap().opt()])
        inc(c0, "cc", 1)
        c1 = g.collective_compute("AllGather", AL.bypass, replica_groups=RG,
                                  ins=[agi_ws.ap().opt()], outs=[ago_ws.ap().opt()])
        inc(c1, "cc", 1)
    mark("ag_fr", "cc")

    @block.gpsimd
    def _(g):
        DMA(g, "gld", silc[:, :], ti["cvec"].ap())
    mark("cvld", "gld")

    @block.scalar
    def _(sc):
        waitm(sc, "cvld", "cst")
        a = sc.activation(silc[:, :], silc[:, :], AF.Silu)
        inc(a, "sc", 1)
    mark("silc", "sc")

    for j in range(12):
        @block.gpsimd
        def _(g, j=j):
            if j > 0:
                g.wait_ge(sems["pe"], marks[f"modmm{j - 1}"]["pe"])
            DMA(g, "gld", wm_s[:, :, :],
                ti["wmod_sl"].ap()
                .rearrange("(k p) m -> p k m", p=128)[:, :, j * 128:(j + 1) * 128])
        mark(f"wmld{j}", "gld")

        @block.tensor
        def _(t, j=j):
            waitm(t, "silc", f"wmld{j}")
            if j >= 3:
                t.wait_ge(sems["dv"], marks[f"modcp{j - 3}"]["dv"])
            for k in range(16):
                last = t.matmul(pb[j % 3][0:1, 0:128], silc[:, k:k + 1], wm_s[:, k, :],
                                start=(k == 0), stop=(k == 15))
            inc(last, "pe", 1)
        mark(f"modmm{j}", "pe")

        @block.vector
        def _(v, j=j):
            waitm(v, f"modmm{j}")
            last = v.tensor_copy(psc_f[0:1, j * 128:(j + 1) * 128], pb[j % 3][0:1, 0:128])
            inc(last, "dv", 1)
        mark(f"modcp{j}", "dv")
    mark("modcp", "dv")
    modcp_dv = cnt["dv"]
    for _b in range(3):
        bank_release(_b, "dv")

    @block.gpsimd
    def _(g):
        waitm(g, "modcp")
        DMA(g, "gld", agi_mod.ap(), psc_f[0:1, 0:1536])
        wait_cur(g, "gld")
        c1 = g.collective_compute("AllGather", AL.bypass, replica_groups=RG,
                                  ins=[agi_mod.ap().opt()], outs=[ago_mod.ap().opt()])
        inc(c1, "cc", 1)
    mark("modex", "cc")

    @block.gpsimd
    def _(g):
        waitm(g, "modex")
        m6 = ago_mod.ap().rearrange("one (s d) -> one s d", s=6)[0]
        DMA(g, "gld", modf[:, :, :], m6[0:2][None, :, :].broadcast_to((128, 2, D)))
        DMA(g, "gld", normw_bc[:, :], ti["norm_w"].ap().broadcast_to((128, D)))
    mark("modbc", "gld")

    @block.gpsimd
    def _(g):
        waitm(g, "agi_fr")
        DMA(g, "gld", cw_s[:, :], ti["conv_w"].ap())
        DMA(g, "gld", cb_s[:, :], ti["conv_b"].ap())
        DMA(g, "gld", wd_s[:, :], ti["W_D"].ap())
        DMA(g, "gld", bd2_s[:, :], ti["b_d2"].ap())
        DMA(g, "gld", a_neg[:, :], ti["A_log"].ap())
        DMA(g, "gld", pw[:, 0:4, 0:128],
            ti["W_d1_sl"].ap().rearrange("r (gg pp) -> r gg pp", pp=128))
        DMA(g, "gld", pw[:, 4:8, 0:128],
            ti["W_d2_sl"].ap().rearrange("(gg p) r -> p gg r", p=128))
        DMA(g, "gld", pw[0:NS, 8:12, 0:128],
            ti["W_Bm_sl"].ap().rearrange("n (gg p) -> n gg p", p=128))
        DMA(g, "gld", pw[0:NS, 12:16, 0:128],
            ti["W_Cm_sl"].ap().rearrange("n (gg p) -> n gg p", p=128))
    mark("smallw", "gld")

    @block.sync
    def _(s):
        waitm(s, "smallw")
        for gg in range(4):
            DMAT(s, "syn", wd1T[:, gg:gg + 1, :], pw[:, gg, 0:128])
            DMAT(s, "syn", wd2T[:, gg:gg + 1, :], pw[:, 4 + gg, 0:128])
            DMAT(s, "syn", wbmT[:, gg:gg + 1, :], pw[0:NS, 8 + gg, 0:128])
            DMAT(s, "syn", wcmT[:, gg:gg + 1, :], pw[0:NS, 12 + gg, 0:128])
    mark("smallT", "syn")

    @block.scalar
    def _(sc):
        waitm(sc, "smallw")
        a = sc.activation(a_neg[:, :], a_neg[:, :], AF.Exp)
        inc(a, "sc", 1)
    mark("aexp", "sc")

    @block.vector
    def _(v):
        waitm(v, "aexp")
        a = v.tensor_scalar(a_neg[:, :], a_neg[:, :], -1.0, None, AL.mult)
        inc(a, "dv", 1)
    mark("aneg", "dv")

    def prep_plain(src_name, agi, nrows_sb, width, label):
        nch = {"wo": 16}[label]

    @block.gpsimd
    def _(g):
        waitm(g, "smallT")
        pwv = pw.ap().rearrange("p c m -> p (c m)")[:, 0:16 * 512] \
            .rearrange("p (c m) -> p c m", m=512)
        DMA(g, "gld", pwv,
            ti["W_out_sl"].ap().rearrange("(c p) m -> p c m", p=128))
    mark("pwo", "gld")

    @block.sync
    def _(s):
        waitm(s, "pwo")
        pt2 = pt_s.ap().rearrange("p q m -> p (q m)").rearrange("p (f j) -> p f j", f=4)
        pwv = pw.ap().rearrange("p c m -> p (c m)")[:, 0:16 * 512] \
            .rearrange("p (c m) -> p c m", m=512)
        for c in range(16):
            DMAT(s, "syn", pt2[:, :, c * 128:(c + 1) * 128], pwv[:, c, :])
        s.wait_ge(sems["syn"], cnt["syn"])
        DMA(s, "syn", agi_wo.ap().rearrange("(f p) d2 -> p f d2", p=128),
            pt2[:, :, 0:2048])
    mark("agi_wo", "syn")

    @block.gpsimd
    def _(g):
        waitm(g, "agi_wo")
        c1 = g.collective_compute("AllGather", AL.bypass, replica_groups=RG,
                                  ins=[agi_wo.ap().opt()], outs=[ago_wo.ap().opt()])
        inc(c1, "cc", 1)
    mark("ag_wo", "cc")


    mark("prep_done", "syn", "gld", "dv", "sc", "gst", "sld", "cc")

    if STAGE >= 5:
        for i, wn in enumerate(("w1_sl", "w2_sl", "w3_sl")):
            @block.gpsimd
            def _(g, i=i, wn=wn):
                if i == 0:
                    waitm(g, "agi_wo")
                else:
                    waitm(g, f"wabs{i - 1}")
                DMA(g, "gld", pw[:, 0:64, :],
                    ti[wn].ap().rearrange("(c p) m -> p c m", p=128))
            mark(f"pq{i + 1}", "gld")

            @block.vector
            def _(v, i=i):
                waitm(v, f"pq{i + 1}")
                a = v.reduce_sum(pabs[:, i:i + 1],
                                 pw.ap().rearrange("p c m -> p (c m)"),
                                 axis=AX, apply_absolute_value=True)
                inc(a, "dv", 1)
                if i == 2:
                    wait_cur(v, "dv")
                    a = v.memset(pabs[:, 3:4], 0.0)
                    inc(a, "dv", 1)
            mark(f"wabs{i}", "dv")

        @block.gpsimd
        def _(g):
            waitm(g, "wabs2")
            DMA(g, "gld", ari_ws.ap(), pabs[:, 0:4])
            wait_cur(g, "gld")
            c1 = g.collective_compute("AllReduce", AL.add, replica_groups=RG,
                                      ins=[ari_ws.ap().opt()],
                                      outs=[aro_ws.ap().opt()])
            inc(c1, "cc", 1)
            wait_cur(g, "cc")
            g.wait_ge(sems["pe"], marks["modmm11"]["pe"])
            _dma_guard(g, "gld")
            inc(g.dma_start(out=ones_f[0:4, 0:128],
                            in_=aro_ws.ap().rearrange("p w -> w p"),
                            allow_slow_non_contiguous=True), "gld", 16)
        mark("wsT", "gld")

        @block.vector
        def _(v):
            waitm(v, "wsT")
            a = v.reduce_sum(pabs[0:4, 4:5], ones_f[0:4, 0:128], axis=AX)
            inc(a, "dv", 1)
            wait_cur(v, "dv")
            a = v.tensor_scalar(pabs[0:4, 5:6], pabs[0:4, 4:5],
                                1.0 / (DFF * D), 1e-5, AL.mult, AL.max)
            inc(a, "dv", 1)
            wait_cur(v, "dv")
            a = v.reciprocal(pabs[0:4, 6:7], pabs[0:4, 5:6])
            inc(a, "dv", 1)
        mark("ws4", "dv")

        @block.gpsimd
        def _(g):
            waitm(g, "ws4")
            wv = ari_ws.ap()[0:2, :].rearrange("a b -> (a b)") \
                .rearrange("(w c) -> w c", c=2)
            DMA(g, "gld", wv, pabs[0:4, 5:7])
            wait_cur(g, "gld")
            _dma_guard(g, "gld")
            inc(g.dma_start(out=wsc[:, 0:3],
                            in_=wv[0:3, 0][None, :].broadcast_to((128, 3)),
                            allow_slow_non_contiguous=True), "gld", 16)
            inc(g.dma_start(out=wsc[:, 4:7],
                            in_=wv[0:3, 1][None, :].broadcast_to((128, 3)),
                            allow_slow_non_contiguous=True), "gld", 16)
            wait_cur(g, "gld")
        mark("wscal", "gld")
        mark("pq3", "gld")

    for c in range(NCH):
        po = 32 * (c % 2)

        @block.scalar
        def _(sc, c=c):
            if c >= 2:
                sc.wait_ge(sems["dv"], marks[f"xn{c - 2}"]["dv"])
            DMA(sc, "sld", xt_f[:, c % 2, :], ti["resid"].ap()[c, :, :])
        mark(f"res{c}", "sld")

        @block.vector
        def _(v, c=c, po=po):
            waitm(v, f"res{c}", "modbc")
            if c >= 2:
                waitm(v, f"hbu{c - 2}")
            x = xt_f[:, c % 2, :]
            for s4 in range(4):
                a = v.bn_stats(st_f[:, po + 6 * s4: po + 6 * (s4 + 1)],
                               x[:, 512 * s4: 512 * (s4 + 1)])
            inc(a, "dv", 1)
            wait_cur(v, "dv")
            a = v.bn_aggr(st_f[:, po + 24: po + 26], st_f[:, po: po + 24])
            inc(a, "dv", 1)
        mark(f"st{c}", "dv")

        @block.scalar
        def _(sc, c=c, po=po):
            waitm(sc, f"st{c}")
            a = sc.activation(st_f[:, po + 26: po + 27], st_f[:, po + 25: po + 26],
                              AF.Sqrt, bias=cst[:, 0:1])
            inc(a, "sc", 1)
        mark(f"sqv{c}", "sc")

        @block.vector
        def _(v, c=c, po=po):
            waitm(v, f"sqv{c}")
            a = v.reciprocal(st_f[:, po + 27: po + 28], st_f[:, po + 26: po + 27])
            inc(a, "dv", 1)
            wait_cur(v, "dv")
            x = xt_f[:, c % 2, :]
            hb = hb_f[:, c % 2, :]
            tmp = xn_f[:, c % 2, :]
            if c >= 2:
                v.wait_ge(sems["syn"], marks[f"xnT{c - 2}"]["syn"])
            a = v.tensor_scalar(hb, x, st_f[:, po + 24: po + 25],
                                st_f[:, po + 27: po + 28], AL.subtract, AL.mult)
            inc(a, "dv", 1)
            wait_cur(v, "dv")
            a = v.tensor_tensor(tmp, hb, modf[:, 1, :], AL.mult)
            inc(a, "dv", 1)
            wait_cur(v, "dv")
            a = v.tensor_tensor(hb, hb, tmp, AL.add)
            inc(a, "dv", 1)
            wait_cur(v, "dv")
            a = v.tensor_tensor(hb, hb, modf[:, 0, :], AL.add)
            inc(a, "dv", 1)
        mark(f"hb{c}", "dv")

        @block.scalar
        def _(sc, c=c, po=po):
            waitm(sc, f"hb{c}")
            DMA(sc, "sld", h_sp.ap()[c, :, :], hb_f[:, c % 2, :])
            a = sc.activation(scr_sc[:, :], hb_f[:, c % 2, 0:1024], AF.Square,
                              accum_out=st_f[:, po + 28: po + 29])
            inc(a, "sc", 1)
            wait_cur(sc, "sc")
            a = sc.activation(scr_sc[:, :], hb_f[:, c % 2, 1024:2048], AF.Square,
                              accum_out=st_f[:, po + 31: po + 32])
            inc(a, "sc", 1)
        mark(f"hbsq{c}", "sc", "sld")

        @block.vector
        def _(v, c=c, po=po):
            waitm(v, f"hbsq{c}")
            a = v.tensor_tensor(st_f[:, po + 28: po + 29],
                                st_f[:, po + 28: po + 29],
                                st_f[:, po + 31: po + 32], AL.add)
            inc(a, "dv", 1)
        mark(f"hbss{c}", "dv")

        @block.scalar
        def _(sc, c=c, po=po):
            sc.wait_ge(sems["dv"], marks[f"hbss{c}"]["dv"])
            a = sc.activation(st_f[:, po + 29: po + 30], st_f[:, po + 28: po + 29],
                              AF.Sqrt, bias=cst[:, 0:1], scale=cst[:, 3:4])
            inc(a, "sc", 1)
        mark(f"hbu{c}", "sc", "sld")

        @block.vector
        def _(v, c=c, po=po):
            waitm(v, f"hbu{c}")
            a = v.reciprocal(st_f[:, po + 30: po + 31], st_f[:, po + 29: po + 30])
            inc(a, "dv", 1)
            wait_cur(v, "dv")
            xn = xn_f[:, c % 2, :]
            a = v.tensor_scalar(xn, hb_f[:, c % 2, :], st_f[:, po + 30: po + 31],
                                None, AL.mult)
            inc(a, "dv", 1)
            wait_cur(v, "dv")
            a = v.tensor_tensor(xn, xn, normw_bc[:, :], AL.mult)
            inc(a, "dv", 1)
        mark(f"xn{c}", "dv")

        @block.sync
        def _(s, c=c):
            waitm(s, f"xn{c}")
            DMAT(s, "syn", xnT[:, :, c * BLK:(c + 1) * BLK], xn_f[:, c % 2, :])
        mark(f"xnT{c}", "syn")

    def front_pass(weight_dram, to_skip):
        tag = "ws" if to_skip else "wi"
        for q in range(8):
            @block.scalar
            def _(sc, q=q):
                prevq = f"fmmq_{tag}_{q - 1}" if q > 0 else ("fmmq_wi_7" if to_skip else None)
                if prevq:
                    sc.wait_ge(sems["pe"], marks[prevq]["pe"])
                DMA(sc, "sld", wq_f[:, :, :],
                    weight_dram.ap().rearrange("(k p) e -> p k e", p=128)[:, :, q * 512:(q + 1) * 512])
            mark(f"fwq_{tag}_{q}", "sld")

            for cg in range(2):
                for ec in range(4):
                    bank = ec % 2 + (2 if to_skip else 0)

                    @block.tensor
                    def _(t, q=q, cg=cg, ec=ec, bank=bank):
                        waitm(t, f"fwq_{tag}_{q}")
                        if q == 0 and cg == 0 and ec == 0:
                            waitm(t, "ag_fr", f"xnT{NCH - 1}")
                            t.wait_ge(sems["dv"], modcp_dv)
                        pe_take_bank(t, bank)
                        for k in range(16):
                            last = t.matmul(pb[bank][:, :],
                                            wq_f[:, k, ec * 128:(ec + 1) * 128],
                                            xnT[:, k, cg * 512:(cg + 1) * 512],
                                            start=(k == 0), stop=(k == 15))
                        inc(last, "pe", 1)
                    mark(f"fmm_{tag}_{q}_{cg}_{ec}", "pe")

                    @block.scalar
                    def _(sc, q=q, cg=cg, ec=ec, bank=bank):
                        waitm(sc, f"fmm_{tag}_{q}_{cg}_{ec}")
                        if to_skip:
                            par = ec % 2
                            pm = marks.get(f"sksd_{tag}_{q}_{cg}_{ec - 2}") or \
                                marks.get(f"sksd_{tag}_{q}_{cg - 1}_{2 + ec}") or \
                                marks.get(f"sksd_{tag}_{q - 1}_1_{2 + ec}")
                            if pm:
                                sc.wait_ge(sems["sld"], pm["sld"])
                            a = sc.activation(sk_st[:, par, :], pb[bank][:, :], AF.Copy)
                            inc(a, "sc", 1)
                            bank_release(bank, "sc")
                            wait_cur(sc, "sc")
                            DMA(sc, "sld",
                                skip_d.ap()[q * 4 + ec, :, cg * 512:(cg + 1) * 512],
                                sk_st[:, par, :])
                            mark(f"sksd_{tag}_{q}_{cg}_{ec}", "sld")
                        else:
                            pm = marks.get(f"a2ast_{q - 2}_{cg}")
                            if pm:
                                sc.wait_ge(sems["sst"], pm["sst"])
                            a = sc.activation(stgf[:, q % 2, ec, cg, :],
                                              pb[bank][:, :], AF.Copy)
                            inc(a, "sc", 1)
                            bank_release(bank, "sc")
                mark(f"fevg_{tag}_{q}_{cg}", "sc", "sld")
            mark(f"fmmq_{tag}_{q}", "pe")

            if not to_skip:
                for cg in range(2):
                    @block.scalar
                    def _(sc, q=q, cg=cg):
                        sc.wait_ge(sems["sc"], marks[f"fevg_wi_{q}_{cg}"]["sc"])
                        for cb in range(4):
                            c = cg * 4 + cb
                            DMA(sc, "sst",
                                a2a_fi.ap()[c].rearrange(
                                    "j (eo p) t -> p j eo t", p=128)[:, q],
                                stgf[:, q % 2, :, cg, cb * 128:(cb + 1) * 128])
                    mark(f"a2ast_{q}_{cg}", "sst")

    front_pass(ago_wi, to_skip=False)

    for c in range(NCH):
        @block.gpsimd
        def _(g, c=c):
            cg = c // 4
            g.wait_ge(sems["sst"], marks[f"a2ast_7_{cg}"]["sst"])
            with allow_ici_all_to_all():
                c1 = g.collective_compute("AllToAll", AL.bypass, replica_groups=RG,
                                          ins=[a2a_fi.ap()[c].opt()],
                                          outs=[a2a_fo.ap()[c].opt()])
            inc(c1, "cc", 1)
        mark(f"a2af_{c}", "cc")

    mark("front_free", "gst", "syn", "dv", "sc", "sld")
    front_pass(ago_ws, to_skip=True)
    mark("front_done", "pe", "sc", "sld")

    if STAGE >= 5:
        def prep_ffn(w_name, agi, ago, scol, nch, width, mark_pw, tag,
                     load=True, prev_tag=None):
            pwv = pw.ap().rearrange("p c m -> p (c m)")[:, 0:nch * width] \
                .rearrange("p (c m) -> p c m", m=width)

            if load:
                @block.gpsimd
                def _(g):
                    waitm(g, mark_pw)
                    if prev_tag:
                        waitm(g, f"agi_{prev_tag}")
                    DMA(g, "gld", pwv,
                        ti[w_name].ap().rearrange("(c p) m -> p c m", p=128))
                mark(f"ld_{tag}", "gld")

            flat = pw.ap().rearrange("p c m -> p (c m)")
            for piece in range(16):
                sl = slice(piece * 1024, (piece + 1) * 1024)

                @block.gpsimd
                def _(g, sl=sl, piece=piece):
                    if piece == 0:
                        if load:
                            waitm(g, f"ld_{tag}", "wscal")
                        else:
                            waitm(g, "wscal")
                    else:
                        wait_cur(g, "gp")
                    a = g.tensor_scalar(pqs[:, :], flat[:, sl],
                                        wsc[:, 4 + scol:5 + scol], None,
                                        AL.mult)
                    inc(a, "gp", 1)
                    wait_cur(g, "gp")
                    a = g.tensor_scalar(pqs[:, :], pqs[:, :], MAGIC, None,
                                        AL.add)
                    inc(a, "gp", 1)
                    wait_cur(g, "gp")
                    a = g.tensor_scalar(pqs[:, :], pqs[:, :], -MAGIC, None,
                                        AL.add)
                    inc(a, "gp", 1)
                    wait_cur(g, "gp")
                    a = g.tensor_scalar(flat[:, sl], pqs[:, :], -1.0, 1.0,
                                        AL.max, AL.min)
                    inc(a, "gp", 1)
            mark(f"q_{tag}", "gp")

            nf = width // 128
            pt2 = pt_s.ap().rearrange("p q m -> p (q m)")[:, 0:nf * nch * 128] \
                .rearrange("p (f j) -> p f j", f=nf)

            @block.sync
            def _(s):
                waitm(s, f"q_{tag}")
                for c in range(nch):
                    DMAT(s, "syn", pt2[:, :, c * 128:(c + 1) * 128], pwv[:, c, :])
                s.wait_ge(sems["syn"], cnt["syn"])
                DMA(s, "syn", agi.ap().rearrange("(f p) x -> p f x", p=128), pt2)
            mark(f"agi_{tag}", "syn")


        prep_ffn("w3_sl", agi_w3, ago_w3, 2, 16, FSH, "wsum", "w3", load=False)
        prep_ffn("w1_sl", agi_w1, ago_w1, 0, 64, DSH, "q_w3", "w1",
                 prev_tag="w3")
        prep_ffn("w2_sl", agi_w2, ago_w2, 1, 64, DSH, "q_w1", "w2",
                 prev_tag="w1")
    mark("front_free", "gst", "syn", "dv", "sc", "sld", "gp", "gld")
    mark("front_done", "pe", "sc", "sld")

    free_group(PR)
    free_group(EA)
    SCg = "scan"
    xcat = alloc(SCg, "xcat", [128, 4, 4 + CH_T], BF16)
    xsilu = alloc(SCg, "xsilu", [128, 2, 4, CH_T], BF16)
    dt_t = alloc(SCg, "dt_t", [128, 4, CH_T], BF16)
    dt1f = alloc(SCg, "dt1f", [128, CH_T], BF16)
    bbc = alloc(SCg, "bbc", [128, NS, 512], BF16)
    cbc = alloc(SCg, "cbc", [128, NS, 512], BF16)
    dub = alloc(SCg, "dub", [128, 2, 512], BF16)
    abuf = alloc(SCg, "abuf", [128, NS, 514], BF16)
    bbuf = alloc(SCg, "bbuf", [128, NS, 514], BF16)
    hprod = alloc(SCg, "hprod", [128, NS, 514], BF16)
    y_t = alloc(SCg, "y_t", [128, 4, CH_T], BF16)
    yo_t = alloc(SCg, "yo_t", [128, 4, CH_T], BF16)
    sp0 = alloc(SCg, "sp0", [128, 4, 512], BF16)
    sp1 = alloc(SCg, "sp1", [128, 4, 512], BF16)
    prj_s = alloc(SCg, "prj_s", [128, CH_T], F32)
    prj_bc = alloc(SCg, "prj_bc", [48, CH_T], F32)

    @block.vector
    def _(v):
        waitm(v, "front_free")
        v.memset(abuf[:, :, 512:513], 0.0)
        v.memset(abuf[:, :, 513:514], 1.0)
        a = v.memset(bbuf[:, :, 512:514], 0.0)
        inc(a, "dv", 1)
    mark("abz", "dv")

    def scan_front(c):
        xpar = c % 2
        xs = xsilu[:, xpar]

        @block.scalar
        def _(sc, c=c):
            waitm(sc, f"a2af_{c}")
            if c == 0:
                waitm(sc, "front_free")
            if c >= 1:
                sc.wait_ge(sems["dv"], marks[f"conv{c - 1}"]["dv"])
            for gg in range(4):
                DMA(sc, "sld",
                    xcat[:, gg, 4:4 + CH_T].rearrange("p (j t) -> p j t", t=BLK),
                    a2a_fo.ap()[c].rearrange("j (g p) t -> p g j t", p=128)[:, gg])
        mark(f"xm{c}", "sld")

        @block.vector
        def _(v, c=c, xs=xs):
            waitm(v, f"xm{c}", "aneg")
            if c == 0:
                waitm(v, "front_free")
                a = v.memset(xcat[:, :, 1:4], 0.0)
            else:
                a = v.tensor_copy(xcat[:, :, 1:4], carry_x[:, :, 0:3])
            inc(a, "dv", 1)
            if c >= 2:
                v.wait_ge(sems["dv"], marks[f"ytot{c - 2}"]["dv"])
            wait_cur(v, "dv")
            for gg in range(4):
                acc = xs[:, gg, :]
                a = v.tensor_scalar(acc, xcat[:, gg, 1:1 + CH_T],
                                    cw_s[:, 4 * gg:4 * gg + 1], None, AL.mult)
                inc(a, "dv", 1)
                wait_cur(v, "dv")
                for k in (1, 2, 3):
                    a = v.scalar_tensor_tensor(acc, xcat[:, gg, 1 + k:1 + k + CH_T],
                                               cw_s[:, 4 * gg + k:4 * gg + k + 1],
                                               acc, AL.mult, AL.add)
                    inc(a, "dv", 1)
                    wait_cur(v, "dv")
            a = v.tensor_copy(carry_x[:, :, 0:3], xcat[:, :, 1 + CH_T:4 + CH_T])
            inc(a, "dv", 1)
        mark(f"conv{c}", "dv")

        @block.scalar
        def _(sc, c=c, xs=xs):
            waitm(sc, f"conv{c}")
            for gg in range(4):
                a = sc.activation(xs[:, gg, :], xs[:, gg, :], AF.Silu,
                                  bias=cb_s[:, gg:gg + 1])
            inc(a, "sc", 1)
        mark(f"xsl{c}", "sc")

        for half in range(2):
            @block.tensor
            def _(t, c=c, half=half, xs=xs):
                t.wait_ge(sems["sc"], marks[f"xsl{c}"]["sc"])
                if c == 0 and half == 0:
                    waitm(t, "smallT")
                for b in (4, 5, 6):
                    pe_take_bank(t, b)
                sl = slice(half * 512, (half + 1) * 512)
                for gg in range(4):
                    t.matmul(pb[4][:, :], wd1T[:, gg, :], xs[:, gg, sl],
                             start=(gg == 0), stop=(gg == 3))
                for gg in range(4):
                    t.matmul(pb[5][0:NS, :], wbmT[:, gg, :], xs[:, gg, sl],
                             start=(gg == 0), stop=(gg == 3))
                for gg in range(4):
                    last = t.matmul(pb[6][0:NS, :], wcmT[:, gg, :], xs[:, gg, sl],
                                    start=(gg == 0), stop=(gg == 3))
                inc(last, "pe", 1)
            mark(f"prjmm{c}_{half}", "pe")

            @block.vector
            def _(v, c=c, half=half):
                waitm(v, f"prjmm{c}_{half}")
                if c > 0 and half == 0:
                    v.wait_ge(sems["sst"], marks[f"arst{c - 1}"]["sst"])
                sl = slice(half * 512, (half + 1) * 512)
                v.tensor_copy(prj_s[:, sl], pb[4][:, :])
                v.tensor_copy(prj_bc[0:NS, sl], pb[5][0:NS, :])
                a = v.tensor_copy(prj_bc[32:32 + NS, sl], pb[6][0:NS, :])
                inc(a, "dv", 1)
            mark(f"prjcp{c}_{half}", "dv")
            for b in (4, 5, 6):
                bank_release(b, "dv")

        @block.scalar
        def _(sc, c=c):
            sc.wait_ge(sems["dv"], marks[f"prjcp{c}_1"]["dv"])
            if c == 0:
                waitm(sc, "front_free")
            DMA(sc, "sst", ari_p.ap()[c, 0:128, :], prj_s[:, :])
            DMA(sc, "sst", ari_p.ap()[c, 128:144, :], prj_bc[0:NS, :])
            DMA(sc, "sst", ari_p.ap()[c, 144:160, :], prj_bc[32:32 + NS, :])
        mark(f"arst{c}", "sst")

        @block.gpsimd
        def _(g, c=c):
            g.wait_ge(sems["sst"], marks[f"arst{c}"]["sst"])
            c1 = g.collective_compute("AllReduce", AL.add, replica_groups=RG,
                                      ins=[ari_p.ap()[c].opt()],
                                      outs=[aro_p.ap()[c].opt()])
            inc(c1, "cc", 1)
        mark(f"ar{c}", "cc")

    def scan_back(c):
        xpar = c % 2
        xs = xsilu[:, xpar]

        @block.gpsimd
        def _(g, c=c):
            waitm(g, f"ar{c}")
            if c > 0:
                g.wait_ge(sems["pe"], marks[f"dtmm{c - 1}_3_1"]["pe"])
            DMA(g, "gld", dt1f[:, :], aro_p.ap()[c, 0:128, :])
            DMA(g, "gld", bct_d.ap()[c], aro_p.ap()[c, 128:160, :])
        mark(f"arp{c}", "gld")

        for half in range(2):
            for gg in range(4):
                bk = 6 + (gg * 2 + half) % 2

                @block.tensor
                def _(t, c=c, gg=gg, half=half, bk=bk):
                    waitm(t, f"arp{c}")
                    pe_take_bank(t, bk)
                    last = t.matmul(pb[bk][:, :], wd2T[:, gg, :],
                                    dt1f[:, half * 512:(half + 1) * 512],
                                    start=True, stop=True)
                    inc(last, "pe", 1)
                mark(f"dtmm{c}_{gg}_{half}", "pe")

                @block.vector
                def _(v, c=c, gg=gg, half=half, bk=bk):
                    waitm(v, f"dtmm{c}_{gg}_{half}")
                    if gg == 0:
                        pm = marks.get(f"dtsp{c}_{half - 1}") or \
                            marks.get(f"dtsp{c - 1}_1")
                        if pm:
                            v.wait_ge(sems["dv"], pm["dv"])
                    a = v.tensor_scalar(sp0[:, gg, :], pb[bk][:, :],
                                        bd2_s[:, gg:gg + 1], None, AL.add)
                    inc(a, "dv", 1)
                    bank_release(bk, "dv")
            mark(f"dtpre{c}_{half}", "dv")

            @block.scalar
            def _(sc, c=c, half=half):
                waitm(sc, f"dtpre{c}_{half}")
                s0 = sp0.ap().rearrange("p g t -> p (g t)")
                s1 = sp1.ap().rearrange("p g t -> p (g t)")
                a = sc.activation(s1, s0, AF.Abs)
                inc(a, "sc", 1)
                wait_cur(sc, "sc")
                a = sc.activation(s1, s1, AF.Exp, scale=cst[:, 5:6])
                inc(a, "sc", 1)
                wait_cur(sc, "sc")
                a = sc.activation(s1, s1, AF.Ln, bias=cst[:, 6:7])
                inc(a, "sc", 1)
            mark(f"dtln{c}_{half}", "sc")

            @block.vector
            def _(v, c=c, half=half):
                waitm(v, f"dtln{c}_{half}")
                if c > 0 and half == 0:
                    v.wait_ge(sems["dv"], marks[f"scanl{c - 1}"]["dv"])
                sl = slice(half * 512, (half + 1) * 512)
                a = v.scalar_tensor_tensor(
                    dt_t[:, :, sl], sp0[:, :, :], 0.0, sp1[:, :, :],
                    AL.max, AL.add)
                inc(a, "dv", 1)
            mark(f"dtsp{c}_{half}", "dv")
        mark(f"dtsp{c}", "dv")

        for half in range(2):
            sl = slice(half * 512, (half + 1) * 512)

            @block.scalar
            def _(sc, c=c, half=half):
                waitm(sc, f"arp{c}")
                pm = marks.get(f"scanh{c}_{half - 1}") or \
                    marks.get(f"scanh{c - 1}_1")
                if pm:
                    sc.wait_ge(sems["dv"], pm["dv"])
                DMA(sc, "sld", bbc[:, :, :],
                    bct_d.ap()[c, 0:NS, sl][None, :, :]
                    .broadcast_to((128, NS, 512)))
                DMA(sc, "sld", cbc[:, :, :],
                    bct_d.ap()[c, NS:2 * NS, sl][None, :, :]
                    .broadcast_to((128, NS, 512)))
            mark(f"bcld{c}_{half}", "sld")

            for gg in range(4):
                @block.vector
                def _(v, c=c, half=half, gg=gg, sl=sl, xs=xs):
                    if gg == 0:
                        waitm(v, f"bcld{c}_{half}")
                        waitm(v, f"dtsp{c}_{half}")
                        if half == 0:
                            pm = marks.get(f"bwdst{c - 1}")
                            if pm:
                                v.wait_ge(sems["sst"], pm["sst"])
                    a = v.tensor_tensor(dub[:, gg % 2, :], dt_t[:, gg, sl],
                                        xs[:, gg, sl], AL.mult)
                    inc(a, "dv", 1)
                mark(f"du{c}_{half}_{gg}", "dv")

                for nh in range(2):
                    @block.scalar
                    def _(sc, c=c, half=half, gg=gg, sl=sl, nh=nh):
                        if gg == 0 and half == 0 and nh == 0:
                            waitm(sc, f"dtsp{c}")
                            if c == 0:
                                waitm(sc, "abz")
                        pm = marks.get(f"scne{c}_{half}_{gg - 1}") or \
                            marks.get(f"scne{c}_{half - 1}_3") or \
                            marks.get(f"scne{c - 1}_1_3")
                        if pm:
                            sc.wait_ge(sems["dv"], pm["dv"])
                        for n in range(8 * nh, 8 * nh + 8):
                            a = sc.activation(abuf[:, n, 0:512], dt_t[:, gg, sl],
                                              AF.Exp,
                                              scale=a_neg[:, gg * 16 + n:gg * 16 + n + 1])
                        inc(a, "sc", 1)
                    mark(f"aex{c}_{half}_{gg}_{nh}", "sc")

                @block.vector
                def _(v, c=c, half=half, gg=gg, sl=sl):
                    waitm(v, f"aex{c}_{half}_{gg}_0", f"du{c}_{half}_{gg}")
                    a = v.tensor_tensor(bbuf[:, 0:8, 0:512],
                                        dub[:, gg % 2, :][:, None, :]
                                        .broadcast_to((128, 8, 512)),
                                        bbc[:, 0:8, :], AL.mult)
                    inc(a, "dv", 1)
                    waitm(v, f"aex{c}_{half}_{gg}_1")
                    a = v.tensor_tensor(bbuf[:, 8:16, 0:512],
                                        dub[:, gg % 2, :][:, None, :]
                                        .broadcast_to((128, 8, 512)),
                                        bbc[:, 8:16, :], AL.mult)
                    inc(a, "dv", 1)
                    a = v.tensor_copy(bbuf[:, 0:15, 512:513],
                                      carry_h[:, gg * 16 + 1: gg * 16 + 16]
                                      [:, :, None])
                    inc(a, "dv", 1)
                    wait_cur(v, "dv")
                    a = v.tensor_tensor_scan(
                        hprod.ap().rearrange("p n t -> p (n t)"),
                        abuf.ap().rearrange("p n t -> p (n t)"),
                        bbuf.ap().rearrange("p n t -> p (n t)"),
                        carry_h[:, gg * 16: gg * 16 + 1],
                        AL.mult, AL.add)
                    inc(a, "dv", 1)
                    mark(f"scne{c}_{half}_{gg}", "dv")
                    wait_cur(v, "dv")
                    a = v.tensor_copy(carry_h[:, gg * 16: gg * 16 + 16],
                                      hprod[:, :, 511:512]
                                      .rearrange("p n one -> p (n one)"))
                    inc(a, "dv", 1)
                    wait_cur(v, "dv")
                    a = v.tensor_tensor(hprod[:, :, 0:512], hprod[:, :, 0:512],
                                        cbc[:, :, :], AL.mult)
                    inc(a, "dv", 1)
                    mark(f"scnh{c}_{half}_{gg}_0", "dv")
                    mark(f"scnh{c}_{half}_{gg}_1", "dv")
                    wait_cur(v, "dv")
                    a = v.tensor_tensor(hprod[:, 0:8, 0:512], hprod[:, 0:8, 0:512],
                                        hprod[:, 8:16, 0:512], AL.add)
                    inc(a, "dv", 1)
                    wait_cur(v, "dv")
                    a = v.tensor_tensor(hprod[:, 0:4, 0:512], hprod[:, 0:4, 0:512],
                                        hprod[:, 4:8, 0:512], AL.add)
                    inc(a, "dv", 1)
                    wait_cur(v, "dv")
                    a = v.tensor_tensor(hprod[:, 0:2, 0:512], hprod[:, 0:2, 0:512],
                                        hprod[:, 2:4, 0:512], AL.add)
                    inc(a, "dv", 1)
                    wait_cur(v, "dv")
                    a = v.tensor_tensor(y_t[:, gg, sl], hprod[:, 0, 0:512],
                                        hprod[:, 1, 0:512], AL.add)
                    inc(a, "dv", 1)
                mark(f"scang{c}_{half}_{gg}", "dv")
            mark(f"scanh{c}_{half}", "dv")
        mark(f"scanl{c}", "dv")

        @block.vector
        def _(v, c=c, xs=xs):
            wait_cur(v, "dv")
            for gg in range(4):
                a = v.scalar_tensor_tensor(yo_t[:, gg, :], xs[:, gg, :],
                                           wd_s[:, gg:gg + 1], y_t[:, gg, :],
                                           AL.mult, AL.add)
                inc(a, "dv", 1)
        mark(f"ytot{c}", "dv")

        @block.scalar
        def _(sc, c=c):
            sc.wait_ge(sems["dv"], marks[f"ytot{c}"]["dv"])
            for gg in range(4):
                DMA(sc, "sst",
                    a2a_bi.ap()[c].rearrange("j (g p) t -> p g j t", p=128)[:, gg],
                    yo_t[:, gg, :].rearrange("p (j t) -> p j t", t=BLK))
        mark(f"bwdst{c}", "sst")

        @block.gpsimd
        def _(g, c=c):
            g.wait_ge(sems["sst"], marks[f"bwdst{c}"]["sst"])
            with allow_ici_all_to_all():
                c1 = g.collective_compute("AllToAll", AL.bypass, replica_groups=RG,
                                          ins=[a2a_bi.ap()[c].opt()],
                                          outs=[a2a_bo.ap()[c].opt()])
            inc(c1, "cc", 1)
        mark(f"a2ab_{c}", "cc")

    for c in range(NCH):
        scan_front(c)
        if c >= 1:
            scan_back(c - 1)
    scan_back(NCH - 1)

    if STAGE >= 4:
        mark("scan_all", "dv", "sc", "pe", "cc", "gst", "sld", "syn", "gld",
             "gp", "dbgd")
        if STAGE >= 5:
            for _tag, _agi, _ago in (("w3", agi_w3, ago_w3),
                                     ("w1", agi_w1, ago_w1),
                                     ("w2", agi_w2, ago_w2)):
                @block.gpsimd
                def _(g, _tag=_tag, _agi=_agi, _ago=_ago):
                    waitm(g, f"agi_{_tag}")
                    c1 = g.collective_compute(
                        "AllGather", AL.bypass, replica_groups=RG,
                        ins=[_agi.ap().opt()], outs=[_ago.ap().opt()])
                    inc(c1, "cc", 1)
                mark(f"ag_{_tag}", "cc")
        free_group(SCg)
        free_group(FL)
        BK = "back"
        modb = alloc(BK, "modb", [128, 4, D], BF16)
        bscr = alloc(BK, "bscr", [128, 2, D], BF16)
        fscr = alloc(BK, "fscr", [128, D], F32)
        bwk = alloc(BK, "bwk", [128, D], BF16)
        x2b = alloc(BK, "x2b", [128, 4, D], BF16)
        qsb = alloc(BK, "qsb", [128, 4, 16], F32)
        outb = alloc(BK, "outb", [128, 2, 512], F32)
        st2 = alloc(BK, "st2", [128, 32], F32)
        xqT2 = alloc(BK, "xqT2", [128, 16, 4 * BLK], BF16)
        hqT0v = xqT2.ap().rearrange("p a b -> p (a b)") \
            .rearrange("p (f j) -> p f j", j=BLK)

        for s in ("wl0", "wl1", "wl2", "wl3"):
            sem(s)

        @block.gpsimd
        def _(g):
            waitm(g, "scan_all")
            m6 = ago_mod.ap().rearrange("one (s d) -> one s d", s=6)[0]
            DMA(g, "gld", modb[:, :, :],
                m6[2:6][None, :, :].broadcast_to((128, 4, D)))
        mark("modbk", "gld")

        @block.scalar
        def _(sc):
            waitm(sc, "modbk", "scan_all")
            sc.activation(bscr[:, 0, :], modb[:, 0, :], AF.Sign)
            a = sc.activation(bscr[:, 1, :], modb[:, 3, :], AF.Sign)
            inc(a, "sc", 1)
            wait_cur(sc, "sc")
            sc.activation(modb[:, 0, :], modb[:, 0, :], AF.Abs)
            a = sc.activation(modb[:, 3, :], modb[:, 3, :], AF.Abs)
            inc(a, "sc", 1)
        mark("gsign", "sc")

        @block.vector
        def _(v):
            waitm(v, "gsign", "scan_all")
            v.tensor_scalar(modb[:, 0, :], modb[:, 0, :], 0.25, None, AL.max)
            a = v.tensor_scalar(modb[:, 3, :], modb[:, 3, :], 0.25, None, AL.max)
            inc(a, "dv", 1)
            wait_cur(v, "dv")
            v.tensor_tensor(modb[:, 0, :], modb[:, 0, :], bscr[:, 0, :], AL.mult)
            a = v.tensor_tensor(modb[:, 3, :], modb[:, 3, :], bscr[:, 1, :], AL.mult)
            inc(a, "dv", 1)
        mark("gclamp", "dv")

        for half in range(2):
            BA = f"backA{half}"
            yr4 = alloc(BA, f"yr4_{half}", [128, 4, 32, BLK], BF16)
            ssb = alloc(BA, f"ssb_{half}", [128, 32, BLK], BF16)
            mix4 = alloc(BA, f"mix4_{half}", [128, 4, D], BF16)
            wob = alloc(BA, f"wob_{half}", [128, 4, 512], BF16)

            for ci in range(4):
                c = half * 4 + ci

                @block.scalar
                def _(sc, c=c, ci=ci, half=half):
                    waitm(sc, f"a2ab_{c}")
                    if half == 0 and ci == 0:
                        waitm(sc, "gclamp")
                    if half == 1 and ci == 0:
                        waitm(sc, f"bkdone_0")
                    if ci > 0:
                        sc.wait_ge(sems["dv"], marks[f"yf{half}_{ci - 1}"]["dv"])
                    for gg in range(4):
                        DMA(sc, "sld",
                            yr4[:, ci, gg:32:4, :],
                            a2a_bo.ap()[c].rearrange("j (g p) t -> p g j t",
                                                     p=128)[:, gg])
                    DMA(sc, "sld", ssb[:, :, :],
                        skip_d.ap()[:, :, c * BLK:(c + 1) * BLK]
                        .rearrange("e p t -> p e t"))
                mark(f"yrl{half}_{ci}", "sld")

                @block.scalar
                def _(sc, ci=ci, half=half):
                    waitm(sc, f"yrl{half}_{ci}")
                    a = sc.activation(ssb.ap().rearrange("p e t -> p (e t)"),
                                      ssb.ap().rearrange("p e t -> p (e t)"),
                                      AF.Silu)
                    inc(a, "sc", 1)
                mark(f"ss{half}_{ci}", "sc")

                @block.vector
                def _(v, ci=ci, half=half):
                    waitm(v, f"ss{half}_{ci}")
                    a = v.tensor_tensor(
                        yr4.ap()[:, ci].rearrange("p e t -> p (e t)"),
                        yr4.ap()[:, ci].rearrange("p e t -> p (e t)"),
                        ssb.ap().rearrange("p e t -> p (e t)"), AL.mult)
                    inc(a, "dv", 1)
                mark(f"yf{half}_{ci}", "dv")

            for dgrp in range(4):
                for ec in range(32):
                    wl = f"wl{ec % 4}"

                    @block.scalar
                    def _(sc, dgrp=dgrp, ec=ec, wl=wl, half=half):
                        gi = dgrp * 32 + ec - 4
                        if gi >= 0:
                            pm = marks.get(f"womm{half}_{gi // 32}_{gi % 32}")
                            if pm:
                                sc.wait_ge(sems["pe"], pm["pe"])
                        DMA(sc, wl, wob[:, ec % 4, :],
                            ago_wo.ap()[ec * 128:(ec + 1) * 128,
                                        dgrp * 512:(dgrp + 1) * 512])
                        mark(f"wol{half}_{dgrp}_{ec}", wl)

                    @block.tensor
                    def _(t, dgrp=dgrp, ec=ec, half=half):
                        waitm(t, f"wol{half}_{dgrp}_{ec}")
                        if ec == 0:
                            for ci in range(4):
                                waitm(t, f"yf{half}_{ci}")
                                pe_take_bank(t, 4 * (dgrp % 2) + ci)
                        for ci in range(4):
                            last = t.matmul(pb[4 * (dgrp % 2) + ci][:, :],
                                            yr4[:, ci, ec, :],
                                            wob[:, ec % 4, :],
                                            start=(ec == 0), stop=(ec == 31))
                        inc(last, "pe", 1)
                        mark(f"womm{half}_{dgrp}_{ec}", "pe")

                @block.scalar
                def _(sc, dgrp=dgrp, half=half):
                    waitm(sc, f"womm{half}_{dgrp}_31")
                    for ci in range(4):
                        a = sc.activation(mix4[:, ci, dgrp * 512:(dgrp + 1) * 512],
                                          pb[4 * (dgrp % 2) + ci][:, :], AF.Copy)
                        inc(a, "sc", 1)
                        bank_release(4 * (dgrp % 2) + ci, "sc")
                mark(f"woev{half}_{dgrp}", "sc")

            for ci in range(4):
                c = half * 4 + ci

                @block.scalar
                def _(sc, c=c, ci=ci, half=half):
                    for dgrp in range(4):
                        waitm(sc, f"woev{half}_{dgrp}")
                    if ci > 0:
                        sc.wait_ge(sems["syn"], marks[f"xqT{half}_{ci - 1}"]["syn"])
                    DMA(sc, "sld", bwk[:, :], h_sp.ap()[c, :, :])
                    DMA(sc, "sld", fscr[:, :], ti["resid"].ap()[c, :, :])
                mark(f"hrl{half}_{ci}", "sld")

                @block.vector
                def _(v, c=c, ci=ci, half=half):
                    waitm(v, f"hrl{half}_{ci}")
                    a = v.tensor_tensor(bscr[:, 0, :], mix4[:, ci, :], bwk[:, :],
                                        AL.add)
                    inc(a, "dv", 1)
                    wait_cur(v, "dv")
                    a = v.tensor_tensor(bscr[:, 0, :], bscr[:, 0, :],
                                        modb[:, 0, :], AL.mult)
                    inc(a, "dv", 1)
                    wait_cur(v, "dv")
                    a = v.tensor_tensor(x2b[:, ci, :], fscr[:, :], bscr[:, 0, :],
                                        AL.add)
                    inc(a, "dv", 1)
                mark(f"x2{half}_{ci}", "dv")

                if DEBUG and STAGE >= 4:
                    @block.gpsimd
                    def _(g, c=c, ci=ci, half=half):
                        g.wait_ge(sems["dv"], marks[f"x2{half}_{ci}"]["dv"])
                        DMA(g, "dbgd", dbg["x2"].ap()[c], x2b[:, ci, :])
                    mark(f"dbgx2{c}", "dbgd")

                @block.vector
                def _(v, ci=ci, half=half):
                    waitm(v, f"x2{half}_{ci}")
                    x = x2b[:, ci, :]
                    for s4 in range(4):
                        a = v.bn_stats(st2[:, 6 * s4:6 * (s4 + 1)],
                                       x[:, 512 * s4:512 * (s4 + 1)])
                    inc(a, "dv", 1)
                    wait_cur(v, "dv")
                    a = v.bn_aggr(st2[:, 24:26], st2[:, 0:24])
                    inc(a, "dv", 1)
                mark(f"st2{half}_{ci}", "dv")

                @block.scalar
                def _(sc, ci=ci, half=half):
                    waitm(sc, f"st2{half}_{ci}")
                    a = sc.activation(st2[:, 26:27], st2[:, 25:26], AF.Sqrt,
                                      bias=cst[:, 0:1])
                    inc(a, "sc", 1)
                mark(f"sq2{half}_{ci}", "sc")

                @block.vector
                def _(v, ci=ci, half=half):
                    waitm(v, f"sq2{half}_{ci}")
                    a = v.reciprocal(st2[:, 27:28], st2[:, 26:27])
                    inc(a, "dv", 1)
                    wait_cur(v, "dv")
                    a = v.tensor_scalar(bwk[:, :], x2b[:, ci, :], st2[:, 24:25],
                                        st2[:, 27:28], AL.subtract, AL.mult)
                    inc(a, "dv", 1)
                    wait_cur(v, "dv")
                    a = v.tensor_tensor(bscr[:, 0, :], bwk[:, :], modb[:, 2, :],
                                        AL.mult)
                    inc(a, "dv", 1)
                    wait_cur(v, "dv")
                    a = v.tensor_tensor(bwk[:, :], bwk[:, :], bscr[:, 0, :],
                                        AL.add)
                    inc(a, "dv", 1)
                    wait_cur(v, "dv")
                    a = v.tensor_tensor(bwk[:, :], bwk[:, :], modb[:, 1, :],
                                        AL.add)
                    inc(a, "dv", 1)
                    wait_cur(v, "dv")
                    a = v.reduce_max(qsb[:, ci, 0:1], bwk[:, :], axis=AX,
                                     apply_absolute_value=True)
                    inc(a, "dv", 1)
                    wait_cur(v, "dv")
                    a = v.tensor_scalar(qsb[:, ci, 0:1], qsb[:, ci, 0:1],
                                        1e-5, None, AL.max)
                    inc(a, "dv", 1)
                mark(f"h2m{half}_{ci}", "dv")

                @block.scalar
                def _(sc, ci=ci, half=half):
                    waitm(sc, f"h2m{half}_{ci}")
                    a = sc.activation(bscr[:, 1, :], bwk[:, :], AF.Square,
                                      accum_out=qsb[:, ci, 1:2])
                    inc(a, "sc", 1)
                    wait_cur(sc, "sc")
                    a = sc.activation(qsb[:, ci, 2:3], qsb[:, ci, 1:2], AF.Sqrt,
                                      bias=cst[:, 1:2], scale=cst[:, 3:4])
                    inc(a, "sc", 1)
                mark(f"h2s{half}_{ci}", "sc")

                @block.vector
                def _(v, ci=ci, half=half):
                    waitm(v, f"h2s{half}_{ci}")
                    v.reciprocal(qsb[:, ci, 3:4], qsb[:, ci, 2:3])
                    a = v.reciprocal(qsb[:, ci, 4:5], qsb[:, ci, 0:1])
                    inc(a, "dv", 1)
                    wait_cur(v, "dv")
                    v.tensor_scalar(qsb[:, ci, 5:6], qsb[:, ci, 4:5], 127.0,
                                    None, AL.mult)
                    a = v.tensor_tensor(qsb[:, ci, 6:7], qsb[:, ci, 0:1],
                                        qsb[:, ci, 3:4], AL.mult)
                    inc(a, "dv", 1)
                    wait_cur(v, "dv")
                    a = v.tensor_scalar(qsb[:, ci, 7:8], qsb[:, ci, 6:7],
                                        1.0 / 127.0, None, AL.mult)
                    inc(a, "dv", 1)
                    wait_cur(v, "dv")
                    v.tensor_scalar(qsb[:, ci, 8:9], qsb[:, ci, 7:8],
                                    wsc[:, 0:1], None, AL.mult)
                    a = v.tensor_scalar(qsb[:, ci, 9:10], qsb[:, ci, 7:8],
                                        wsc[:, 1:2], None, AL.mult)
                    inc(a, "dv", 1)
                mark(f"qx{half}_{ci}", "dv")

                @block.scalar
                def _(sc, ci=ci, half=half):
                    waitm(sc, f"qx{half}_{ci}")
                    a = sc.activation(fscr[:, :], bwk[:, :], AF.Relu,
                                      bias=cst[:, 2:3], scale=qsb[:, ci, 5:6])
                    inc(a, "sc", 1)
                mark(f"xr{half}_{ci}", "sc")

                @block.vector
                def _(v, ci=ci, half=half):
                    waitm(v, f"xr{half}_{ci}")
                    a = v.tensor_scalar(bwk[:, :], fscr[:, :], -MAGIC, None,
                                        AL.add)
                    inc(a, "dv", 1)
                mark(f"xq{half}_{ci}", "dv")

                @block.sync
                def _(s, ci=ci, half=half):
                    waitm(s, f"xq{half}_{ci}")
                    DMAT(s, "syn", xqT2[:, :, ci * BLK:(ci + 1) * BLK],
                         bwk[:, :])
                mark(f"xqT{half}_{ci}", "syn")

            free_group(BA)
            BB = f"backB{half}"
            hfb = alloc(BB, f"hfb_{half}", [128, 4, DFF], BF16)
            wfb = alloc(BB, f"wfb_{half}", [128, 2, 16, 512], BF16)
            w3b = alloc(BB, f"w3b_{half}", [128, 4, 512], BF16)

            hst = alloc(BB, f"hst_{half}", [128, 4, 192], F32)

            @block.vector
            def _(v, half=half):
                waitm(v, f"xq{half}_3")
                for dgrp in range(4):
                    waitm(v, f"woev{half}_{dgrp}")
                a = v.memset(qsb[:, :, 10:12], 0.0)
                inc(a, "dv", 1)
            mark(f"hacc0{half}", "dv")

            for fg in range(32):
                wl = f"wl{fg % 2}"

                @block.scalar
                def _(sc, fg=fg, wl=wl, half=half):
                    if fg == 0:
                        waitm(sc, f"xq{half}_3")
                        for dgrp in range(4):
                            waitm(sc, f"woev{half}_{dgrp}")
                        waitm(sc, "ag_w1", "ag_w2")
                    pm = marks.get(f"w12mm{half}_{fg - 2}")
                    if pm:
                        sc.wait_ge(sems["pe"], pm["pe"])
                    DMA(sc, wl, wfb[:, fg % 2, :, 0:256],
                        ago_w1.ap().rearrange("(k p) f -> p k f", p=128)
                        [:, :, fg * 256:(fg + 1) * 256])
                    DMA(sc, wl, wfb[:, fg % 2, :, 256:512],
                        ago_w2.ap().rearrange("(k p) f -> p k f", p=128)
                        [:, :, fg * 256:(fg + 1) * 256])
                    mark(f"wfl{half}_{fg}", wl)

                for ci in range(4):
                    it = fg * 4 + ci
                    bk = it % 4
                    bp = it % 2

                    @block.tensor
                    def _(t, fg=fg, ci=ci, bk=bk, half=half):
                        if fg == 0 and ci == 0:
                            for cj in range(4):
                                waitm(t, f"xqT{half}_{cj}")
                        waitm(t, f"wfl{half}_{fg}")
                        pe_take_bank(t, bk)
                        for k in range(16):
                            last = t.matmul(pb[bk][:, :],
                                            xqT2[:, k, ci * BLK:(ci + 1) * BLK],
                                            wfb[:, fg % 2, k, :],
                                            start=(k == 0), stop=(k == 15))
                        inc(last, "pe", 1)
                        mark(f"w12mm{half}_{fg}_{ci}", "pe")
                    if ci == 3:
                        mark(f"w12mm{half}_{fg}", "pe")

                    @block.scalar
                    def _(sc, fg=fg, ci=ci, bk=bk, bp=bp, it=it, half=half):
                        waitm(sc, f"w12mm{half}_{fg}_{ci}")
                        pm = (marks.get(f"hsec{half}_{(it - 2) // 4}_{(it - 2) % 4}")
                              if it >= 2 else
                              marks.get(f"hsec{half - 1}_31_{it + 2}"))
                        if pm:
                            sc.wait_ge(sems["dv"], pm["dv"])
                        a = sc.activation(bscr[:, 0, bp * 256:bp * 256 + 256],
                                          pb[bk][:, 0:256],
                                          AF.Silu, scale=qsb[:, ci, 8:9])
                        inc(a, "sc", 1)
                    mark(f"sil{half}_{fg}_{ci}", "sc")

                    @block.vector
                    def _(v, fg=fg, ci=ci, bk=bk, bp=bp, half=half):
                        waitm(v, f"sil{half}_{fg}_{ci}")
                        a = v.scalar_tensor_tensor(
                            hfb[:, ci, fg * 256:(fg + 1) * 256],
                            pb[bk][:, 256:512], qsb[:, ci, 9:10],
                            bscr[:, 0, bp * 256:bp * 256 + 256],
                            AL.mult, AL.mult)
                        inc(a, "dv", 1)
                        bank_release(bk, "dv")
                        wait_cur(v, "dv")
                        a = v.reduce_max(qsb[:, ci, 12:13],
                                         hfb[:, ci, fg * 256:(fg + 1) * 256],
                                         axis=AX, apply_absolute_value=True)
                        inc(a, "dv", 1)
                        v.bn_stats(hst[:, ci, 6 * fg:6 * (fg + 1)],
                                   hfb[:, ci, fg * 256:(fg + 1) * 256])
                        wait_cur(v, "dv")
                        a = v.tensor_tensor(qsb[:, ci, 10:11], qsb[:, ci, 10:11],
                                            qsb[:, ci, 12:13], AL.max)
                        inc(a, "dv", 1)
                    mark(f"hsec{half}_{fg}_{ci}", "dv")
                    if fg == 31:
                        mark(f"hss{half}_31_{ci}", "dv")

            hqT_loc = {}
            for ci in range(4):
                @block.vector
                def _(v, ci=ci, half=half):
                    v.wait_ge(sems["dv"], marks[f"hss{half}_31_{ci}"]["dv"])
                    a = v.bn_aggr(qsb[:, ci, 14:16], hst[:, ci, :])
                    inc(a, "dv", 1)
                    wait_cur(v, "dv")
                    a = v.tensor_tensor(qsb[:, ci, 11:12], qsb[:, ci, 14:15],
                                        qsb[:, ci, 14:15], AL.mult)
                    inc(a, "dv", 1)
                    wait_cur(v, "dv")
                    a = v.tensor_tensor(qsb[:, ci, 11:12], qsb[:, ci, 11:12],
                                        qsb[:, ci, 15:16], AL.add)
                    inc(a, "dv", 1)
                mark(f"hag{half}_{ci}", "dv")

                @block.scalar
                def _(sc, ci=ci, half=half):
                    sc.wait_ge(sems["dv"], marks[f"hag{half}_{ci}"]["dv"])
                    a = sc.activation(qsb[:, ci, 14:15], qsb[:, ci, 11:12],
                                      AF.Sqrt, bias=cst[:, 1:2], scale=cst[:, 6:7])
                    inc(a, "sc", 1)
                mark(f"hq1{half}_{ci}", "sc")

                @block.vector
                def _(v, ci=ci, half=half):
                    waitm(v, f"hq1{half}_{ci}")
                    a = v.tensor_scalar(qsb[:, ci, 10:11], qsb[:, ci, 10:11],
                                        1e-5, None, AL.max)
                    inc(a, "dv", 1)
                    wait_cur(v, "dv")
                    v.reciprocal(qsb[:, ci, 15:16], qsb[:, ci, 14:15])
                    a = v.reciprocal(qsb[:, ci, 12:13], qsb[:, ci, 10:11])
                    inc(a, "dv", 1)
                    wait_cur(v, "dv")
                    v.tensor_scalar(qsb[:, ci, 12:13], qsb[:, ci, 12:13],
                                    127.0, None, AL.mult)
                    a = v.tensor_tensor(qsb[:, ci, 13:14], qsb[:, ci, 10:11],
                                        qsb[:, ci, 15:16], AL.mult)
                    inc(a, "dv", 1)
                    wait_cur(v, "dv")
                    a = v.tensor_scalar(qsb[:, ci, 13:14], qsb[:, ci, 13:14],
                                        1.0 / 127.0, None, AL.mult)
                    inc(a, "dv", 1)
                    wait_cur(v, "dv")
                    a = v.tensor_scalar(qsb[:, ci, 13:14], qsb[:, ci, 13:14],
                                        wsc[:, 2:3], None, AL.mult)
                    inc(a, "dv", 1)
                mark(f"hq2{half}_{ci}", "dv")

                for piece in range(4):
                    @block.scalar
                    def _(sc, ci=ci, piece=piece, half=half):
                        waitm(sc, f"hq2{half}_{ci}")
                        pm = marks.get(f"hrq{half}_{ci}_{piece - 1}") or \
                            (marks.get(f"hrq{half}_{ci - 1}_3") if piece == 0 else None)
                        if pm:
                            sc.wait_ge(sems["dv"], pm["dv"])
                        a = sc.activation(fscr[:, :],
                                          hfb[:, ci, piece * D:(piece + 1) * D],
                                          AF.Relu, bias=cst[:, 2:3],
                                          scale=qsb[:, ci, 12:13])
                        inc(a, "sc", 1)
                    mark(f"hrr{half}_{ci}_{piece}", "sc")

                    @block.vector
                    def _(v, ci=ci, piece=piece, half=half):
                        waitm(v, f"hrr{half}_{ci}_{piece}")
                        a = v.tensor_scalar(hfb[:, ci, piece * D:(piece + 1) * D],
                                            fscr[:, :], -MAGIC, None, AL.add)
                        inc(a, "dv", 1)
                    mark(f"hrq{half}_{ci}_{piece}", "dv")

                @block.sync
                def _(s, ci=ci, half=half):
                    waitm(s, f"hrq{half}_{ci}_3")
                    if ci > 0:
                        waitm(s, f"hqT{half}_{ci - 1}")
                    if ci == 0:
                        s.wait_ge(sems["pe"], marks[f"w12mm{half}_31"]["pe"])
                        dst = hqT0v
                    else:
                        dst = hfb.ap()[:, ci - 1].rearrange("p (f j) -> p f j",
                                                            j=BLK)
                    DMAT(s, "syn", dst, hfb.ap()[:, ci])
                mark(f"hqT{half}_{ci}", "syn")
                hqT_loc[ci] = ci

            for dgrp in range(4):
                for fk in range(64):
                    wl = f"wl{fk % 4}"

                    @block.scalar
                    def _(sc, dgrp=dgrp, fk=fk, wl=wl, half=half):
                        if dgrp == 0 and fk == 0:
                            waitm(sc, f"xq{half}_3")
                            waitm(sc, "ag_w3")
                        gi = dgrp * 64 + fk - 4
                        if gi >= 0:
                            pm = marks.get(f"w3mm{half}_{gi // 64}_{gi % 64}")
                            if pm:
                                sc.wait_ge(sems["pe"], pm["pe"])
                        DMA(sc, wl, w3b[:, fk % 4, :],
                            ago_w3.ap()[fk * 128:(fk + 1) * 128,
                                        dgrp * 512:(dgrp + 1) * 512])
                        mark(f"w3l{half}_{dgrp}_{fk}", wl)

                    @block.tensor
                    def _(t, dgrp=dgrp, fk=fk, half=half):
                        waitm(t, f"w3l{half}_{dgrp}_{fk}")
                        if fk == 0:
                            if dgrp == 0:
                                for cj in range(4):
                                    waitm(t, f"hqT{half}_{cj}")
                            for ci in range(4):
                                pe_take_bank(t, 4 + ci)
                        for ci in range(4):
                            src = hqT0v if ci == 0 else \
                                hfb.ap()[:, ci - 1].rearrange("p (f j) -> p f j",
                                                              j=BLK)
                            last = t.matmul(pb[4 + ci][:, :],
                                            src[:, fk, :],
                                            w3b[:, fk % 4, :],
                                            start=(fk == 0), stop=(fk == 63))
                        inc(last, "pe", 1)
                        mark(f"w3mm{half}_{dgrp}_{fk}", "pe")

                for ci in range(4):
                    c = half * 4 + ci

                    @block.vector
                    def _(v, dgrp=dgrp, ci=ci, half=half):
                        waitm(v, f"w3mm{half}_{dgrp}_63")
                        pm = marks.get(f"oev{half}_{dgrp}_{ci - 1}") or \
                            (marks.get(f"oev{half}_{dgrp - 1}_3") if ci == 0 else None)
                        if pm:
                            v.wait_ge(sems["outd"], pm["outd"])
                        a = v.tensor_scalar(bscr[:, 0, 0:512], pb[4 + ci][:, :],
                                            qsb[:, ci, 13:14], None, AL.mult)
                        inc(a, "dv", 1)
                        wait_cur(v, "dv")
                        a = v.tensor_tensor(bscr[:, 0, 0:512], bscr[:, 0, 0:512],
                                            modb[:, 3, dgrp * 512:(dgrp + 1) * 512],
                                            AL.mult)
                        inc(a, "dv", 1)
                        wait_cur(v, "dv")
                        a = v.tensor_tensor(outb[:, ci % 2, :],
                                            x2b[:, ci, dgrp * 512:(dgrp + 1) * 512],
                                            bscr[:, 0, 0:512], AL.add)
                        inc(a, "dv", 1)
                    mark(f"ow{half}_{dgrp}_{ci}", "dv")

                    @block.scalar
                    def _(sc, dgrp=dgrp, ci=ci, c=c, half=half):
                        sc.wait_ge(sems["dv"], marks[f"ow{half}_{dgrp}_{ci}"]["dv"])
                        DMA(sc, "outd", out_t.ap()[c, :, dgrp * 512:(dgrp + 1) * 512],
                            outb[:, ci % 2, :])
                    mark(f"oev{half}_{dgrp}_{ci}", "outd")
                for ci in range(4):
                    bank_release(4 + ci, "dv")

            free_group(BB)
            mark(f"bkdone_{half}", "dv", "sc", "pe", "syn", "sld", "outd",
                 "wl0", "wl1", "gld", "dbgd")

    if STAGE < 5:
        @block.scalar
        def _(sc):
            for c in range(NCH):
                DMA(sc, "outd", out_t.ap()[c], ti["resid"].ap()[c])

    @block.sync
    def _(s):
        wait_cur(s, "outd")
        if DEBUG:
            wait_cur(s, "dbgd")
        wait_cur(s, "cc")
        wait_cur(s, "gst")
        wait_cur(s, "sld")
        wait_cur(s, "gld")
    return nc


def kernel(L0_attn_resid, L0_attn_norm_w, L0_attn_W_skip, L0_attn_W_in,
           L0_attn_conv_w, L0_attn_conv_b, L0_attn_W_d1, L0_attn_W_d2,
           L0_attn_b_d2, L0_attn_W_Bm, L0_attn_W_Cm, L0_attn_A_log,
           L0_attn_W_D, L0_attn_W_out, L0_ffn_w1, L0_ffn_w2, L0_ffn_w3,
           L0_c, L0_wmod):
    resid = np.ascontiguousarray(np.asarray(L0_attn_resid, np.float32)[0])
    in_maps = []
    for i in range(NCORES):
        es = slice(ES * i, ES * (i + 1))
        m = {
            "resid": np.ascontiguousarray(resid.reshape(NCH, NCORES, BLK, D)[:, i]),
            "cvec": np.ascontiguousarray(np.asarray(L0_c, np.float32).reshape(16, 128).T),
            "wmod_sl": np.ascontiguousarray(np.asarray(L0_wmod, np.float32)[:, 1536 * i:1536 * (i + 1)]),
            "norm_w": np.asarray(L0_attn_norm_w, np.float32).reshape(1, D),
            "conv_w": np.ascontiguousarray(
                np.asarray(L0_attn_conv_w, np.float32)[es, 0, :]
                .reshape(4, 128, 4).transpose(1, 0, 2).reshape(128, 16)),
            "conv_b": np.ascontiguousarray(
                np.asarray(L0_attn_conv_b, np.float32)[es].reshape(4, 128).T),
            "A_log": np.ascontiguousarray(
                np.asarray(L0_attn_A_log, np.float32)[es]
                .reshape(4, 128, NS).transpose(1, 0, 2).reshape(128, 64)),
            "W_D": np.ascontiguousarray(
                np.asarray(L0_attn_W_D, np.float32)[es].reshape(4, 128).T),
            "b_d2": np.ascontiguousarray(
                np.asarray(L0_attn_b_d2, np.float32)[es].reshape(4, 128).T),
            "W_d1_sl": np.ascontiguousarray(np.asarray(L0_attn_W_d1, np.float32)[:, es]),
            "W_d2_sl": np.ascontiguousarray(np.asarray(L0_attn_W_d2, np.float32)[es, :]),
            "W_Bm_sl": np.ascontiguousarray(np.asarray(L0_attn_W_Bm, np.float32)[:, es]),
            "W_Cm_sl": np.ascontiguousarray(np.asarray(L0_attn_W_Cm, np.float32)[:, es]),
            "W_in_sl": np.ascontiguousarray(np.asarray(L0_attn_W_in, np.float32)[:, DSH * i:DSH * (i + 1)]),
            "W_skip_sl": np.ascontiguousarray(np.asarray(L0_attn_W_skip, np.float32)[:, DSH * i:DSH * (i + 1)]),
            "W_out_sl": np.ascontiguousarray(np.asarray(L0_attn_W_out, np.float32)[:, es]),
            "w1_sl": np.ascontiguousarray(np.asarray(L0_ffn_w1, np.float32)[:, DSH * i:DSH * (i + 1)]),
            "w2_sl": np.ascontiguousarray(np.asarray(L0_ffn_w2, np.float32)[:, DSH * i:DSH * (i + 1)]),
            "w3_sl": np.ascontiguousarray(np.asarray(L0_ffn_w3, np.float32)[:, FSH * i:FSH * (i + 1)]),
        }
        in_maps.append(m)
    nc = build_program()
    res = run_bass_kernel_spmd(nc, in_maps, core_ids=list(range(NCORES)))
    out = np.zeros((L, D), np.float32)
    ov = out.reshape(NCH, NCORES, BLK, D)
    for i in range(NCORES):
        ov[:, i] = res.results[i]["out"]
    if DEBUG:
        kernel.dbg = res.results
    return out.reshape(1, L, D)


# version 11 (final)
# speedup vs baseline: 1.2445x; 1.2445x over previous
import numpy as np
import concourse.bass as bass
import concourse.mybir as mybir
import bass_rust as _br
from concourse.bass_utils import run_bass_kernel_spmd
from concourse.aot_env import allow_ici_all_to_all

F32 = mybir.dt.float32
BF16 = mybir.dt.bfloat16
AF = mybir.ActivationFunctionType
AL = mybir.AluOpType
AX = mybir.AxisListType.X

NCORES = 8
D = 2048
E = 4096
ES = E // NCORES
R = 128
NS = 16
L = 8192
DFF = 8192
NCH = 8
BLK = 128
CH_T = NCORES * BLK
DSH = D // NCORES
FSH = DFF // NCORES
MAGIC = 12582912.0
RG = [list(range(NCORES))]

DEBUG = False
STAGE = 5


def build_program():
    nc = bass.Bass("TRN2", target_bir_lowering=False, debug=False)
    ti = {}

    def din(name, shape, dtype=F32):
        ti[name] = nc.dram_tensor(name, list(shape), dtype, kind="ExternalInput")
        return ti[name]

    din("resid", (NCH, BLK, D))
    din("cvec", (128, 16))
    din("wmod_sl", (D, 1536))
    din("norm_w", (1, D))
    din("conv_w", (128, 16))
    din("conv_b", (128, 4))
    din("A_log", (128, 64))
    din("W_D", (128, 4))
    din("b_d2", (128, 4))
    din("W_d1_sl", (R, ES))
    din("W_d2_sl", (ES, R))
    din("W_Bm_sl", (NS, ES))
    din("W_Cm_sl", (NS, ES))
    din("W_in_sl", (E, DSH))
    din("W_skip_sl", (E, DSH))
    din("W_out_sl", (D, ES))
    din("w1_sl", (DFF, DSH))
    din("w2_sl", (DFF, DSH))
    din("w3_sl", (D, FSH))
    out_t = nc.dram_tensor("out", [NCH, BLK, D], F32, kind="ExternalOutput")
    dbg = {}
    if DEBUG:
        if STAGE >= 1:
            dbg["xsilu"] = nc.dram_tensor("d_xsilu", [4, 128, L], F32, kind="ExternalOutput")
        if STAGE >= 2:
            dbg["dt"] = nc.dram_tensor("d_dt", [4, 128, L], F32, kind="ExternalOutput")
        if STAGE >= 3:
            dbg["y"] = nc.dram_tensor("d_y", [4, 128, L], F32, kind="ExternalOutput")
        if STAGE >= 4:
            dbg["x2"] = nc.dram_tensor("d_x2", [NCH, BLK, D], F32, kind="ExternalOutput")

    dram = {}
    shared_out = {"ago_wi", "ago_ws", "ago_wo", "ago_w1", "ago_w2", "ago_w3",
                  "ago_mod", "aro_ws", "aro_p"}
    for nm, shape, dt_ in (
        ("agi_wi", [DSH, E], BF16), ("ago_wi", [D, E], BF16),
        ("agi_ws", [DSH, E], BF16), ("ago_ws", [D, E], BF16),
        ("agi_wo", [ES, D], BF16), ("ago_wo", [E, D], BF16),
        ("agi_w1", [DSH, DFF], BF16), ("ago_w1", [D, DFF], BF16),
        ("agi_w2", [DSH, DFF], BF16), ("ago_w2", [D, DFF], BF16),
        ("agi_w3", [FSH, D], BF16), ("ago_w3", [DFF, D], BF16),
        ("agi_mod", [1, 1536], F32), ("ago_mod", [1, 12288], F32),
        ("ari_ws", [128, 4], F32), ("aro_ws", [128, 4], F32),
        ("modv_d", [6, D], F32),
        ("a2a_fi", [NCH, NCORES, ES, BLK], BF16),
        ("a2a_fo", [NCH, NCORES, ES, BLK], BF16),
        ("ari_p", [NCH, 160, CH_T], F32), ("aro_p", [NCH, 160, CH_T], F32),
        ("bct_d", [NCH, 32, CH_T], BF16),
        ("a2a_bi", [NCH, NCORES, ES, BLK], BF16),
        ("a2a_bo", [NCH, NCORES, ES, BLK], BF16),
        ("h_sp", [NCH, BLK, D], BF16),
        ("skip_d", [32, 128, CH_T], BF16),
    ):
        dram[nm] = nc.dram_tensor(
            nm, shape, dt_,
            addr_space="Shared" if nm in shared_out else "Local")
    agi_wi, ago_wi = dram["agi_wi"], dram["ago_wi"]
    agi_ws, ago_ws = dram["agi_ws"], dram["ago_ws"]
    agi_wo, ago_wo = dram["agi_wo"], dram["ago_wo"]
    agi_w1, ago_w1 = dram["agi_w1"], dram["ago_w1"]
    agi_w2, ago_w2 = dram["agi_w2"], dram["ago_w2"]
    agi_w3, ago_w3 = dram["agi_w3"], dram["ago_w3"]
    agi_mod, ago_mod = dram["agi_mod"], dram["ago_mod"]
    ari_ws, aro_ws = dram["ari_ws"], dram["aro_ws"]
    modv_d = dram["modv_d"]
    a2a_fi, a2a_fo = dram["a2a_fi"], dram["a2a_fo"]
    ari_p, aro_p = dram["ari_p"], dram["aro_p"]
    bct_d = dram["bct_d"]
    a2a_bi, a2a_bo = dram["a2a_bi"], dram["a2a_bo"]
    h_sp, skip_d = dram["h_sp"], dram["skip_d"]

    groups = {}

    def alloc(group, name, shape, dtype):
        cm = nc.sbuf_tensor(name, list(shape), dtype)
        h = cm.__enter__()
        groups.setdefault(group, []).append(cm)
        return h

    def free_group(group):
        for cm in reversed(groups.pop(group, [])):
            cm.__exit__(None, None, None)

    psum_cms = []

    def psum(name, shape, dtype=F32):
        cm = nc.psum_tensor(name, list(shape), dtype)
        psum_cms.append(cm)
        return cm.__enter__()

    sems = {}
    cnt = {}
    sem_cms = []

    def sem(name):
        cm = nc.semaphore(name)
        sem_cms.append(cm)
        sems[name] = cm.__enter__()
        cnt[name] = 0

    def inc(instr, name, k=1):
        instr.then_inc(sems[name], k)
        cnt[name] += k
        return instr

    marks = {}

    DMA_SEMS = ("gld", "sld", "syn", "gst", "outd", "dbgd", "sst")

    def mark(name, *semnames):
        marks[name] = {s: cnt[s] for s in semnames}
        for s in semnames:
            if s in DMA_SEMS:
                dma_barrier[s] = cnt[s]

    def waitm(eng, *names):
        for name in names:
            for s, n in marks[name].items():
                if n > 0:
                    eng.wait_ge(sems[s], n)

    def wait_cur(eng, semname):
        if cnt[semname] > 0:
            eng.wait_ge(sems[semname], cnt[semname])

    dma_barrier = {}
    dma_applied = {}

    def _dma_guard(eng, semname):
        b = dma_barrier.get(semname, 0)
        if b > dma_applied.get(semname, 0):
            eng.wait_ge(sems[semname], b)
            dma_applied[semname] = b

    def DMA(eng, semname, out, in_):
        _dma_guard(eng, semname)
        return inc(eng.dma_start(out=out, in_=in_), semname, 16)

    def DMAT(eng, semname, out, in_):
        _dma_guard(eng, semname)
        return inc(eng.dma_start_transpose(out, in_), semname, 16)

    for s in ("gld", "sld", "syn", "cc", "pe", "dv", "sc", "gst", "gp", "outd",
              "dbgd", "sst"):
        sem(s)

    bank_free = [None] * 8

    def pe_take_bank(t, i):
        if bank_free[i]:
            s, n = bank_free[i]
            if n > 0:
                t.wait_ge(sems[s], n)

    def bank_release(i, semname):
        bank_free[i] = (semname, cnt[semname])

    P = "persist"
    cw_s = alloc(P, "cw_s", [128, 16], F32)
    cb_s = alloc(P, "cb_s", [128, 4], F32)
    wd_s = alloc(P, "wd_s", [128, 4], F32)
    bd2_s = alloc(P, "bd2_s", [128, 4], F32)
    a_neg = alloc(P, "a_neg", [128, 64], F32)
    wsc = alloc(P, "wsc", [128, 8], F32)
    wd1T = alloc(P, "wd1T", [128, 4, 128], BF16)
    wd2T = alloc(P, "wd2T", [128, 4, 128], BF16)
    wbmT = alloc(P, "wbmT", [128, 4, NS], BF16)
    wcmT = alloc(P, "wcmT", [128, 4, NS], BF16)
    cst = alloc(P, "cst", [128, 8], F32)
    carry_h = alloc(P, "carry_h", [128, 64], F32)
    carry_x = alloc(P, "carry_x", [128, 4, 4], BF16)
    tiny = alloc(P, "tiny", [128, 8], F32)

    SK = "skgrp"
    sk_st = alloc(SK, "sk_st", [128, 2, 512], BF16)
    FL = "frontlong"
    wq_f = alloc(FL, "wq_f", [128, 16, 512], BF16)
    xnT = alloc(FL, "xnT", [128, 16, NCH * BLK], BF16)
    EA = "earlyA"
    modf = alloc(EA, "modf", [128, 2, D], BF16)
    normw_bc = alloc(EA, "normw_bc", [128, D], BF16)
    xt_f = alloc(EA, "xt_f", [128, 2, D], F32)
    hb_f = alloc(EA, "hb_f", [128, 2, D], BF16)
    xn_f = alloc(EA, "xn_f", [128, 2, D], BF16)
    st_f = alloc(EA, "st_f", [128, 64], F32)
    scr_sc = alloc(EA, "scr_sc", [128, 1024], BF16)
    stgf = alloc(EA, "stgf", [128, 2, 4, 2, 512], BF16)

    pb = [psum(f"pb{i}", [128, 512], F32) for i in range(8)]

    blk_cm = nc.Block()
    block = blk_cm.__enter__()

    @block.vector
    def _(v):
        v.memset(cst[:, 0:1], 1e-5)
        v.memset(cst[:, 1:2], 1e-6)
        v.memset(cst[:, 2:3], MAGIC)
        v.memset(cst[:, 3:4], 1.0 / D)
        v.memset(cst[:, 4:5], 1.0 / DFF)
        v.memset(cst[:, 5:6], -1.0)
        v.memset(cst[:, 6:7], 1.0)
        v.memset(carry_h[:, :], 0.0)
        a = v.memset(tiny[:, :], 0.0)
        inc(a, "dv", 1)
    mark("cst", "dv")

    PR = "prep"
    pw = alloc(PR, "pw", [128, 64, DSH], BF16)
    pt_s = alloc(PR, "pt_s", [128, 2, 8192], BF16)
    wm_s = alloc(PR, "wm_s", [128, 16, 128], F32)
    psc_f = alloc(PR, "psc_f", [1, 1536], F32)
    silc = alloc(PR, "silc", [128, 16], F32)
    pabs = alloc(PR, "pabs", [128, 8], F32)
    ones_f = alloc(PR, "ones_f", [128, 128], F32)
    pqs = alloc(PR, "pqs", [128, 1024], F32)

    @block.gpsimd
    def _(g):
        DMA(g, "gld", pw[:, 0:32, :],
            ti["W_in_sl"].ap().rearrange("(c p) m -> p c m", p=128))
        DMA(g, "gld", pw[:, 32:64, :],
            ti["W_skip_sl"].ap().rearrange("(c p) m -> p c m", p=128))
    mark("pw_fr", "gld")

    @block.sync
    def _(s):
        waitm(s, "pw_fr")
        for c in range(64):
            half, cc = divmod(c, 32)
            tgt = pt_s[:, :, half * 4096 + cc * 128: half * 4096 + (cc + 1) * 128]
            DMAT(s, "syn", tgt, pw[:, c, :])
        s.wait_ge(sems["syn"], cnt["syn"])
        DMA(s, "syn", agi_wi.ap().rearrange("(q p) e -> p q e", p=128),
            pt_s[:, :, 0:4096])
        DMA(s, "syn", agi_ws.ap().rearrange("(q p) e -> p q e", p=128),
            pt_s[:, :, 4096:8192])
    mark("agi_fr", "syn")

    @block.gpsimd
    def _(g):
        waitm(g, "agi_fr")
        c0 = g.collective_compute("AllGather", AL.bypass, replica_groups=RG,
                                  ins=[agi_wi.ap().opt()], outs=[ago_wi.ap().opt()])
        inc(c0, "cc", 1)
        c1 = g.collective_compute("AllGather", AL.bypass, replica_groups=RG,
                                  ins=[agi_ws.ap().opt()], outs=[ago_ws.ap().opt()])
        inc(c1, "cc", 1)
    mark("ag_fr", "cc")

    @block.gpsimd
    def _(g):
        DMA(g, "gld", silc[:, :], ti["cvec"].ap())
    mark("cvld", "gld")

    @block.scalar
    def _(sc):
        waitm(sc, "cvld", "cst")
        a = sc.activation(silc[:, :], silc[:, :], AF.Silu)
        inc(a, "sc", 1)
    mark("silc", "sc")

    for j in range(12):
        @block.gpsimd
        def _(g, j=j):
            if j > 0:
                g.wait_ge(sems["pe"], marks[f"modmm{j - 1}"]["pe"])
            DMA(g, "gld", wm_s[:, :, :],
                ti["wmod_sl"].ap()
                .rearrange("(k p) m -> p k m", p=128)[:, :, j * 128:(j + 1) * 128])
        mark(f"wmld{j}", "gld")

        @block.tensor
        def _(t, j=j):
            waitm(t, "silc", f"wmld{j}")
            if j >= 3:
                t.wait_ge(sems["dv"], marks[f"modcp{j - 3}"]["dv"])
            for k in range(16):
                last = t.matmul(pb[j % 3][0:1, 0:128], silc[:, k:k + 1], wm_s[:, k, :],
                                start=(k == 0), stop=(k == 15))
            inc(last, "pe", 1)
        mark(f"modmm{j}", "pe")

        @block.vector
        def _(v, j=j):
            waitm(v, f"modmm{j}")
            last = v.tensor_copy(psc_f[0:1, j * 128:(j + 1) * 128], pb[j % 3][0:1, 0:128])
            inc(last, "dv", 1)
        mark(f"modcp{j}", "dv")
    mark("modcp", "dv")
    modcp_dv = cnt["dv"]
    for _b in range(3):
        bank_release(_b, "dv")

    @block.gpsimd
    def _(g):
        waitm(g, "modcp")
        DMA(g, "gld", agi_mod.ap(), psc_f[0:1, 0:1536])
        wait_cur(g, "gld")
        c1 = g.collective_compute("AllGather", AL.bypass, replica_groups=RG,
                                  ins=[agi_mod.ap().opt()], outs=[ago_mod.ap().opt()])
        inc(c1, "cc", 1)
    mark("modex", "cc")

    @block.gpsimd
    def _(g):
        waitm(g, "modex")
        m6 = ago_mod.ap().rearrange("one (s d) -> one s d", s=6)[0]
        DMA(g, "gld", modf[:, :, :], m6[0:2][None, :, :].broadcast_to((128, 2, D)))
        DMA(g, "gld", normw_bc[:, :], ti["norm_w"].ap().broadcast_to((128, D)))
    mark("modbc", "gld")

    @block.gpsimd
    def _(g):
        waitm(g, "agi_fr")
        DMA(g, "gld", cw_s[:, :], ti["conv_w"].ap())
        DMA(g, "gld", cb_s[:, :], ti["conv_b"].ap())
        DMA(g, "gld", wd_s[:, :], ti["W_D"].ap())
        DMA(g, "gld", bd2_s[:, :], ti["b_d2"].ap())
        DMA(g, "gld", a_neg[:, :], ti["A_log"].ap())
        DMA(g, "gld", pw[:, 0:4, 0:128],
            ti["W_d1_sl"].ap().rearrange("r (gg pp) -> r gg pp", pp=128))
        DMA(g, "gld", pw[:, 4:8, 0:128],
            ti["W_d2_sl"].ap().rearrange("(gg p) r -> p gg r", p=128))
        DMA(g, "gld", pw[0:NS, 8:12, 0:128],
            ti["W_Bm_sl"].ap().rearrange("n (gg p) -> n gg p", p=128))
        DMA(g, "gld", pw[0:NS, 12:16, 0:128],
            ti["W_Cm_sl"].ap().rearrange("n (gg p) -> n gg p", p=128))
    mark("smallw", "gld")

    @block.sync
    def _(s):
        waitm(s, "smallw")
        for gg in range(4):
            DMAT(s, "syn", wd1T[:, gg:gg + 1, :], pw[:, gg, 0:128])
            DMAT(s, "syn", wd2T[:, gg:gg + 1, :], pw[:, 4 + gg, 0:128])
            DMAT(s, "syn", wbmT[:, gg:gg + 1, :], pw[0:NS, 8 + gg, 0:128])
            DMAT(s, "syn", wcmT[:, gg:gg + 1, :], pw[0:NS, 12 + gg, 0:128])
    mark("smallT", "syn")

    @block.scalar
    def _(sc):
        waitm(sc, "smallw")
        a = sc.activation(a_neg[:, :], a_neg[:, :], AF.Exp)
        inc(a, "sc", 1)
    mark("aexp", "sc")

    @block.vector
    def _(v):
        waitm(v, "aexp")
        a = v.tensor_scalar(a_neg[:, :], a_neg[:, :], -1.0, None, AL.mult)
        inc(a, "dv", 1)
    mark("aneg", "dv")

    def prep_plain(src_name, agi, nrows_sb, width, label):
        nch = {"wo": 16}[label]

    @block.gpsimd
    def _(g):
        waitm(g, "smallT")
        pwv = pw.ap().rearrange("p c m -> p (c m)")[:, 0:16 * 512] \
            .rearrange("p (c m) -> p c m", m=512)
        DMA(g, "gld", pwv,
            ti["W_out_sl"].ap().rearrange("(c p) m -> p c m", p=128))
    mark("pwo", "gld")

    @block.sync
    def _(s):
        waitm(s, "pwo")
        pt2 = pt_s.ap().rearrange("p q m -> p (q m)").rearrange("p (f j) -> p f j", f=4)
        pwv = pw.ap().rearrange("p c m -> p (c m)")[:, 0:16 * 512] \
            .rearrange("p (c m) -> p c m", m=512)
        for c in range(16):
            DMAT(s, "syn", pt2[:, :, c * 128:(c + 1) * 128], pwv[:, c, :])
        s.wait_ge(sems["syn"], cnt["syn"])
        DMA(s, "syn", agi_wo.ap().rearrange("(f p) d2 -> p f d2", p=128),
            pt2[:, :, 0:2048])
    mark("agi_wo", "syn")

    @block.gpsimd
    def _(g):
        waitm(g, "agi_wo")
        c1 = g.collective_compute("AllGather", AL.bypass, replica_groups=RG,
                                  ins=[agi_wo.ap().opt()], outs=[ago_wo.ap().opt()])
        inc(c1, "cc", 1)
    mark("ag_wo", "cc")


    mark("prep_done", "syn", "gld", "dv", "sc", "gst", "sld", "cc")

    if STAGE >= 5:
        for i, wn in enumerate(("w1_sl", "w2_sl", "w3_sl")):
            @block.gpsimd
            def _(g, i=i, wn=wn):
                if i == 0:
                    waitm(g, "agi_wo")
                else:
                    waitm(g, f"wabs{i - 1}")
                DMA(g, "gld", pw[:, 0:64, :],
                    ti[wn].ap().rearrange("(c p) m -> p c m", p=128))
            mark(f"pq{i + 1}", "gld")

            @block.vector
            def _(v, i=i):
                waitm(v, f"pq{i + 1}")
                a = v.reduce_sum(pabs[:, i:i + 1],
                                 pw.ap().rearrange("p c m -> p (c m)"),
                                 axis=AX, apply_absolute_value=True)
                inc(a, "dv", 1)
                if i == 2:
                    wait_cur(v, "dv")
                    a = v.memset(pabs[:, 3:4], 0.0)
                    inc(a, "dv", 1)
            mark(f"wabs{i}", "dv")

        @block.gpsimd
        def _(g):
            waitm(g, "wabs2")
            DMA(g, "gld", ari_ws.ap(), pabs[:, 0:4])
            wait_cur(g, "gld")
            c1 = g.collective_compute("AllReduce", AL.add, replica_groups=RG,
                                      ins=[ari_ws.ap().opt()],
                                      outs=[aro_ws.ap().opt()])
            inc(c1, "cc", 1)
            wait_cur(g, "cc")
            g.wait_ge(sems["pe"], marks["modmm11"]["pe"])
            _dma_guard(g, "gld")
            inc(g.dma_start(out=ones_f[0:4, 0:128],
                            in_=aro_ws.ap().rearrange("p w -> w p"),
                            allow_slow_non_contiguous=True), "gld", 16)
        mark("wsT", "gld")

        @block.vector
        def _(v):
            waitm(v, "wsT")
            a = v.reduce_sum(pabs[0:4, 4:5], ones_f[0:4, 0:128], axis=AX)
            inc(a, "dv", 1)
            wait_cur(v, "dv")
            a = v.tensor_scalar(pabs[0:4, 5:6], pabs[0:4, 4:5],
                                1.0 / (DFF * D), 1e-5, AL.mult, AL.max)
            inc(a, "dv", 1)
            wait_cur(v, "dv")
            a = v.reciprocal(pabs[0:4, 6:7], pabs[0:4, 5:6])
            inc(a, "dv", 1)
        mark("ws4", "dv")

        @block.gpsimd
        def _(g):
            waitm(g, "ws4")
            wv = ari_ws.ap()[0:2, :].rearrange("a b -> (a b)") \
                .rearrange("(w c) -> w c", c=2)
            DMA(g, "gld", wv, pabs[0:4, 5:7])
            wait_cur(g, "gld")
            _dma_guard(g, "gld")
            inc(g.dma_start(out=wsc[:, 0:3],
                            in_=wv[0:3, 0][None, :].broadcast_to((128, 3)),
                            allow_slow_non_contiguous=True), "gld", 16)
            inc(g.dma_start(out=wsc[:, 4:7],
                            in_=wv[0:3, 1][None, :].broadcast_to((128, 3)),
                            allow_slow_non_contiguous=True), "gld", 16)
            wait_cur(g, "gld")
        mark("wscal", "gld")
        mark("pq3", "gld")

    for c in range(NCH):
        po = 32 * (c % 2)

        @block.scalar
        def _(sc, c=c):
            if c >= 2:
                sc.wait_ge(sems["dv"], marks[f"xn{c - 2}"]["dv"])
            DMA(sc, "sld", xt_f[:, c % 2, :], ti["resid"].ap()[c, :, :])
        mark(f"res{c}", "sld")

        @block.vector
        def _(v, c=c, po=po):
            waitm(v, f"res{c}", "modbc")
            if c >= 2:
                waitm(v, f"hbu{c - 2}")
            x = xt_f[:, c % 2, :]
            for s4 in range(4):
                a = v.bn_stats(st_f[:, po + 6 * s4: po + 6 * (s4 + 1)],
                               x[:, 512 * s4: 512 * (s4 + 1)])
            inc(a, "dv", 1)
            wait_cur(v, "dv")
            a = v.bn_aggr(st_f[:, po + 24: po + 26], st_f[:, po: po + 24])
            inc(a, "dv", 1)
        mark(f"st{c}", "dv")

        @block.scalar
        def _(sc, c=c, po=po):
            waitm(sc, f"st{c}")
            a = sc.activation(st_f[:, po + 26: po + 27], st_f[:, po + 25: po + 26],
                              AF.Sqrt, bias=cst[:, 0:1])
            inc(a, "sc", 1)
        mark(f"sqv{c}", "sc")

        @block.vector
        def _(v, c=c, po=po):
            waitm(v, f"sqv{c}")
            a = v.reciprocal(st_f[:, po + 27: po + 28], st_f[:, po + 26: po + 27])
            inc(a, "dv", 1)
            wait_cur(v, "dv")
            x = xt_f[:, c % 2, :]
            hb = hb_f[:, c % 2, :]
            tmp = xn_f[:, c % 2, :]
            if c >= 2:
                v.wait_ge(sems["syn"], marks[f"xnT{c - 2}"]["syn"])
            a = v.tensor_scalar(hb, x, st_f[:, po + 24: po + 25],
                                st_f[:, po + 27: po + 28], AL.subtract, AL.mult)
            inc(a, "dv", 1)
            wait_cur(v, "dv")
            a = v.tensor_tensor(tmp, hb, modf[:, 1, :], AL.mult)
            inc(a, "dv", 1)
            wait_cur(v, "dv")
            a = v.tensor_tensor(hb, hb, tmp, AL.add)
            inc(a, "dv", 1)
            wait_cur(v, "dv")
            a = v.tensor_tensor(hb, hb, modf[:, 0, :], AL.add)
            inc(a, "dv", 1)
        mark(f"hb{c}", "dv")

        @block.scalar
        def _(sc, c=c, po=po):
            waitm(sc, f"hb{c}")
            DMA(sc, "sld", h_sp.ap()[c, :, :], hb_f[:, c % 2, :])
            a = sc.activation(scr_sc[:, :], hb_f[:, c % 2, 0:1024], AF.Square,
                              accum_out=st_f[:, po + 28: po + 29])
            inc(a, "sc", 1)
            wait_cur(sc, "sc")
            a = sc.activation(scr_sc[:, :], hb_f[:, c % 2, 1024:2048], AF.Square,
                              accum_out=st_f[:, po + 31: po + 32])
            inc(a, "sc", 1)
        mark(f"hbsq{c}", "sc", "sld")

        @block.vector
        def _(v, c=c, po=po):
            waitm(v, f"hbsq{c}")
            a = v.tensor_tensor(st_f[:, po + 28: po + 29],
                                st_f[:, po + 28: po + 29],
                                st_f[:, po + 31: po + 32], AL.add)
            inc(a, "dv", 1)
        mark(f"hbss{c}", "dv")

        @block.scalar
        def _(sc, c=c, po=po):
            sc.wait_ge(sems["dv"], marks[f"hbss{c}"]["dv"])
            a = sc.activation(st_f[:, po + 29: po + 30], st_f[:, po + 28: po + 29],
                              AF.Sqrt, bias=cst[:, 0:1], scale=cst[:, 3:4])
            inc(a, "sc", 1)
        mark(f"hbu{c}", "sc", "sld")

        @block.vector
        def _(v, c=c, po=po):
            waitm(v, f"hbu{c}")
            a = v.reciprocal(st_f[:, po + 30: po + 31], st_f[:, po + 29: po + 30])
            inc(a, "dv", 1)
            wait_cur(v, "dv")
            xn = xn_f[:, c % 2, :]
            a = v.tensor_scalar(xn, hb_f[:, c % 2, :], st_f[:, po + 30: po + 31],
                                None, AL.mult)
            inc(a, "dv", 1)
            wait_cur(v, "dv")
            a = v.tensor_tensor(xn, xn, normw_bc[:, :], AL.mult)
            inc(a, "dv", 1)
        mark(f"xn{c}", "dv")

        @block.sync
        def _(s, c=c):
            waitm(s, f"xn{c}")
            DMAT(s, "syn", xnT[:, :, c * BLK:(c + 1) * BLK], xn_f[:, c % 2, :])
        mark(f"xnT{c}", "syn")

    def front_pass(weight_dram, to_skip):
        tag = "ws" if to_skip else "wi"
        for q in range(8):
            @block.scalar
            def _(sc, q=q):
                prevq = f"fmmq_{tag}_{q - 1}" if q > 0 else ("fmmq_wi_7" if to_skip else None)
                if prevq:
                    sc.wait_ge(sems["pe"], marks[prevq]["pe"])
                DMA(sc, "sld", wq_f[:, :, :],
                    weight_dram.ap().rearrange("(k p) e -> p k e", p=128)[:, :, q * 512:(q + 1) * 512])
            mark(f"fwq_{tag}_{q}", "sld")

            for cg in range(2):
                for ec in range(4):
                    bank = ec % 2 + (2 if to_skip else 0)

                    @block.tensor
                    def _(t, q=q, cg=cg, ec=ec, bank=bank):
                        waitm(t, f"fwq_{tag}_{q}")
                        if q == 0 and cg == 0 and ec == 0:
                            waitm(t, "ag_fr", f"xnT{NCH - 1}")
                            t.wait_ge(sems["dv"], modcp_dv)
                        pe_take_bank(t, bank)
                        for k in range(16):
                            last = t.matmul(pb[bank][:, :],
                                            wq_f[:, k, ec * 128:(ec + 1) * 128],
                                            xnT[:, k, cg * 512:(cg + 1) * 512],
                                            start=(k == 0), stop=(k == 15))
                        inc(last, "pe", 1)
                    mark(f"fmm_{tag}_{q}_{cg}_{ec}", "pe")

                    @block.scalar
                    def _(sc, q=q, cg=cg, ec=ec, bank=bank):
                        waitm(sc, f"fmm_{tag}_{q}_{cg}_{ec}")
                        if to_skip:
                            par = ec % 2
                            pm = marks.get(f"sksd_{tag}_{q}_{cg}_{ec - 2}") or \
                                marks.get(f"sksd_{tag}_{q}_{cg - 1}_{2 + ec}") or \
                                marks.get(f"sksd_{tag}_{q - 1}_1_{2 + ec}")
                            if pm:
                                sc.wait_ge(sems["sld"], pm["sld"])
                            a = sc.activation(sk_st[:, par, :], pb[bank][:, :], AF.Copy)
                            inc(a, "sc", 1)
                            bank_release(bank, "sc")
                            wait_cur(sc, "sc")
                            DMA(sc, "sld",
                                skip_d.ap()[q * 4 + ec, :, cg * 512:(cg + 1) * 512],
                                sk_st[:, par, :])
                            mark(f"sksd_{tag}_{q}_{cg}_{ec}", "sld")
                        else:
                            pm = marks.get(f"a2ast_{q - 2}_{cg}")
                            if pm:
                                sc.wait_ge(sems["sst"], pm["sst"])
                            a = sc.activation(stgf[:, q % 2, ec, cg, :],
                                              pb[bank][:, :], AF.Copy)
                            inc(a, "sc", 1)
                            bank_release(bank, "sc")
                mark(f"fevg_{tag}_{q}_{cg}", "sc", "sld")
            mark(f"fmmq_{tag}_{q}", "pe")

            if not to_skip:
                for cg in range(2):
                    @block.scalar
                    def _(sc, q=q, cg=cg):
                        sc.wait_ge(sems["sc"], marks[f"fevg_wi_{q}_{cg}"]["sc"])
                        for cb in range(4):
                            c = cg * 4 + cb
                            DMA(sc, "sst",
                                a2a_fi.ap()[c].rearrange(
                                    "j (eo p) t -> p j eo t", p=128)[:, q],
                                stgf[:, q % 2, :, cg, cb * 128:(cb + 1) * 128])
                    mark(f"a2ast_{q}_{cg}", "sst")

    front_pass(ago_wi, to_skip=False)

    for c in range(NCH):
        @block.gpsimd
        def _(g, c=c):
            cg = c // 4
            g.wait_ge(sems["sst"], marks[f"a2ast_7_{cg}"]["sst"])
            with allow_ici_all_to_all():
                c1 = g.collective_compute("AllToAll", AL.bypass, replica_groups=RG,
                                          ins=[a2a_fi.ap()[c].opt()],
                                          outs=[a2a_fo.ap()[c].opt()])
            inc(c1, "cc", 1)
        mark(f"a2af_{c}", "cc")

    mark("front_free", "gst", "syn", "dv", "sc", "sld")
    front_pass(ago_ws, to_skip=True)
    mark("front_done", "pe", "sc", "sld")

    if STAGE >= 5:
        def prep_ffn(w_name, agi, ago, scol, nch, width, mark_pw, tag,
                     load=True, prev_tag=None):
            pwv = pw.ap().rearrange("p c m -> p (c m)")[:, 0:nch * width] \
                .rearrange("p (c m) -> p c m", m=width)

            if load:
                @block.gpsimd
                def _(g):
                    waitm(g, mark_pw)
                    if prev_tag:
                        waitm(g, f"agi_{prev_tag}")
                    DMA(g, "gld", pwv,
                        ti[w_name].ap().rearrange("(c p) m -> p c m", p=128))
                mark(f"ld_{tag}", "gld")

            flat = pw.ap().rearrange("p c m -> p (c m)")
            wm_sf = wm_s.ap().rearrange("p k m -> p (k m)")
            for piece in range(16):
                sl = slice(piece * 1024, (piece + 1) * 1024)
                buf = pqs[:, :] if piece % 2 == 0 else wm_sf[:, 0:1024]

                @block.scalar
                def _(sc, sl=sl, piece=piece, buf=buf):
                    if piece == 0:
                        if load:
                            waitm(sc, f"ld_{tag}", "wscal")
                        else:
                            waitm(sc, "wscal")
                    if piece == 1 and tag == "w3":
                        sc.wait_ge(sems["pe"], marks["modmm11"]["pe"])
                    if piece >= 2:
                        sc.wait_ge(sems["dv"],
                                   marks[f"qc_{tag}_{piece - 2}"]["dv"])
                    a = sc.activation(buf, flat[:, sl], AF.Relu,
                                      bias=cst[:, 2:3], scale=wsc[:, 4 + scol:5 + scol])
                    inc(a, "sc", 1)
                    wait_cur(sc, "sc")
                    a = sc.activation(buf, buf, AF.Copy, bias=-MAGIC)
                    inc(a, "sc", 1)
                mark(f"rq_{tag}_{piece}", "sc")

                @block.vector
                def _(v, sl=sl, piece=piece, buf=buf):
                    waitm(v, f"rq_{tag}_{piece}")
                    inc(v.tensor_scalar(flat[:, sl], buf, -1.0, 1.0,
                                        AL.max, AL.min), "dv", 1)
                mark(f"qc_{tag}_{piece}", "dv")
            mark(f"q_{tag}", "dv")

            nf = width // 128
            pt2 = pt_s.ap().rearrange("p q m -> p (q m)")[:, 0:nf * nch * 128] \
                .rearrange("p (f j) -> p f j", f=nf)

            @block.sync
            def _(s):
                waitm(s, f"q_{tag}")
                for c in range(nch):
                    DMAT(s, "syn", pt2[:, :, c * 128:(c + 1) * 128], pwv[:, c, :])
                s.wait_ge(sems["syn"], cnt["syn"])
                DMA(s, "syn", agi.ap().rearrange("(f p) x -> p f x", p=128), pt2)
            mark(f"agi_{tag}", "syn")


        prep_ffn("w3_sl", agi_w3, ago_w3, 2, 16, FSH, "wsum", "w3", load=False)
        prep_ffn("w1_sl", agi_w1, ago_w1, 0, 64, DSH, "q_w3", "w1",
                 prev_tag="w3")
        prep_ffn("w2_sl", agi_w2, ago_w2, 1, 64, DSH, "q_w1", "w2",
                 prev_tag="w1")
    mark("front_free", "gst", "syn", "dv", "sc", "sld", "gp", "gld")
    mark("front_done", "pe", "sc", "sld")

    free_group(PR)
    free_group(EA)
    SCg = "scan"
    xcat = alloc(SCg, "xcat", [128, 4, 4 + CH_T], BF16)
    xsilu = alloc(SCg, "xsilu", [128, 2, 4, CH_T], BF16)
    dt_t = alloc(SCg, "dt_t", [128, 4, CH_T], BF16)
    dt1f = alloc(SCg, "dt1f", [128, CH_T], BF16)
    bbc = alloc(SCg, "bbc", [128, NS, 512], BF16)
    cbc = alloc(SCg, "cbc", [128, NS, 512], BF16)
    dub = alloc(SCg, "dub", [128, 2, 512], BF16)
    abuf = alloc(SCg, "abuf", [128, NS, 514], BF16)
    bbuf = alloc(SCg, "bbuf", [128, NS, 514], BF16)
    hprod = alloc(SCg, "hprod", [128, NS, 514], BF16)
    y_t = alloc(SCg, "y_t", [128, 4, CH_T], BF16)
    yo_t = alloc(SCg, "yo_t", [128, 4, CH_T], BF16)
    sp0 = alloc(SCg, "sp0", [128, 4, 512], BF16)
    sp1 = alloc(SCg, "sp1", [128, 4, 512], BF16)
    prj_s = alloc(SCg, "prj_s", [128, CH_T], F32)
    prj_bc = alloc(SCg, "prj_bc", [48, CH_T], F32)

    @block.vector
    def _(v):
        waitm(v, "front_free")
        v.memset(abuf[:, :, 512:513], 0.0)
        v.memset(abuf[:, :, 513:514], 1.0)
        a = v.memset(bbuf[:, :, 512:514], 0.0)
        inc(a, "dv", 1)
    mark("abz", "dv")

    def scan_front(c):
        xpar = c % 2
        xs = xsilu[:, xpar]

        @block.scalar
        def _(sc, c=c):
            waitm(sc, f"a2af_{c}")
            if c == 0:
                waitm(sc, "front_free")
            if c >= 1:
                sc.wait_ge(sems["dv"], marks[f"conv{c - 1}"]["dv"])
            for gg in range(4):
                DMA(sc, "sld",
                    xcat[:, gg, 4:4 + CH_T].rearrange("p (j t) -> p j t", t=BLK),
                    a2a_fo.ap()[c].rearrange("j (g p) t -> p g j t", p=128)[:, gg])
        mark(f"xm{c}", "sld")

        @block.vector
        def _(v, c=c, xs=xs):
            waitm(v, f"xm{c}", "aneg")
            if c == 0:
                waitm(v, "front_free")
                a = v.memset(xcat[:, :, 1:4], 0.0)
            else:
                a = v.tensor_copy(xcat[:, :, 1:4], carry_x[:, :, 0:3])
            inc(a, "dv", 1)
            if c >= 2:
                v.wait_ge(sems["dv"], marks[f"ytot{c - 2}"]["dv"])
            wait_cur(v, "dv")
            for gg in range(4):
                acc = xs[:, gg, :]
                a = v.tensor_scalar(acc, xcat[:, gg, 1:1 + CH_T],
                                    cw_s[:, 4 * gg:4 * gg + 1], None, AL.mult)
                inc(a, "dv", 1)
                wait_cur(v, "dv")
                for k in (1, 2, 3):
                    a = v.scalar_tensor_tensor(acc, xcat[:, gg, 1 + k:1 + k + CH_T],
                                               cw_s[:, 4 * gg + k:4 * gg + k + 1],
                                               acc, AL.mult, AL.add)
                    inc(a, "dv", 1)
                    wait_cur(v, "dv")
            a = v.tensor_copy(carry_x[:, :, 0:3], xcat[:, :, 1 + CH_T:4 + CH_T])
            inc(a, "dv", 1)
        mark(f"conv{c}", "dv")

        @block.scalar
        def _(sc, c=c, xs=xs):
            waitm(sc, f"conv{c}")
            for gg in range(4):
                a = sc.activation(xs[:, gg, :], xs[:, gg, :], AF.Silu,
                                  bias=cb_s[:, gg:gg + 1])
            inc(a, "sc", 1)
        mark(f"xsl{c}", "sc")

        for half in range(2):
            @block.tensor
            def _(t, c=c, half=half, xs=xs):
                t.wait_ge(sems["sc"], marks[f"xsl{c}"]["sc"])
                if c == 0 and half == 0:
                    waitm(t, "smallT")
                for b in (4, 5, 6):
                    pe_take_bank(t, b)
                sl = slice(half * 512, (half + 1) * 512)
                for gg in range(4):
                    t.matmul(pb[4][:, :], wd1T[:, gg, :], xs[:, gg, sl],
                             start=(gg == 0), stop=(gg == 3))
                for gg in range(4):
                    t.matmul(pb[5][0:NS, :], wbmT[:, gg, :], xs[:, gg, sl],
                             start=(gg == 0), stop=(gg == 3))
                for gg in range(4):
                    last = t.matmul(pb[6][0:NS, :], wcmT[:, gg, :], xs[:, gg, sl],
                                    start=(gg == 0), stop=(gg == 3))
                inc(last, "pe", 1)
            mark(f"prjmm{c}_{half}", "pe")

            @block.vector
            def _(v, c=c, half=half):
                waitm(v, f"prjmm{c}_{half}")
                if c > 0 and half == 0:
                    v.wait_ge(sems["sst"], marks[f"arst{c - 1}"]["sst"])
                sl = slice(half * 512, (half + 1) * 512)
                v.tensor_copy(prj_s[:, sl], pb[4][:, :])
                v.tensor_copy(prj_bc[0:NS, sl], pb[5][0:NS, :])
                a = v.tensor_copy(prj_bc[32:32 + NS, sl], pb[6][0:NS, :])
                inc(a, "dv", 1)
            mark(f"prjcp{c}_{half}", "dv")
            for b in (4, 5, 6):
                bank_release(b, "dv")

        @block.scalar
        def _(sc, c=c):
            sc.wait_ge(sems["dv"], marks[f"prjcp{c}_1"]["dv"])
            if c == 0:
                waitm(sc, "front_free")
            DMA(sc, "sst", ari_p.ap()[c, 0:128, :], prj_s[:, :])
            DMA(sc, "sst", ari_p.ap()[c, 128:144, :], prj_bc[0:NS, :])
            DMA(sc, "sst", ari_p.ap()[c, 144:160, :], prj_bc[32:32 + NS, :])
        mark(f"arst{c}", "sst")

        @block.gpsimd
        def _(g, c=c):
            g.wait_ge(sems["sst"], marks[f"arst{c}"]["sst"])
            c1 = g.collective_compute("AllReduce", AL.add, replica_groups=RG,
                                      ins=[ari_p.ap()[c].opt()],
                                      outs=[aro_p.ap()[c].opt()])
            inc(c1, "cc", 1)
        mark(f"ar{c}", "cc")

    def scan_back(c):
        xpar = c % 2
        xs = xsilu[:, xpar]

        @block.gpsimd
        def _(g, c=c):
            waitm(g, f"ar{c}")
            if c > 0:
                g.wait_ge(sems["pe"], marks[f"dtmm{c - 1}_3_1"]["pe"])
            DMA(g, "gld", dt1f[:, :], aro_p.ap()[c, 0:128, :])
            DMA(g, "gld", bct_d.ap()[c], aro_p.ap()[c, 128:160, :])
        mark(f"arp{c}", "gld")

        for half in range(2):
            for gg in range(4):
                bk = 6 + (gg * 2 + half) % 2

                @block.tensor
                def _(t, c=c, gg=gg, half=half, bk=bk):
                    waitm(t, f"arp{c}")
                    pe_take_bank(t, bk)
                    last = t.matmul(pb[bk][:, :], wd2T[:, gg, :],
                                    dt1f[:, half * 512:(half + 1) * 512],
                                    start=True, stop=True)
                    inc(last, "pe", 1)
                mark(f"dtmm{c}_{gg}_{half}", "pe")

                @block.vector
                def _(v, c=c, gg=gg, half=half, bk=bk):
                    waitm(v, f"dtmm{c}_{gg}_{half}")
                    if gg == 0:
                        pm = marks.get(f"dtsp{c}_{half - 1}") or \
                            marks.get(f"dtsp{c - 1}_1")
                        if pm:
                            v.wait_ge(sems["dv"], pm["dv"])
                    a = v.tensor_scalar(sp0[:, gg, :], pb[bk][:, :],
                                        bd2_s[:, gg:gg + 1], None, AL.add)
                    inc(a, "dv", 1)
                    bank_release(bk, "dv")
            mark(f"dtpre{c}_{half}", "dv")

            @block.scalar
            def _(sc, c=c, half=half):
                waitm(sc, f"dtpre{c}_{half}")
                s0 = sp0.ap().rearrange("p g t -> p (g t)")
                s1 = sp1.ap().rearrange("p g t -> p (g t)")
                a = sc.activation(s1, s0, AF.Abs)
                inc(a, "sc", 1)
                wait_cur(sc, "sc")
                a = sc.activation(s1, s1, AF.Exp, scale=cst[:, 5:6])
                inc(a, "sc", 1)
                wait_cur(sc, "sc")
                a = sc.activation(s1, s1, AF.Ln, bias=cst[:, 6:7])
                inc(a, "sc", 1)
            mark(f"dtln{c}_{half}", "sc")

            @block.vector
            def _(v, c=c, half=half):
                waitm(v, f"dtln{c}_{half}")
                if c > 0 and half == 0:
                    v.wait_ge(sems["dv"], marks[f"scanl{c - 1}"]["dv"])
                sl = slice(half * 512, (half + 1) * 512)
                a = v.scalar_tensor_tensor(
                    dt_t[:, :, sl], sp0[:, :, :], 0.0, sp1[:, :, :],
                    AL.max, AL.add)
                inc(a, "dv", 1)
            mark(f"dtsp{c}_{half}", "dv")
        mark(f"dtsp{c}", "dv")

        for half in range(2):
            sl = slice(half * 512, (half + 1) * 512)

            @block.scalar
            def _(sc, c=c, half=half):
                waitm(sc, f"arp{c}")
                pm = marks.get(f"scanh{c}_{half - 1}") or \
                    marks.get(f"scanh{c - 1}_1")
                if pm:
                    sc.wait_ge(sems["dv"], pm["dv"])
                DMA(sc, "sld", bbc[:, :, :],
                    bct_d.ap()[c, 0:NS, sl][None, :, :]
                    .broadcast_to((128, NS, 512)))
                DMA(sc, "sld", cbc[:, :, :],
                    bct_d.ap()[c, NS:2 * NS, sl][None, :, :]
                    .broadcast_to((128, NS, 512)))
            mark(f"bcld{c}_{half}", "sld")

            for gg in range(4):
                @block.vector
                def _(v, c=c, half=half, gg=gg, sl=sl, xs=xs):
                    if gg == 0:
                        waitm(v, f"bcld{c}_{half}")
                        waitm(v, f"dtsp{c}_{half}")
                        if half == 0:
                            pm = marks.get(f"bwdst{c - 1}")
                            if pm:
                                v.wait_ge(sems["sst"], pm["sst"])
                    a = v.tensor_tensor(dub[:, gg % 2, :], dt_t[:, gg, sl],
                                        xs[:, gg, sl], AL.mult)
                    inc(a, "dv", 1)
                mark(f"du{c}_{half}_{gg}", "dv")

                for nh in range(2):
                    @block.scalar
                    def _(sc, c=c, half=half, gg=gg, sl=sl, nh=nh):
                        if gg == 0 and half == 0 and nh == 0:
                            waitm(sc, f"dtsp{c}")
                            if c == 0:
                                waitm(sc, "abz")
                        pm = marks.get(f"scne{c}_{half}_{gg - 1}") or \
                            marks.get(f"scne{c}_{half - 1}_3") or \
                            marks.get(f"scne{c - 1}_1_3")
                        if pm:
                            sc.wait_ge(sems["dv"], pm["dv"])
                        for n in range(8 * nh, 8 * nh + 8):
                            a = sc.activation(abuf[:, n, 0:512], dt_t[:, gg, sl],
                                              AF.Exp,
                                              scale=a_neg[:, gg * 16 + n:gg * 16 + n + 1])
                        inc(a, "sc", 1)
                    mark(f"aex{c}_{half}_{gg}_{nh}", "sc")

                @block.vector
                def _(v, c=c, half=half, gg=gg, sl=sl):
                    waitm(v, f"aex{c}_{half}_{gg}_0", f"du{c}_{half}_{gg}")
                    a = v.tensor_tensor(bbuf[:, 0:8, 0:512],
                                        dub[:, gg % 2, :][:, None, :]
                                        .broadcast_to((128, 8, 512)),
                                        bbc[:, 0:8, :], AL.mult)
                    inc(a, "dv", 1)
                    waitm(v, f"aex{c}_{half}_{gg}_1")
                    a = v.tensor_tensor(bbuf[:, 8:16, 0:512],
                                        dub[:, gg % 2, :][:, None, :]
                                        .broadcast_to((128, 8, 512)),
                                        bbc[:, 8:16, :], AL.mult)
                    inc(a, "dv", 1)
                    a = v.tensor_copy(bbuf[:, 0:15, 512:513],
                                      carry_h[:, gg * 16 + 1: gg * 16 + 16]
                                      [:, :, None])
                    inc(a, "dv", 1)
                    wait_cur(v, "dv")
                    a = v.tensor_tensor_scan(
                        hprod.ap().rearrange("p n t -> p (n t)"),
                        abuf.ap().rearrange("p n t -> p (n t)"),
                        bbuf.ap().rearrange("p n t -> p (n t)"),
                        carry_h[:, gg * 16: gg * 16 + 1],
                        AL.mult, AL.add)
                    inc(a, "dv", 1)
                    mark(f"scne{c}_{half}_{gg}", "dv")
                    wait_cur(v, "dv")
                    a = v.tensor_copy(carry_h[:, gg * 16: gg * 16 + 16],
                                      hprod[:, :, 511:512]
                                      .rearrange("p n one -> p (n one)"))
                    inc(a, "dv", 1)
                    wait_cur(v, "dv")
                    a = v.tensor_tensor(hprod[:, :, 0:512], hprod[:, :, 0:512],
                                        cbc[:, :, :], AL.mult)
                    inc(a, "dv", 1)
                    mark(f"scnh{c}_{half}_{gg}_0", "dv")
                    mark(f"scnh{c}_{half}_{gg}_1", "dv")
                    wait_cur(v, "dv")
                    a = v.tensor_tensor(hprod[:, 0:8, 0:512], hprod[:, 0:8, 0:512],
                                        hprod[:, 8:16, 0:512], AL.add)
                    inc(a, "dv", 1)
                    wait_cur(v, "dv")
                    a = v.tensor_tensor(hprod[:, 0:4, 0:512], hprod[:, 0:4, 0:512],
                                        hprod[:, 4:8, 0:512], AL.add)
                    inc(a, "dv", 1)
                    wait_cur(v, "dv")
                    a = v.tensor_tensor(hprod[:, 0:2, 0:512], hprod[:, 0:2, 0:512],
                                        hprod[:, 2:4, 0:512], AL.add)
                    inc(a, "dv", 1)
                    wait_cur(v, "dv")
                    a = v.tensor_tensor(y_t[:, gg, sl], hprod[:, 0, 0:512],
                                        hprod[:, 1, 0:512], AL.add)
                    inc(a, "dv", 1)
                mark(f"scang{c}_{half}_{gg}", "dv")
            mark(f"scanh{c}_{half}", "dv")
        mark(f"scanl{c}", "dv")

        @block.vector
        def _(v, c=c, xs=xs):
            wait_cur(v, "dv")
            for gg in range(4):
                a = v.scalar_tensor_tensor(yo_t[:, gg, :], xs[:, gg, :],
                                           wd_s[:, gg:gg + 1], y_t[:, gg, :],
                                           AL.mult, AL.add)
                inc(a, "dv", 1)
        mark(f"ytot{c}", "dv")

        @block.scalar
        def _(sc, c=c):
            sc.wait_ge(sems["dv"], marks[f"ytot{c}"]["dv"])
            for gg in range(4):
                DMA(sc, "sst",
                    a2a_bi.ap()[c].rearrange("j (g p) t -> p g j t", p=128)[:, gg],
                    yo_t[:, gg, :].rearrange("p (j t) -> p j t", t=BLK))
        mark(f"bwdst{c}", "sst")

        @block.gpsimd
        def _(g, c=c):
            g.wait_ge(sems["sst"], marks[f"bwdst{c}"]["sst"])
            with allow_ici_all_to_all():
                c1 = g.collective_compute("AllToAll", AL.bypass, replica_groups=RG,
                                          ins=[a2a_bi.ap()[c].opt()],
                                          outs=[a2a_bo.ap()[c].opt()])
            inc(c1, "cc", 1)
        mark(f"a2ab_{c}", "cc")

    for c in range(NCH):
        scan_front(c)
        if c >= 1:
            scan_back(c - 1)
    scan_back(NCH - 1)

    if STAGE >= 4:
        mark("scan_all", "dv", "sc", "pe", "cc", "gst", "sld", "syn", "gld",
             "gp", "dbgd")
        if STAGE >= 5:
            for _tag, _agi, _ago in (("w3", agi_w3, ago_w3),
                                     ("w1", agi_w1, ago_w1),
                                     ("w2", agi_w2, ago_w2)):
                @block.gpsimd
                def _(g, _tag=_tag, _agi=_agi, _ago=_ago):
                    waitm(g, f"agi_{_tag}")
                    c1 = g.collective_compute(
                        "AllGather", AL.bypass, replica_groups=RG,
                        ins=[_agi.ap().opt()], outs=[_ago.ap().opt()])
                    inc(c1, "cc", 1)
                mark(f"ag_{_tag}", "cc")
        free_group(SCg)
        free_group(FL)
        BK = "back"
        modb = alloc(BK, "modb", [128, 4, D], BF16)
        bscr = alloc(BK, "bscr", [128, 2, D], BF16)
        fscr = alloc(BK, "fscr", [128, D], F32)
        bwk = alloc(BK, "bwk", [128, D], BF16)
        x2b = alloc(BK, "x2b", [128, 4, D], BF16)
        qsb = alloc(BK, "qsb", [128, 4, 16], F32)
        outb = alloc(BK, "outb", [128, 2, 512], F32)
        st2 = alloc(BK, "st2", [128, 32], F32)
        xqT2 = alloc(BK, "xqT2", [128, 16, 4 * BLK], BF16)
        hqT0v = xqT2.ap().rearrange("p a b -> p (a b)") \
            .rearrange("p (f j) -> p f j", j=BLK)

        for s in ("wl0", "wl1", "wl2", "wl3"):
            sem(s)

        @block.gpsimd
        def _(g):
            waitm(g, "scan_all")
            m6 = ago_mod.ap().rearrange("one (s d) -> one s d", s=6)[0]
            DMA(g, "gld", modb[:, :, :],
                m6[2:6][None, :, :].broadcast_to((128, 4, D)))
        mark("modbk", "gld")

        @block.scalar
        def _(sc):
            waitm(sc, "modbk", "scan_all")
            sc.activation(bscr[:, 0, :], modb[:, 0, :], AF.Sign)
            a = sc.activation(bscr[:, 1, :], modb[:, 3, :], AF.Sign)
            inc(a, "sc", 1)
            wait_cur(sc, "sc")
            sc.activation(modb[:, 0, :], modb[:, 0, :], AF.Abs)
            a = sc.activation(modb[:, 3, :], modb[:, 3, :], AF.Abs)
            inc(a, "sc", 1)
        mark("gsign", "sc")

        @block.vector
        def _(v):
            waitm(v, "gsign", "scan_all")
            v.tensor_scalar(modb[:, 0, :], modb[:, 0, :], 0.25, None, AL.max)
            a = v.tensor_scalar(modb[:, 3, :], modb[:, 3, :], 0.25, None, AL.max)
            inc(a, "dv", 1)
            wait_cur(v, "dv")
            v.tensor_tensor(modb[:, 0, :], modb[:, 0, :], bscr[:, 0, :], AL.mult)
            a = v.tensor_tensor(modb[:, 3, :], modb[:, 3, :], bscr[:, 1, :], AL.mult)
            inc(a, "dv", 1)
        mark("gclamp", "dv")

        for half in range(2):
            BA = f"backA{half}"
            yr4 = alloc(BA, f"yr4_{half}", [128, 4, 32, BLK], BF16)
            ssb = alloc(BA, f"ssb_{half}", [128, 32, BLK], BF16)
            mix4 = alloc(BA, f"mix4_{half}", [128, 4, D], BF16)
            wob = alloc(BA, f"wob_{half}", [128, 4, 512], BF16)

            for ci in range(4):
                c = half * 4 + ci

                @block.scalar
                def _(sc, c=c, ci=ci, half=half):
                    waitm(sc, f"a2ab_{c}")
                    if half == 0 and ci == 0:
                        waitm(sc, "gclamp")
                    if half == 1 and ci == 0:
                        waitm(sc, f"bkdone_0")
                    if ci > 0:
                        sc.wait_ge(sems["dv"], marks[f"yf{half}_{ci - 1}"]["dv"])
                    for gg in range(4):
                        DMA(sc, "sld",
                            yr4[:, ci, gg:32:4, :],
                            a2a_bo.ap()[c].rearrange("j (g p) t -> p g j t",
                                                     p=128)[:, gg])
                    DMA(sc, "sld", ssb[:, :, :],
                        skip_d.ap()[:, :, c * BLK:(c + 1) * BLK]
                        .rearrange("e p t -> p e t"))
                mark(f"yrl{half}_{ci}", "sld")

                @block.scalar
                def _(sc, ci=ci, half=half):
                    waitm(sc, f"yrl{half}_{ci}")
                    a = sc.activation(ssb.ap().rearrange("p e t -> p (e t)"),
                                      ssb.ap().rearrange("p e t -> p (e t)"),
                                      AF.Silu)
                    inc(a, "sc", 1)
                mark(f"ss{half}_{ci}", "sc")

                @block.vector
                def _(v, ci=ci, half=half):
                    waitm(v, f"ss{half}_{ci}")
                    a = v.tensor_tensor(
                        yr4.ap()[:, ci].rearrange("p e t -> p (e t)"),
                        yr4.ap()[:, ci].rearrange("p e t -> p (e t)"),
                        ssb.ap().rearrange("p e t -> p (e t)"), AL.mult)
                    inc(a, "dv", 1)
                mark(f"yf{half}_{ci}", "dv")

            for dgrp in range(4):
                for ec in range(32):
                    wl = f"wl{ec % 4}"

                    @block.scalar
                    def _(sc, dgrp=dgrp, ec=ec, wl=wl, half=half):
                        gi = dgrp * 32 + ec - 4
                        if gi >= 0:
                            pm = marks.get(f"womm{half}_{gi // 32}_{gi % 32}")
                            if pm:
                                sc.wait_ge(sems["pe"], pm["pe"])
                        DMA(sc, wl, wob[:, ec % 4, :],
                            ago_wo.ap()[ec * 128:(ec + 1) * 128,
                                        dgrp * 512:(dgrp + 1) * 512])
                        mark(f"wol{half}_{dgrp}_{ec}", wl)

                    @block.tensor
                    def _(t, dgrp=dgrp, ec=ec, half=half):
                        waitm(t, f"wol{half}_{dgrp}_{ec}")
                        if ec == 0:
                            for ci in range(4):
                                waitm(t, f"yf{half}_{ci}")
                                pe_take_bank(t, 4 * (dgrp % 2) + ci)
                        for ci in range(4):
                            last = t.matmul(pb[4 * (dgrp % 2) + ci][:, :],
                                            yr4[:, ci, ec, :],
                                            wob[:, ec % 4, :],
                                            start=(ec == 0), stop=(ec == 31))
                        inc(last, "pe", 1)
                        mark(f"womm{half}_{dgrp}_{ec}", "pe")

                @block.scalar
                def _(sc, dgrp=dgrp, half=half):
                    waitm(sc, f"womm{half}_{dgrp}_31")
                    for ci in range(4):
                        a = sc.activation(mix4[:, ci, dgrp * 512:(dgrp + 1) * 512],
                                          pb[4 * (dgrp % 2) + ci][:, :], AF.Copy)
                        inc(a, "sc", 1)
                        bank_release(4 * (dgrp % 2) + ci, "sc")
                mark(f"woev{half}_{dgrp}", "sc")

            for ci in range(4):
                c = half * 4 + ci

                @block.scalar
                def _(sc, c=c, ci=ci, half=half):
                    for dgrp in range(4):
                        waitm(sc, f"woev{half}_{dgrp}")
                    if ci > 0:
                        sc.wait_ge(sems["syn"], marks[f"xqT{half}_{ci - 1}"]["syn"])
                    DMA(sc, "sld", bwk[:, :], h_sp.ap()[c, :, :])
                    DMA(sc, "sld", fscr[:, :], ti["resid"].ap()[c, :, :])
                mark(f"hrl{half}_{ci}", "sld")

                @block.vector
                def _(v, c=c, ci=ci, half=half):
                    waitm(v, f"hrl{half}_{ci}")
                    a = v.tensor_tensor(bscr[:, 0, :], mix4[:, ci, :], bwk[:, :],
                                        AL.add)
                    inc(a, "dv", 1)
                    wait_cur(v, "dv")
                    a = v.tensor_tensor(bscr[:, 0, :], bscr[:, 0, :],
                                        modb[:, 0, :], AL.mult)
                    inc(a, "dv", 1)
                    wait_cur(v, "dv")
                    a = v.tensor_tensor(x2b[:, ci, :], fscr[:, :], bscr[:, 0, :],
                                        AL.add)
                    inc(a, "dv", 1)
                mark(f"x2{half}_{ci}", "dv")

                if DEBUG and STAGE >= 4:
                    @block.gpsimd
                    def _(g, c=c, ci=ci, half=half):
                        g.wait_ge(sems["dv"], marks[f"x2{half}_{ci}"]["dv"])
                        DMA(g, "dbgd", dbg["x2"].ap()[c], x2b[:, ci, :])
                    mark(f"dbgx2{c}", "dbgd")

                @block.vector
                def _(v, ci=ci, half=half):
                    waitm(v, f"x2{half}_{ci}")
                    x = x2b[:, ci, :]
                    for s4 in range(4):
                        a = v.bn_stats(st2[:, 6 * s4:6 * (s4 + 1)],
                                       x[:, 512 * s4:512 * (s4 + 1)])
                    inc(a, "dv", 1)
                    wait_cur(v, "dv")
                    a = v.bn_aggr(st2[:, 24:26], st2[:, 0:24])
                    inc(a, "dv", 1)
                mark(f"st2{half}_{ci}", "dv")

                @block.scalar
                def _(sc, ci=ci, half=half):
                    waitm(sc, f"st2{half}_{ci}")
                    a = sc.activation(st2[:, 26:27], st2[:, 25:26], AF.Sqrt,
                                      bias=cst[:, 0:1])
                    inc(a, "sc", 1)
                mark(f"sq2{half}_{ci}", "sc")

                @block.vector
                def _(v, ci=ci, half=half):
                    waitm(v, f"sq2{half}_{ci}")
                    a = v.reciprocal(st2[:, 27:28], st2[:, 26:27])
                    inc(a, "dv", 1)
                    wait_cur(v, "dv")
                    a = v.tensor_scalar(bwk[:, :], x2b[:, ci, :], st2[:, 24:25],
                                        st2[:, 27:28], AL.subtract, AL.mult)
                    inc(a, "dv", 1)
                    wait_cur(v, "dv")
                    a = v.tensor_tensor(bscr[:, 0, :], bwk[:, :], modb[:, 2, :],
                                        AL.mult)
                    inc(a, "dv", 1)
                    wait_cur(v, "dv")
                    a = v.tensor_tensor(bwk[:, :], bwk[:, :], bscr[:, 0, :],
                                        AL.add)
                    inc(a, "dv", 1)
                    wait_cur(v, "dv")
                    a = v.tensor_tensor(bwk[:, :], bwk[:, :], modb[:, 1, :],
                                        AL.add)
                    inc(a, "dv", 1)
                    wait_cur(v, "dv")
                    a = v.reduce_max(qsb[:, ci, 0:1], bwk[:, :], axis=AX,
                                     apply_absolute_value=True)
                    inc(a, "dv", 1)
                    wait_cur(v, "dv")
                    a = v.tensor_scalar(qsb[:, ci, 0:1], qsb[:, ci, 0:1],
                                        1e-5, None, AL.max)
                    inc(a, "dv", 1)
                mark(f"h2m{half}_{ci}", "dv")

                @block.scalar
                def _(sc, ci=ci, half=half):
                    waitm(sc, f"h2m{half}_{ci}")
                    a = sc.activation(bscr[:, 1, :], bwk[:, :], AF.Square,
                                      accum_out=qsb[:, ci, 1:2])
                    inc(a, "sc", 1)
                    wait_cur(sc, "sc")
                    a = sc.activation(qsb[:, ci, 2:3], qsb[:, ci, 1:2], AF.Sqrt,
                                      bias=cst[:, 1:2], scale=cst[:, 3:4])
                    inc(a, "sc", 1)
                mark(f"h2s{half}_{ci}", "sc")

                @block.vector
                def _(v, ci=ci, half=half):
                    waitm(v, f"h2s{half}_{ci}")
                    v.reciprocal(qsb[:, ci, 3:4], qsb[:, ci, 2:3])
                    a = v.reciprocal(qsb[:, ci, 4:5], qsb[:, ci, 0:1])
                    inc(a, "dv", 1)
                    wait_cur(v, "dv")
                    v.tensor_scalar(qsb[:, ci, 5:6], qsb[:, ci, 4:5], 127.0,
                                    None, AL.mult)
                    a = v.tensor_tensor(qsb[:, ci, 6:7], qsb[:, ci, 0:1],
                                        qsb[:, ci, 3:4], AL.mult)
                    inc(a, "dv", 1)
                    wait_cur(v, "dv")
                    a = v.tensor_scalar(qsb[:, ci, 7:8], qsb[:, ci, 6:7],
                                        1.0 / 127.0, None, AL.mult)
                    inc(a, "dv", 1)
                    wait_cur(v, "dv")
                    v.tensor_scalar(qsb[:, ci, 8:9], qsb[:, ci, 7:8],
                                    wsc[:, 0:1], None, AL.mult)
                    a = v.tensor_scalar(qsb[:, ci, 9:10], qsb[:, ci, 7:8],
                                        wsc[:, 1:2], None, AL.mult)
                    inc(a, "dv", 1)
                mark(f"qx{half}_{ci}", "dv")

                @block.scalar
                def _(sc, ci=ci, half=half):
                    waitm(sc, f"qx{half}_{ci}")
                    a = sc.activation(fscr[:, :], bwk[:, :], AF.Relu,
                                      bias=cst[:, 2:3], scale=qsb[:, ci, 5:6])
                    inc(a, "sc", 1)
                mark(f"xr{half}_{ci}", "sc")

                @block.vector
                def _(v, ci=ci, half=half):
                    waitm(v, f"xr{half}_{ci}")
                    a = v.tensor_scalar(bwk[:, :], fscr[:, :], -MAGIC, None,
                                        AL.add)
                    inc(a, "dv", 1)
                mark(f"xq{half}_{ci}", "dv")

                @block.sync
                def _(s, ci=ci, half=half):
                    waitm(s, f"xq{half}_{ci}")
                    DMAT(s, "syn", xqT2[:, :, ci * BLK:(ci + 1) * BLK],
                         bwk[:, :])
                mark(f"xqT{half}_{ci}", "syn")

            free_group(BA)
            BB = f"backB{half}"
            hfb = alloc(BB, f"hfb_{half}", [128, 4, DFF], BF16)
            wfb = alloc(BB, f"wfb_{half}", [128, 2, 16, 512], BF16)
            w3b = alloc(BB, f"w3b_{half}", [128, 4, 512], BF16)

            hst = alloc(BB, f"hst_{half}", [128, 4, 192], F32)

            @block.vector
            def _(v, half=half):
                waitm(v, f"xq{half}_3")
                for dgrp in range(4):
                    waitm(v, f"woev{half}_{dgrp}")
                a = v.memset(qsb[:, :, 10:12], 0.0)
                inc(a, "dv", 1)
            mark(f"hacc0{half}", "dv")

            for fg in range(32):
                wl = f"wl{fg % 2}"

                @block.scalar
                def _(sc, fg=fg, wl=wl, half=half):
                    if fg == 0:
                        waitm(sc, f"xq{half}_3")
                        for dgrp in range(4):
                            waitm(sc, f"woev{half}_{dgrp}")
                        waitm(sc, "ag_w1", "ag_w2")
                    pm = marks.get(f"w12mm{half}_{fg - 2}")
                    if pm:
                        sc.wait_ge(sems["pe"], pm["pe"])
                    DMA(sc, wl, wfb[:, fg % 2, :, 0:256],
                        ago_w1.ap().rearrange("(k p) f -> p k f", p=128)
                        [:, :, fg * 256:(fg + 1) * 256])
                    DMA(sc, wl, wfb[:, fg % 2, :, 256:512],
                        ago_w2.ap().rearrange("(k p) f -> p k f", p=128)
                        [:, :, fg * 256:(fg + 1) * 256])
                    mark(f"wfl{half}_{fg}", wl)

                for ci in range(4):
                    it = fg * 4 + ci
                    bk = it % 4
                    bp = it % 2

                    @block.tensor
                    def _(t, fg=fg, ci=ci, bk=bk, half=half):
                        if fg == 0 and ci == 0:
                            for cj in range(4):
                                waitm(t, f"xqT{half}_{cj}")
                        waitm(t, f"wfl{half}_{fg}")
                        pe_take_bank(t, bk)
                        for k in range(16):
                            last = t.matmul(pb[bk][:, :],
                                            xqT2[:, k, ci * BLK:(ci + 1) * BLK],
                                            wfb[:, fg % 2, k, :],
                                            start=(k == 0), stop=(k == 15))
                        inc(last, "pe", 1)
                        mark(f"w12mm{half}_{fg}_{ci}", "pe")
                    if ci == 3:
                        mark(f"w12mm{half}_{fg}", "pe")

                    @block.scalar
                    def _(sc, fg=fg, ci=ci, bk=bk, bp=bp, it=it, half=half):
                        waitm(sc, f"w12mm{half}_{fg}_{ci}")
                        pm = (marks.get(f"hsec{half}_{(it - 2) // 4}_{(it - 2) % 4}")
                              if it >= 2 else
                              marks.get(f"hsec{half - 1}_31_{it + 2}"))
                        if pm:
                            sc.wait_ge(sems["dv"], pm["dv"])
                        a = sc.activation(bscr[:, 0, bp * 256:bp * 256 + 256],
                                          pb[bk][:, 0:256],
                                          AF.Silu, scale=qsb[:, ci, 8:9])
                        inc(a, "sc", 1)
                    mark(f"sil{half}_{fg}_{ci}", "sc")

                    @block.vector
                    def _(v, fg=fg, ci=ci, bk=bk, bp=bp, half=half):
                        waitm(v, f"sil{half}_{fg}_{ci}")
                        a = v.scalar_tensor_tensor(
                            hfb[:, ci, fg * 256:(fg + 1) * 256],
                            pb[bk][:, 256:512], qsb[:, ci, 9:10],
                            bscr[:, 0, bp * 256:bp * 256 + 256],
                            AL.mult, AL.mult)
                        inc(a, "dv", 1)
                        bank_release(bk, "dv")
                        wait_cur(v, "dv")
                        a = v.reduce_max(qsb[:, ci, 12:13],
                                         hfb[:, ci, fg * 256:(fg + 1) * 256],
                                         axis=AX, apply_absolute_value=True)
                        inc(a, "dv", 1)
                        v.bn_stats(hst[:, ci, 6 * fg:6 * (fg + 1)],
                                   hfb[:, ci, fg * 256:(fg + 1) * 256])
                        wait_cur(v, "dv")
                        a = v.tensor_tensor(qsb[:, ci, 10:11], qsb[:, ci, 10:11],
                                            qsb[:, ci, 12:13], AL.max)
                        inc(a, "dv", 1)
                    mark(f"hsec{half}_{fg}_{ci}", "dv")
                    if fg == 31:
                        mark(f"hss{half}_31_{ci}", "dv")

            hqT_loc = {}
            for ci in range(4):
                @block.vector
                def _(v, ci=ci, half=half):
                    v.wait_ge(sems["dv"], marks[f"hss{half}_31_{ci}"]["dv"])
                    a = v.bn_aggr(qsb[:, ci, 14:16], hst[:, ci, :])
                    inc(a, "dv", 1)
                    wait_cur(v, "dv")
                    a = v.tensor_tensor(qsb[:, ci, 11:12], qsb[:, ci, 14:15],
                                        qsb[:, ci, 14:15], AL.mult)
                    inc(a, "dv", 1)
                    wait_cur(v, "dv")
                    a = v.tensor_tensor(qsb[:, ci, 11:12], qsb[:, ci, 11:12],
                                        qsb[:, ci, 15:16], AL.add)
                    inc(a, "dv", 1)
                mark(f"hag{half}_{ci}", "dv")

                @block.scalar
                def _(sc, ci=ci, half=half):
                    sc.wait_ge(sems["dv"], marks[f"hag{half}_{ci}"]["dv"])
                    a = sc.activation(qsb[:, ci, 14:15], qsb[:, ci, 11:12],
                                      AF.Sqrt, bias=cst[:, 1:2], scale=cst[:, 6:7])
                    inc(a, "sc", 1)
                mark(f"hq1{half}_{ci}", "sc")

                @block.vector
                def _(v, ci=ci, half=half):
                    waitm(v, f"hq1{half}_{ci}")
                    a = v.tensor_scalar(qsb[:, ci, 10:11], qsb[:, ci, 10:11],
                                        1e-5, None, AL.max)
                    inc(a, "dv", 1)
                    wait_cur(v, "dv")
                    v.reciprocal(qsb[:, ci, 15:16], qsb[:, ci, 14:15])
                    a = v.reciprocal(qsb[:, ci, 12:13], qsb[:, ci, 10:11])
                    inc(a, "dv", 1)
                    wait_cur(v, "dv")
                    v.tensor_scalar(qsb[:, ci, 12:13], qsb[:, ci, 12:13],
                                    127.0, None, AL.mult)
                    a = v.tensor_tensor(qsb[:, ci, 13:14], qsb[:, ci, 10:11],
                                        qsb[:, ci, 15:16], AL.mult)
                    inc(a, "dv", 1)
                    wait_cur(v, "dv")
                    a = v.tensor_scalar(qsb[:, ci, 13:14], qsb[:, ci, 13:14],
                                        1.0 / 127.0, None, AL.mult)
                    inc(a, "dv", 1)
                    wait_cur(v, "dv")
                    a = v.tensor_scalar(qsb[:, ci, 13:14], qsb[:, ci, 13:14],
                                        wsc[:, 2:3], None, AL.mult)
                    inc(a, "dv", 1)
                mark(f"hq2{half}_{ci}", "dv")

                for piece in range(4):
                    @block.scalar
                    def _(sc, ci=ci, piece=piece, half=half):
                        waitm(sc, f"hq2{half}_{ci}")
                        pm = marks.get(f"hrq{half}_{ci}_{piece - 1}") or \
                            (marks.get(f"hrq{half}_{ci - 1}_3") if piece == 0 else None)
                        if pm:
                            sc.wait_ge(sems["dv"], pm["dv"])
                        a = sc.activation(fscr[:, :],
                                          hfb[:, ci, piece * D:(piece + 1) * D],
                                          AF.Relu, bias=cst[:, 2:3],
                                          scale=qsb[:, ci, 12:13])
                        inc(a, "sc", 1)
                    mark(f"hrr{half}_{ci}_{piece}", "sc")

                    @block.vector
                    def _(v, ci=ci, piece=piece, half=half):
                        waitm(v, f"hrr{half}_{ci}_{piece}")
                        a = v.tensor_scalar(hfb[:, ci, piece * D:(piece + 1) * D],
                                            fscr[:, :], -MAGIC, None, AL.add)
                        inc(a, "dv", 1)
                    mark(f"hrq{half}_{ci}_{piece}", "dv")

                @block.sync
                def _(s, ci=ci, half=half):
                    waitm(s, f"hrq{half}_{ci}_3")
                    if ci > 0:
                        waitm(s, f"hqT{half}_{ci - 1}")
                    if ci == 0:
                        s.wait_ge(sems["pe"], marks[f"w12mm{half}_31"]["pe"])
                        dst = hqT0v
                    else:
                        dst = hfb.ap()[:, ci - 1].rearrange("p (f j) -> p f j",
                                                            j=BLK)
                    DMAT(s, "syn", dst, hfb.ap()[:, ci])
                mark(f"hqT{half}_{ci}", "syn")
                hqT_loc[ci] = ci

            for dgrp in range(4):
                for fk in range(64):
                    wl = f"wl{fk % 4}"

                    @block.scalar
                    def _(sc, dgrp=dgrp, fk=fk, wl=wl, half=half):
                        if dgrp == 0 and fk == 0:
                            waitm(sc, f"xq{half}_3")
                            waitm(sc, "ag_w3")
                        gi = dgrp * 64 + fk - 4
                        if gi >= 0:
                            pm = marks.get(f"w3mm{half}_{gi // 64}_{gi % 64}")
                            if pm:
                                sc.wait_ge(sems["pe"], pm["pe"])
                        DMA(sc, wl, w3b[:, fk % 4, :],
                            ago_w3.ap()[fk * 128:(fk + 1) * 128,
                                        dgrp * 512:(dgrp + 1) * 512])
                        mark(f"w3l{half}_{dgrp}_{fk}", wl)

                    @block.tensor
                    def _(t, dgrp=dgrp, fk=fk, half=half):
                        waitm(t, f"w3l{half}_{dgrp}_{fk}")
                        if fk == 0:
                            if dgrp == 0:
                                for cj in range(4):
                                    waitm(t, f"hqT{half}_{cj}")
                            for ci in range(4):
                                pe_take_bank(t, 4 + ci)
                        for ci in range(4):
                            src = hqT0v if ci == 0 else \
                                hfb.ap()[:, ci - 1].rearrange("p (f j) -> p f j",
                                                              j=BLK)
                            last = t.matmul(pb[4 + ci][:, :],
                                            src[:, fk, :],
                                            w3b[:, fk % 4, :],
                                            start=(fk == 0), stop=(fk == 63))
                        inc(last, "pe", 1)
                        mark(f"w3mm{half}_{dgrp}_{fk}", "pe")

                for ci in range(4):
                    c = half * 4 + ci

                    @block.vector
                    def _(v, dgrp=dgrp, ci=ci, half=half):
                        waitm(v, f"w3mm{half}_{dgrp}_63")
                        pm = marks.get(f"oev{half}_{dgrp}_{ci - 1}") or \
                            (marks.get(f"oev{half}_{dgrp - 1}_3") if ci == 0 else None)
                        if pm:
                            v.wait_ge(sems["outd"], pm["outd"])
                        a = v.tensor_scalar(bscr[:, 0, 0:512], pb[4 + ci][:, :],
                                            qsb[:, ci, 13:14], None, AL.mult)
                        inc(a, "dv", 1)
                        wait_cur(v, "dv")
                        a = v.tensor_tensor(bscr[:, 0, 0:512], bscr[:, 0, 0:512],
                                            modb[:, 3, dgrp * 512:(dgrp + 1) * 512],
                                            AL.mult)
                        inc(a, "dv", 1)
                        wait_cur(v, "dv")
                        a = v.tensor_tensor(outb[:, ci % 2, :],
                                            x2b[:, ci, dgrp * 512:(dgrp + 1) * 512],
                                            bscr[:, 0, 0:512], AL.add)
                        inc(a, "dv", 1)
                    mark(f"ow{half}_{dgrp}_{ci}", "dv")

                    @block.scalar
                    def _(sc, dgrp=dgrp, ci=ci, c=c, half=half):
                        sc.wait_ge(sems["dv"], marks[f"ow{half}_{dgrp}_{ci}"]["dv"])
                        DMA(sc, "outd", out_t.ap()[c, :, dgrp * 512:(dgrp + 1) * 512],
                            outb[:, ci % 2, :])
                    mark(f"oev{half}_{dgrp}_{ci}", "outd")
                for ci in range(4):
                    bank_release(4 + ci, "dv")

            free_group(BB)
            mark(f"bkdone_{half}", "dv", "sc", "pe", "syn", "sld", "outd",
                 "wl0", "wl1", "gld", "dbgd")

    if STAGE < 5:
        @block.scalar
        def _(sc):
            for c in range(NCH):
                DMA(sc, "outd", out_t.ap()[c], ti["resid"].ap()[c])

    @block.sync
    def _(s):
        wait_cur(s, "outd")
        if DEBUG:
            wait_cur(s, "dbgd")
        wait_cur(s, "cc")
        wait_cur(s, "gst")
        wait_cur(s, "sld")
        wait_cur(s, "gld")
    return nc


def kernel(L0_attn_resid, L0_attn_norm_w, L0_attn_W_skip, L0_attn_W_in,
           L0_attn_conv_w, L0_attn_conv_b, L0_attn_W_d1, L0_attn_W_d2,
           L0_attn_b_d2, L0_attn_W_Bm, L0_attn_W_Cm, L0_attn_A_log,
           L0_attn_W_D, L0_attn_W_out, L0_ffn_w1, L0_ffn_w2, L0_ffn_w3,
           L0_c, L0_wmod):
    resid = np.ascontiguousarray(np.asarray(L0_attn_resid, np.float32)[0])
    in_maps = []
    for i in range(NCORES):
        es = slice(ES * i, ES * (i + 1))
        m = {
            "resid": np.ascontiguousarray(resid.reshape(NCH, NCORES, BLK, D)[:, i]),
            "cvec": np.ascontiguousarray(np.asarray(L0_c, np.float32).reshape(16, 128).T),
            "wmod_sl": np.ascontiguousarray(np.asarray(L0_wmod, np.float32)[:, 1536 * i:1536 * (i + 1)]),
            "norm_w": np.asarray(L0_attn_norm_w, np.float32).reshape(1, D),
            "conv_w": np.ascontiguousarray(
                np.asarray(L0_attn_conv_w, np.float32)[es, 0, :]
                .reshape(4, 128, 4).transpose(1, 0, 2).reshape(128, 16)),
            "conv_b": np.ascontiguousarray(
                np.asarray(L0_attn_conv_b, np.float32)[es].reshape(4, 128).T),
            "A_log": np.ascontiguousarray(
                np.asarray(L0_attn_A_log, np.float32)[es]
                .reshape(4, 128, NS).transpose(1, 0, 2).reshape(128, 64)),
            "W_D": np.ascontiguousarray(
                np.asarray(L0_attn_W_D, np.float32)[es].reshape(4, 128).T),
            "b_d2": np.ascontiguousarray(
                np.asarray(L0_attn_b_d2, np.float32)[es].reshape(4, 128).T),
            "W_d1_sl": np.ascontiguousarray(np.asarray(L0_attn_W_d1, np.float32)[:, es]),
            "W_d2_sl": np.ascontiguousarray(np.asarray(L0_attn_W_d2, np.float32)[es, :]),
            "W_Bm_sl": np.ascontiguousarray(np.asarray(L0_attn_W_Bm, np.float32)[:, es]),
            "W_Cm_sl": np.ascontiguousarray(np.asarray(L0_attn_W_Cm, np.float32)[:, es]),
            "W_in_sl": np.ascontiguousarray(np.asarray(L0_attn_W_in, np.float32)[:, DSH * i:DSH * (i + 1)]),
            "W_skip_sl": np.ascontiguousarray(np.asarray(L0_attn_W_skip, np.float32)[:, DSH * i:DSH * (i + 1)]),
            "W_out_sl": np.ascontiguousarray(np.asarray(L0_attn_W_out, np.float32)[:, es]),
            "w1_sl": np.ascontiguousarray(np.asarray(L0_ffn_w1, np.float32)[:, DSH * i:DSH * (i + 1)]),
            "w2_sl": np.ascontiguousarray(np.asarray(L0_ffn_w2, np.float32)[:, DSH * i:DSH * (i + 1)]),
            "w3_sl": np.ascontiguousarray(np.asarray(L0_ffn_w3, np.float32)[:, FSH * i:FSH * (i + 1)]),
        }
        in_maps.append(m)
    nc = build_program()
    res = run_bass_kernel_spmd(nc, in_maps, core_ids=list(range(NCORES)))
    out = np.zeros((L, D), np.float32)
    ov = out.reshape(NCH, NCORES, BLK, D)
    for i in range(NCORES):
        ov[:, i] = res.results[i]["out"]
    if DEBUG:
        kernel.dbg = res.results
    return out.reshape(1, L, D)


# version 12
# speedup vs baseline: 1.2712x; 1.0214x over previous
import numpy as np
import concourse.bass as bass
import concourse.mybir as mybir
import bass_rust as _br
from concourse.bass_utils import run_bass_kernel_spmd
from concourse.aot_env import allow_ici_all_to_all

F32 = mybir.dt.float32
BF16 = mybir.dt.bfloat16
AF = mybir.ActivationFunctionType
AL = mybir.AluOpType
AX = mybir.AxisListType.X

NCORES = 8
D = 2048
E = 4096
ES = E // NCORES
R = 128
NS = 16
L = 8192
DFF = 8192
NCH = 8
BLK = 128
CH_T = NCORES * BLK
DSH = D // NCORES
FSH = DFF // NCORES
MAGIC = 12582912.0
RG = [list(range(NCORES))]

DEBUG = False
STAGE = 5


def build_program():
    nc = bass.Bass("TRN2", target_bir_lowering=False, debug=False)
    ti = {}

    def din(name, shape, dtype=F32):
        ti[name] = nc.dram_tensor(name, list(shape), dtype, kind="ExternalInput")
        return ti[name]

    din("resid", (NCH, BLK, D))
    din("cvec", (128, 16))
    din("wmod_sl", (D, 1536))
    din("norm_w", (1, D))
    din("conv_w", (128, 16))
    din("conv_b", (128, 4))
    din("A_log", (128, 64))
    din("W_D", (128, 4))
    din("b_d2", (128, 4))
    din("W_d1_sl", (R, ES))
    din("W_d2_sl", (ES, R))
    din("W_Bm_sl", (NS, ES))
    din("W_Cm_sl", (NS, ES))
    din("W_in_sl", (E, DSH))
    din("W_skip_sl", (E, DSH))
    din("W_out_sl", (D, ES))
    din("w1_sl", (DFF, DSH))
    din("w2_sl", (DFF, DSH))
    din("w3_sl", (D, FSH))
    out_t = nc.dram_tensor("out", [NCH, BLK, D], F32, kind="ExternalOutput")
    dbg = {}
    if DEBUG:
        if STAGE >= 1:
            dbg["xsilu"] = nc.dram_tensor("d_xsilu", [4, 128, L], F32, kind="ExternalOutput")
        if STAGE >= 2:
            dbg["dt"] = nc.dram_tensor("d_dt", [4, 128, L], F32, kind="ExternalOutput")
        if STAGE >= 3:
            dbg["y"] = nc.dram_tensor("d_y", [4, 128, L], F32, kind="ExternalOutput")
        if STAGE >= 4:
            dbg["x2"] = nc.dram_tensor("d_x2", [NCH, BLK, D], F32, kind="ExternalOutput")

    dram = {}
    shared_out = {"ago_wi", "ago_ws", "ago_wo", "ago_w1", "ago_w2", "ago_w3",
                  "ago_mod", "aro_ws", "aro_p"}
    for nm, shape, dt_ in (
        ("agi_wi", [DSH, E], BF16), ("ago_wi", [D, E], BF16),
        ("agi_ws", [DSH, E], BF16), ("ago_ws", [D, E], BF16),
        ("agi_wo", [ES, D], BF16), ("ago_wo", [E, D], BF16),
        ("agi_w1", [DSH, DFF], BF16), ("ago_w1", [D, DFF], BF16),
        ("agi_w2", [DSH, DFF], BF16), ("ago_w2", [D, DFF], BF16),
        ("agi_w3", [FSH, D], BF16), ("ago_w3", [DFF, D], BF16),
        ("agi_mod", [1, 1536], F32), ("ago_mod", [1, 12288], F32),
        ("ari_ws", [128, 4], F32), ("aro_ws", [128, 4], F32),
        ("modv_d", [6, D], F32),
        ("a2a_fi", [NCH, NCORES, ES, BLK], BF16),
        ("a2a_fo", [NCH, NCORES, ES, BLK], BF16),
        ("ari_p", [NCH, 160, CH_T], F32), ("aro_p", [NCH, 160, CH_T], F32),
        ("bct_d", [NCH, 32, CH_T], BF16),
        ("a2a_bi", [NCH, NCORES, ES, BLK], BF16),
        ("a2a_bo", [NCH, NCORES, ES, BLK], BF16),
        ("h_sp", [NCH, BLK, D], BF16),
        ("skip_d", [32, 128, CH_T], BF16),
    ):
        dram[nm] = nc.dram_tensor(
            nm, shape, dt_,
            addr_space="Shared" if nm in shared_out else "Local")
    agi_wi, ago_wi = dram["agi_wi"], dram["ago_wi"]
    agi_ws, ago_ws = dram["agi_ws"], dram["ago_ws"]
    agi_wo, ago_wo = dram["agi_wo"], dram["ago_wo"]
    agi_w1, ago_w1 = dram["agi_w1"], dram["ago_w1"]
    agi_w2, ago_w2 = dram["agi_w2"], dram["ago_w2"]
    agi_w3, ago_w3 = dram["agi_w3"], dram["ago_w3"]
    agi_mod, ago_mod = dram["agi_mod"], dram["ago_mod"]
    ari_ws, aro_ws = dram["ari_ws"], dram["aro_ws"]
    modv_d = dram["modv_d"]
    a2a_fi, a2a_fo = dram["a2a_fi"], dram["a2a_fo"]
    ari_p, aro_p = dram["ari_p"], dram["aro_p"]
    bct_d = dram["bct_d"]
    a2a_bi, a2a_bo = dram["a2a_bi"], dram["a2a_bo"]
    h_sp, skip_d = dram["h_sp"], dram["skip_d"]

    groups = {}

    def alloc(group, name, shape, dtype):
        cm = nc.sbuf_tensor(name, list(shape), dtype)
        h = cm.__enter__()
        groups.setdefault(group, []).append(cm)
        return h

    def free_group(group):
        for cm in reversed(groups.pop(group, [])):
            cm.__exit__(None, None, None)

    psum_cms = []

    def psum(name, shape, dtype=F32):
        cm = nc.psum_tensor(name, list(shape), dtype)
        psum_cms.append(cm)
        return cm.__enter__()

    sems = {}
    cnt = {}
    sem_cms = []

    def sem(name):
        cm = nc.semaphore(name)
        sem_cms.append(cm)
        sems[name] = cm.__enter__()
        cnt[name] = 0

    def inc(instr, name, k=1):
        instr.then_inc(sems[name], k)
        cnt[name] += k
        return instr

    marks = {}

    DMA_SEMS = ("gld", "sld", "syn", "gst", "outd", "dbgd", "sst")

    def mark(name, *semnames):
        marks[name] = {s: cnt[s] for s in semnames}
        for s in semnames:
            if s in DMA_SEMS:
                dma_barrier[s] = cnt[s]

    def waitm(eng, *names):
        for name in names:
            for s, n in marks[name].items():
                if n > 0:
                    eng.wait_ge(sems[s], n)

    def wait_cur(eng, semname):
        if cnt[semname] > 0:
            eng.wait_ge(sems[semname], cnt[semname])

    dma_barrier = {}
    dma_applied = {}

    def _dma_guard(eng, semname):
        b = dma_barrier.get(semname, 0)
        if b > dma_applied.get(semname, 0):
            eng.wait_ge(sems[semname], b)
            dma_applied[semname] = b

    def DMA(eng, semname, out, in_):
        _dma_guard(eng, semname)
        return inc(eng.dma_start(out=out, in_=in_), semname, 16)

    def DMAT(eng, semname, out, in_):
        _dma_guard(eng, semname)
        return inc(eng.dma_start_transpose(out, in_), semname, 16)

    for s in ("gld", "sld", "syn", "cc", "pe", "dv", "sc", "gst", "gp", "outd",
              "dbgd", "sst"):
        sem(s)

    bank_free = [None] * 8

    def pe_take_bank(t, i):
        if bank_free[i]:
            s, n = bank_free[i]
            if n > 0:
                t.wait_ge(sems[s], n)

    def bank_release(i, semname):
        bank_free[i] = (semname, cnt[semname])

    P = "persist"
    cw_s = alloc(P, "cw_s", [128, 16], F32)
    cb_s = alloc(P, "cb_s", [128, 4], F32)
    wd_s = alloc(P, "wd_s", [128, 4], F32)
    bd2_s = alloc(P, "bd2_s", [128, 4], F32)
    a_neg = alloc(P, "a_neg", [128, 64], F32)
    wsc = alloc(P, "wsc", [128, 8], F32)
    wd1T = alloc(P, "wd1T", [128, 4, 128], BF16)
    wd2T = alloc(P, "wd2T", [128, 4, 128], BF16)
    wbmT = alloc(P, "wbmT", [128, 4, NS], BF16)
    wcmT = alloc(P, "wcmT", [128, 4, NS], BF16)
    cst = alloc(P, "cst", [128, 8], F32)
    carry_h = alloc(P, "carry_h", [128, 64], F32)
    carry_x = alloc(P, "carry_x", [128, 4, 4], BF16)
    tiny = alloc(P, "tiny", [128, 8], F32)

    SK = "skgrp"
    sk_st = alloc(SK, "sk_st", [128, 2, 512], BF16)
    FL = "frontlong"
    wq_f = alloc(FL, "wq_f", [128, 16, 512], BF16)
    xnT = alloc(FL, "xnT", [128, 16, NCH * BLK], BF16)
    EA = "earlyA"
    modf = alloc(EA, "modf", [128, 2, D], BF16)
    normw_bc = alloc(EA, "normw_bc", [128, D], BF16)
    xt_f = alloc(EA, "xt_f", [128, 2, D], F32)
    hb_f = alloc(EA, "hb_f", [128, 2, D], BF16)
    xn_f = alloc(EA, "xn_f", [128, 2, D], BF16)
    st_f = alloc(EA, "st_f", [128, 64], F32)
    scr_sc = alloc(EA, "scr_sc", [128, 1024], BF16)
    stgf = alloc(EA, "stgf", [128, 2, 4, 2, 512], BF16)

    pb = [psum(f"pb{i}", [128, 512], F32) for i in range(8)]

    blk_cm = nc.Block()
    block = blk_cm.__enter__()

    @block.vector
    def _(v):
        v.memset(cst[:, 0:1], 1e-5)
        v.memset(cst[:, 1:2], 1e-6)
        v.memset(cst[:, 2:3], MAGIC)
        v.memset(cst[:, 3:4], 1.0 / D)
        v.memset(cst[:, 4:5], 1.0 / DFF)
        v.memset(cst[:, 5:6], -1.0)
        v.memset(cst[:, 6:7], 1.0)
        v.memset(carry_h[:, :], 0.0)
        a = v.memset(tiny[:, :], 0.0)
        inc(a, "dv", 1)
    mark("cst", "dv")

    PR = "prep"
    pw = alloc(PR, "pw", [128, 64, DSH], BF16)
    pt_s = alloc(PR, "pt_s", [128, 2, 8192], BF16)
    wm_s = alloc(PR, "wm_s", [128, 16, 128], F32)
    psc_f = alloc(PR, "psc_f", [1, 1536], F32)
    silc = alloc(PR, "silc", [128, 16], F32)
    pabs = alloc(PR, "pabs", [128, 8], F32)
    ones_f = alloc(PR, "ones_f", [128, 128], F32)
    pqs = alloc(PR, "pqs", [128, 1024], F32)

    @block.gpsimd
    def _(g):
        DMA(g, "gld", pw[:, 0:32, :],
            ti["W_in_sl"].ap().rearrange("(c p) m -> p c m", p=128))
        DMA(g, "gld", pw[:, 32:64, :],
            ti["W_skip_sl"].ap().rearrange("(c p) m -> p c m", p=128))
    mark("pw_fr", "gld")

    @block.sync
    def _(s):
        waitm(s, "pw_fr")
        for c in range(64):
            half, cc = divmod(c, 32)
            tgt = pt_s[:, :, half * 4096 + cc * 128: half * 4096 + (cc + 1) * 128]
            DMAT(s, "syn", tgt, pw[:, c, :])
        s.wait_ge(sems["syn"], cnt["syn"])
        DMA(s, "syn", agi_wi.ap().rearrange("(q p) e -> p q e", p=128),
            pt_s[:, :, 0:4096])
        DMA(s, "syn", agi_ws.ap().rearrange("(q p) e -> p q e", p=128),
            pt_s[:, :, 4096:8192])
    mark("agi_fr", "syn")

    @block.gpsimd
    def _(g):
        waitm(g, "agi_fr")
        c0 = g.collective_compute("AllGather", AL.bypass, replica_groups=RG,
                                  ins=[agi_wi.ap().opt()], outs=[ago_wi.ap().opt()])
        inc(c0, "cc", 1)
        c1 = g.collective_compute("AllGather", AL.bypass, replica_groups=RG,
                                  ins=[agi_ws.ap().opt()], outs=[ago_ws.ap().opt()])
        inc(c1, "cc", 1)
    mark("ag_fr", "cc")

    @block.gpsimd
    def _(g):
        DMA(g, "gld", silc[:, :], ti["cvec"].ap())
    mark("cvld", "gld")

    @block.scalar
    def _(sc):
        waitm(sc, "cvld", "cst")
        a = sc.activation(silc[:, :], silc[:, :], AF.Silu)
        inc(a, "sc", 1)
    mark("silc", "sc")

    for j in range(12):
        @block.gpsimd
        def _(g, j=j):
            if j > 0:
                g.wait_ge(sems["pe"], marks[f"modmm{j - 1}"]["pe"])
            DMA(g, "gld", wm_s[:, :, :],
                ti["wmod_sl"].ap()
                .rearrange("(k p) m -> p k m", p=128)[:, :, j * 128:(j + 1) * 128])
        mark(f"wmld{j}", "gld")

        @block.tensor
        def _(t, j=j):
            waitm(t, "silc", f"wmld{j}")
            if j >= 3:
                t.wait_ge(sems["dv"], marks[f"modcp{j - 3}"]["dv"])
            for k in range(16):
                last = t.matmul(pb[j % 3][0:1, 0:128], silc[:, k:k + 1], wm_s[:, k, :],
                                start=(k == 0), stop=(k == 15))
            inc(last, "pe", 1)
        mark(f"modmm{j}", "pe")

        @block.vector
        def _(v, j=j):
            waitm(v, f"modmm{j}")
            last = v.tensor_copy(psc_f[0:1, j * 128:(j + 1) * 128], pb[j % 3][0:1, 0:128])
            inc(last, "dv", 1)
        mark(f"modcp{j}", "dv")
    mark("modcp", "dv")
    modcp_dv = cnt["dv"]
    for _b in range(3):
        bank_release(_b, "dv")

    @block.gpsimd
    def _(g):
        waitm(g, "modcp")
        DMA(g, "gld", agi_mod.ap(), psc_f[0:1, 0:1536])
        wait_cur(g, "gld")
        c1 = g.collective_compute("AllGather", AL.bypass, replica_groups=RG,
                                  ins=[agi_mod.ap().opt()], outs=[ago_mod.ap().opt()])
        inc(c1, "cc", 1)
    mark("modex", "cc")

    @block.gpsimd
    def _(g):
        waitm(g, "modex")
        m6 = ago_mod.ap().rearrange("one (s d) -> one s d", s=6)[0]
        DMA(g, "gld", modf[:, :, :], m6[0:2][None, :, :].broadcast_to((128, 2, D)))
        DMA(g, "gld", normw_bc[:, :], ti["norm_w"].ap().broadcast_to((128, D)))
    mark("modbc", "gld")

    @block.gpsimd
    def _(g):
        waitm(g, "agi_fr")
        DMA(g, "gld", cw_s[:, :], ti["conv_w"].ap())
        DMA(g, "gld", cb_s[:, :], ti["conv_b"].ap())
        DMA(g, "gld", wd_s[:, :], ti["W_D"].ap())
        DMA(g, "gld", bd2_s[:, :], ti["b_d2"].ap())
        DMA(g, "gld", a_neg[:, :], ti["A_log"].ap())
        DMA(g, "gld", pw[:, 0:4, 0:128],
            ti["W_d1_sl"].ap().rearrange("r (gg pp) -> r gg pp", pp=128))
        DMA(g, "gld", pw[:, 4:8, 0:128],
            ti["W_d2_sl"].ap().rearrange("(gg p) r -> p gg r", p=128))
        DMA(g, "gld", pw[0:NS, 8:12, 0:128],
            ti["W_Bm_sl"].ap().rearrange("n (gg p) -> n gg p", p=128))
        DMA(g, "gld", pw[0:NS, 12:16, 0:128],
            ti["W_Cm_sl"].ap().rearrange("n (gg p) -> n gg p", p=128))
    mark("smallw", "gld")

    @block.sync
    def _(s):
        waitm(s, "smallw")
        for gg in range(4):
            DMAT(s, "syn", wd1T[:, gg:gg + 1, :], pw[:, gg, 0:128])
            DMAT(s, "syn", wd2T[:, gg:gg + 1, :], pw[:, 4 + gg, 0:128])
            DMAT(s, "syn", wbmT[:, gg:gg + 1, :], pw[0:NS, 8 + gg, 0:128])
            DMAT(s, "syn", wcmT[:, gg:gg + 1, :], pw[0:NS, 12 + gg, 0:128])
    mark("smallT", "syn")

    @block.scalar
    def _(sc):
        waitm(sc, "smallw")
        a = sc.activation(a_neg[:, :], a_neg[:, :], AF.Exp)
        inc(a, "sc", 1)
    mark("aexp", "sc")

    @block.vector
    def _(v):
        waitm(v, "aexp")
        a = v.tensor_scalar(a_neg[:, :], a_neg[:, :], -1.0, None, AL.mult)
        inc(a, "dv", 1)
    mark("aneg", "dv")

    def prep_plain(src_name, agi, nrows_sb, width, label):
        nch = {"wo": 16}[label]

    @block.gpsimd
    def _(g):
        waitm(g, "smallT")
        pwv = pw.ap().rearrange("p c m -> p (c m)")[:, 0:16 * 512] \
            .rearrange("p (c m) -> p c m", m=512)
        DMA(g, "gld", pwv,
            ti["W_out_sl"].ap().rearrange("(c p) m -> p c m", p=128))
    mark("pwo", "gld")

    @block.sync
    def _(s):
        waitm(s, "pwo")
        pt2 = pt_s.ap().rearrange("p q m -> p (q m)").rearrange("p (f j) -> p f j", f=4)
        pwv = pw.ap().rearrange("p c m -> p (c m)")[:, 0:16 * 512] \
            .rearrange("p (c m) -> p c m", m=512)
        for c in range(16):
            DMAT(s, "syn", pt2[:, :, c * 128:(c + 1) * 128], pwv[:, c, :])
        s.wait_ge(sems["syn"], cnt["syn"])
        DMA(s, "syn", agi_wo.ap().rearrange("(f p) d2 -> p f d2", p=128),
            pt2[:, :, 0:2048])
    mark("agi_wo", "syn")

    @block.gpsimd
    def _(g):
        waitm(g, "agi_wo")
        c1 = g.collective_compute("AllGather", AL.bypass, replica_groups=RG,
                                  ins=[agi_wo.ap().opt()], outs=[ago_wo.ap().opt()])
        inc(c1, "cc", 1)
    mark("ag_wo", "cc")


    mark("prep_done", "syn", "gld", "dv", "sc", "gst", "sld", "cc")


    for c in range(NCH):
        po = 32 * (c % 2)

        @block.scalar
        def _(sc, c=c):
            if c >= 2:
                sc.wait_ge(sems["dv"], marks[f"xn{c - 2}"]["dv"])
            DMA(sc, "sld", xt_f[:, c % 2, :], ti["resid"].ap()[c, :, :])
        mark(f"res{c}", "sld")

        @block.vector
        def _(v, c=c, po=po):
            waitm(v, f"res{c}", "modbc")
            if c >= 2:
                waitm(v, f"hbu{c - 2}")
            x = xt_f[:, c % 2, :]
            for s4 in range(4):
                a = v.bn_stats(st_f[:, po + 6 * s4: po + 6 * (s4 + 1)],
                               x[:, 512 * s4: 512 * (s4 + 1)])
            inc(a, "dv", 1)
            wait_cur(v, "dv")
            a = v.bn_aggr(st_f[:, po + 24: po + 26], st_f[:, po: po + 24])
            inc(a, "dv", 1)
        mark(f"st{c}", "dv")

        @block.scalar
        def _(sc, c=c, po=po):
            waitm(sc, f"st{c}")
            a = sc.activation(st_f[:, po + 26: po + 27], st_f[:, po + 25: po + 26],
                              AF.Sqrt, bias=cst[:, 0:1])
            inc(a, "sc", 1)
        mark(f"sqv{c}", "sc")

        @block.vector
        def _(v, c=c, po=po):
            waitm(v, f"sqv{c}")
            a = v.reciprocal(st_f[:, po + 27: po + 28], st_f[:, po + 26: po + 27])
            inc(a, "dv", 1)
            wait_cur(v, "dv")
            x = xt_f[:, c % 2, :]
            hb = hb_f[:, c % 2, :]
            tmp = xn_f[:, c % 2, :]
            if c >= 2:
                v.wait_ge(sems["syn"], marks[f"xnT{c - 2}"]["syn"])
            a = v.tensor_scalar(hb, x, st_f[:, po + 24: po + 25],
                                st_f[:, po + 27: po + 28], AL.subtract, AL.mult)
            inc(a, "dv", 1)
            wait_cur(v, "dv")
            a = v.tensor_tensor(tmp, hb, modf[:, 1, :], AL.mult)
            inc(a, "dv", 1)
            wait_cur(v, "dv")
            a = v.tensor_tensor(hb, hb, tmp, AL.add)
            inc(a, "dv", 1)
            wait_cur(v, "dv")
            a = v.tensor_tensor(hb, hb, modf[:, 0, :], AL.add)
            inc(a, "dv", 1)
        mark(f"hb{c}", "dv")

        @block.scalar
        def _(sc, c=c, po=po):
            waitm(sc, f"hb{c}")
            DMA(sc, "sld", h_sp.ap()[c, :, :], hb_f[:, c % 2, :])
            a = sc.activation(scr_sc[:, :], hb_f[:, c % 2, 0:1024], AF.Square,
                              accum_out=st_f[:, po + 28: po + 29])
            inc(a, "sc", 1)
            wait_cur(sc, "sc")
            a = sc.activation(scr_sc[:, :], hb_f[:, c % 2, 1024:2048], AF.Square,
                              accum_out=st_f[:, po + 31: po + 32])
            inc(a, "sc", 1)
        mark(f"hbsq{c}", "sc", "sld")

        @block.vector
        def _(v, c=c, po=po):
            waitm(v, f"hbsq{c}")
            a = v.tensor_tensor(st_f[:, po + 28: po + 29],
                                st_f[:, po + 28: po + 29],
                                st_f[:, po + 31: po + 32], AL.add)
            inc(a, "dv", 1)
        mark(f"hbss{c}", "dv")

        @block.scalar
        def _(sc, c=c, po=po):
            sc.wait_ge(sems["dv"], marks[f"hbss{c}"]["dv"])
            a = sc.activation(st_f[:, po + 29: po + 30], st_f[:, po + 28: po + 29],
                              AF.Sqrt, bias=cst[:, 0:1], scale=cst[:, 3:4])
            inc(a, "sc", 1)
        mark(f"hbu{c}", "sc", "sld")

        @block.vector
        def _(v, c=c, po=po):
            waitm(v, f"hbu{c}")
            a = v.reciprocal(st_f[:, po + 30: po + 31], st_f[:, po + 29: po + 30])
            inc(a, "dv", 1)
            wait_cur(v, "dv")
            xn = xn_f[:, c % 2, :]
            a = v.tensor_scalar(xn, hb_f[:, c % 2, :], st_f[:, po + 30: po + 31],
                                None, AL.mult)
            inc(a, "dv", 1)
            wait_cur(v, "dv")
            a = v.tensor_tensor(xn, xn, normw_bc[:, :], AL.mult)
            inc(a, "dv", 1)
        mark(f"xn{c}", "dv")

        @block.sync
        def _(s, c=c):
            waitm(s, f"xn{c}")
            DMAT(s, "syn", xnT[:, :, c * BLK:(c + 1) * BLK], xn_f[:, c % 2, :])
        mark(f"xnT{c}", "syn")

    if STAGE >= 5:
        for i, wn in enumerate(("w1_sl", "w2_sl", "w3_sl")):
            @block.gpsimd
            def _(g, i=i, wn=wn):
                if i == 0:
                    waitm(g, "agi_wo")
                else:
                    waitm(g, f"wabs{i - 1}")
                DMA(g, "gld", pw[:, 0:64, :],
                    ti[wn].ap().rearrange("(c p) m -> p c m", p=128))
            mark(f"pq{i + 1}", "gld")

            @block.vector
            def _(v, i=i):
                waitm(v, f"pq{i + 1}")
                a = v.reduce_sum(pabs[:, i:i + 1],
                                 pw.ap().rearrange("p c m -> p (c m)"),
                                 axis=AX, apply_absolute_value=True)
                inc(a, "dv", 1)
                if i == 2:
                    wait_cur(v, "dv")
                    a = v.memset(pabs[:, 3:4], 0.0)
                    inc(a, "dv", 1)
            mark(f"wabs{i}", "dv")

        @block.gpsimd
        def _(g):
            waitm(g, "wabs2")
            DMA(g, "gld", ari_ws.ap(), pabs[:, 0:4])
            wait_cur(g, "gld")
            c1 = g.collective_compute("AllReduce", AL.add, replica_groups=RG,
                                      ins=[ari_ws.ap().opt()],
                                      outs=[aro_ws.ap().opt()])
            inc(c1, "cc", 1)
            wait_cur(g, "cc")
            g.wait_ge(sems["pe"], marks["modmm11"]["pe"])
            _dma_guard(g, "gld")
            inc(g.dma_start(out=ones_f[0:4, 0:128],
                            in_=aro_ws.ap().rearrange("p w -> w p"),
                            allow_slow_non_contiguous=True), "gld", 16)
        mark("wsT", "gld")

        @block.vector
        def _(v):
            waitm(v, "wsT")
            a = v.reduce_sum(pabs[0:4, 4:5], ones_f[0:4, 0:128], axis=AX)
            inc(a, "dv", 1)
            wait_cur(v, "dv")
            a = v.tensor_scalar(pabs[0:4, 5:6], pabs[0:4, 4:5],
                                1.0 / (DFF * D), 1e-5, AL.mult, AL.max)
            inc(a, "dv", 1)
            wait_cur(v, "dv")
            a = v.reciprocal(pabs[0:4, 6:7], pabs[0:4, 5:6])
            inc(a, "dv", 1)
        mark("ws4", "dv")

        @block.gpsimd
        def _(g):
            waitm(g, "ws4")
            wv = ari_ws.ap()[0:2, :].rearrange("a b -> (a b)") \
                .rearrange("(w c) -> w c", c=2)
            DMA(g, "gld", wv, pabs[0:4, 5:7])
            wait_cur(g, "gld")
            _dma_guard(g, "gld")
            inc(g.dma_start(out=wsc[:, 0:3],
                            in_=wv[0:3, 0][None, :].broadcast_to((128, 3)),
                            allow_slow_non_contiguous=True), "gld", 16)
            inc(g.dma_start(out=wsc[:, 4:7],
                            in_=wv[0:3, 1][None, :].broadcast_to((128, 3)),
                            allow_slow_non_contiguous=True), "gld", 16)
            wait_cur(g, "gld")
        mark("wscal", "gld")
        mark("pq3", "gld")

    def front_pass(weight_dram, to_skip):
        tag = "ws" if to_skip else "wi"
        for q in range(8):
            @block.scalar
            def _(sc, q=q):
                prevq = f"fmmq_{tag}_{q - 1}" if q > 0 else ("fmmq_wi_7" if to_skip else None)
                if prevq:
                    sc.wait_ge(sems["pe"], marks[prevq]["pe"])
                DMA(sc, "sld", wq_f[:, :, :],
                    weight_dram.ap().rearrange("(k p) e -> p k e", p=128)[:, :, q * 512:(q + 1) * 512])
            mark(f"fwq_{tag}_{q}", "sld")

            for cg in range(2):
                for ec in range(4):
                    bank = ec % 2 + (2 if to_skip else 0)

                    @block.tensor
                    def _(t, q=q, cg=cg, ec=ec, bank=bank):
                        waitm(t, f"fwq_{tag}_{q}")
                        if q == 0 and cg == 0 and ec == 0:
                            waitm(t, "ag_fr", f"xnT{NCH - 1}")
                            t.wait_ge(sems["dv"], modcp_dv)
                        pe_take_bank(t, bank)
                        for k in range(16):
                            last = t.matmul(pb[bank][:, :],
                                            wq_f[:, k, ec * 128:(ec + 1) * 128],
                                            xnT[:, k, cg * 512:(cg + 1) * 512],
                                            start=(k == 0), stop=(k == 15))
                        inc(last, "pe", 1)
                    mark(f"fmm_{tag}_{q}_{cg}_{ec}", "pe")

                    @block.scalar
                    def _(sc, q=q, cg=cg, ec=ec, bank=bank):
                        waitm(sc, f"fmm_{tag}_{q}_{cg}_{ec}")
                        if to_skip:
                            par = ec % 2
                            pm = marks.get(f"sksd_{tag}_{q}_{cg}_{ec - 2}") or \
                                marks.get(f"sksd_{tag}_{q}_{cg - 1}_{2 + ec}") or \
                                marks.get(f"sksd_{tag}_{q - 1}_1_{2 + ec}")
                            if pm:
                                sc.wait_ge(sems["sld"], pm["sld"])
                            a = sc.activation(sk_st[:, par, :], pb[bank][:, :], AF.Copy)
                            inc(a, "sc", 1)
                            bank_release(bank, "sc")
                            wait_cur(sc, "sc")
                            DMA(sc, "sld",
                                skip_d.ap()[q * 4 + ec, :, cg * 512:(cg + 1) * 512],
                                sk_st[:, par, :])
                            mark(f"sksd_{tag}_{q}_{cg}_{ec}", "sld")
                        else:
                            pm = marks.get(f"a2ast_{q - 2}_{cg}")
                            if pm:
                                sc.wait_ge(sems["sst"], pm["sst"])
                            a = sc.activation(stgf[:, q % 2, ec, cg, :],
                                              pb[bank][:, :], AF.Copy)
                            inc(a, "sc", 1)
                            bank_release(bank, "sc")
                mark(f"fevg_{tag}_{q}_{cg}", "sc", "sld")
            mark(f"fmmq_{tag}_{q}", "pe")

            if not to_skip:
                for cg in range(2):
                    @block.scalar
                    def _(sc, q=q, cg=cg):
                        sc.wait_ge(sems["sc"], marks[f"fevg_wi_{q}_{cg}"]["sc"])
                        for cb in range(4):
                            c = cg * 4 + cb
                            DMA(sc, "sst",
                                a2a_fi.ap()[c].rearrange(
                                    "j (eo p) t -> p j eo t", p=128)[:, q],
                                stgf[:, q % 2, :, cg, cb * 128:(cb + 1) * 128])
                    mark(f"a2ast_{q}_{cg}", "sst")

    front_pass(ago_wi, to_skip=False)

    for c in range(NCH):
        @block.gpsimd
        def _(g, c=c):
            cg = c // 4
            g.wait_ge(sems["sst"], marks[f"a2ast_7_{cg}"]["sst"])
            with allow_ici_all_to_all():
                c1 = g.collective_compute("AllToAll", AL.bypass, replica_groups=RG,
                                          ins=[a2a_fi.ap()[c].opt()],
                                          outs=[a2a_fo.ap()[c].opt()])
            inc(c1, "cc", 1)
        mark(f"a2af_{c}", "cc")

    mark("front_free", "gst", "syn", "dv", "sc", "sld")
    front_pass(ago_ws, to_skip=True)
    mark("front_done", "pe", "sc", "sld")

    if STAGE >= 5:
        def prep_ffn(w_name, agi, ago, scol, nch, width, mark_pw, tag,
                     load=True, prev_tag=None):
            pwv = pw.ap().rearrange("p c m -> p (c m)")[:, 0:nch * width] \
                .rearrange("p (c m) -> p c m", m=width)

            if load:
                @block.gpsimd
                def _(g):
                    waitm(g, mark_pw)
                    if prev_tag:
                        waitm(g, f"agi_{prev_tag}")
                    DMA(g, "gld", pwv,
                        ti[w_name].ap().rearrange("(c p) m -> p c m", p=128))
                mark(f"ld_{tag}", "gld")

            flat = pw.ap().rearrange("p c m -> p (c m)")
            wm_sf = wm_s.ap().rearrange("p k m -> p (k m)")
            for piece in range(16):
                sl = slice(piece * 1024, (piece + 1) * 1024)
                buf = pqs[:, :] if piece % 2 == 0 else wm_sf[:, 0:1024]

                @block.scalar
                def _(sc, sl=sl, piece=piece, buf=buf):
                    if piece == 0:
                        if load:
                            waitm(sc, f"ld_{tag}", "wscal")
                        else:
                            waitm(sc, "wscal")
                    if piece == 1 and tag == "w3":
                        sc.wait_ge(sems["pe"], marks["modmm11"]["pe"])
                    if piece >= 2:
                        sc.wait_ge(sems["dv"],
                                   marks[f"qc_{tag}_{piece - 2}"]["dv"])
                    a = sc.activation(buf, flat[:, sl], AF.Relu,
                                      bias=cst[:, 2:3], scale=wsc[:, 4 + scol:5 + scol])
                    inc(a, "sc", 1)
                    wait_cur(sc, "sc")
                    a = sc.activation(buf, buf, AF.Copy, bias=-MAGIC)
                    inc(a, "sc", 1)
                mark(f"rq_{tag}_{piece}", "sc")

                @block.vector
                def _(v, sl=sl, piece=piece, buf=buf):
                    waitm(v, f"rq_{tag}_{piece}")
                    inc(v.tensor_scalar(flat[:, sl], buf, -1.0, 1.0,
                                        AL.max, AL.min), "dv", 1)
                mark(f"qc_{tag}_{piece}", "dv")
            mark(f"q_{tag}", "dv")

            nf = width // 128
            pt2 = pt_s.ap().rearrange("p q m -> p (q m)")[:, 0:nf * nch * 128] \
                .rearrange("p (f j) -> p f j", f=nf)

            @block.sync
            def _(s):
                waitm(s, f"q_{tag}")
                for c in range(nch):
                    DMAT(s, "syn", pt2[:, :, c * 128:(c + 1) * 128], pwv[:, c, :])
                s.wait_ge(sems["syn"], cnt["syn"])
                DMA(s, "syn", agi.ap().rearrange("(f p) x -> p f x", p=128), pt2)
            mark(f"agi_{tag}", "syn")


        prep_ffn("w3_sl", agi_w3, ago_w3, 2, 16, FSH, "wsum", "w3", load=False)
        prep_ffn("w1_sl", agi_w1, ago_w1, 0, 64, DSH, "q_w3", "w1",
                 prev_tag="w3")
        prep_ffn("w2_sl", agi_w2, ago_w2, 1, 64, DSH, "q_w1", "w2",
                 prev_tag="w1")
    mark("front_free", "gst", "syn", "dv", "sc", "sld", "gp", "gld")
    mark("front_done", "pe", "sc", "sld")

    free_group(PR)
    free_group(EA)
    SCg = "scan"
    xcat = alloc(SCg, "xcat", [128, 4, 4 + CH_T], BF16)
    xsilu = alloc(SCg, "xsilu", [128, 2, 4, CH_T], BF16)
    dt_t = alloc(SCg, "dt_t", [128, 4, CH_T], BF16)
    dt1f = alloc(SCg, "dt1f", [128, CH_T], BF16)
    bbc = alloc(SCg, "bbc", [128, NS, 512], BF16)
    cbc = alloc(SCg, "cbc", [128, NS, 512], BF16)
    dub = alloc(SCg, "dub", [128, 2, 512], BF16)
    abuf = alloc(SCg, "abuf", [128, NS, 514], BF16)
    bbuf = alloc(SCg, "bbuf", [128, NS, 514], BF16)
    hprod = alloc(SCg, "hprod", [128, NS, 514], BF16)
    y_t = alloc(SCg, "y_t", [128, 4, CH_T], BF16)
    yo_t = alloc(SCg, "yo_t", [128, 4, CH_T], BF16)
    sp0 = alloc(SCg, "sp0", [128, 4, 512], BF16)
    sp1 = alloc(SCg, "sp1", [128, 4, 512], BF16)
    prj_s = alloc(SCg, "prj_s", [128, CH_T], F32)
    prj_bc = alloc(SCg, "prj_bc", [48, CH_T], F32)

    @block.vector
    def _(v):
        waitm(v, "front_free")
        v.memset(abuf[:, :, 512:513], 0.0)
        v.memset(abuf[:, :, 513:514], 1.0)
        a = v.memset(bbuf[:, :, 512:514], 0.0)
        inc(a, "dv", 1)
    mark("abz", "dv")

    def scan_front(c):
        xpar = c % 2
        xs = xsilu[:, xpar]

        @block.scalar
        def _(sc, c=c):
            waitm(sc, f"a2af_{c}")
            if c == 0:
                waitm(sc, "front_free")
            if c >= 1:
                sc.wait_ge(sems["dv"], marks[f"conv{c - 1}"]["dv"])
            for gg in range(4):
                DMA(sc, "sld",
                    xcat[:, gg, 4:4 + CH_T].rearrange("p (j t) -> p j t", t=BLK),
                    a2a_fo.ap()[c].rearrange("j (g p) t -> p g j t", p=128)[:, gg])
        mark(f"xm{c}", "sld")

        @block.vector
        def _(v, c=c, xs=xs):
            waitm(v, f"xm{c}", "aneg")
            if c == 0:
                waitm(v, "front_free")
                a = v.memset(xcat[:, :, 1:4], 0.0)
            else:
                a = v.tensor_copy(xcat[:, :, 1:4], carry_x[:, :, 0:3])
            inc(a, "dv", 1)
            if c >= 2:
                v.wait_ge(sems["dv"], marks[f"ytot{c - 2}"]["dv"])
            wait_cur(v, "dv")
            for gg in range(4):
                acc = xs[:, gg, :]
                a = v.tensor_scalar(acc, xcat[:, gg, 1:1 + CH_T],
                                    cw_s[:, 4 * gg:4 * gg + 1], None, AL.mult)
                inc(a, "dv", 1)
                wait_cur(v, "dv")
                for k in (1, 2, 3):
                    a = v.scalar_tensor_tensor(acc, xcat[:, gg, 1 + k:1 + k + CH_T],
                                               cw_s[:, 4 * gg + k:4 * gg + k + 1],
                                               acc, AL.mult, AL.add)
                    inc(a, "dv", 1)
                    wait_cur(v, "dv")
            a = v.tensor_copy(carry_x[:, :, 0:3], xcat[:, :, 1 + CH_T:4 + CH_T])
            inc(a, "dv", 1)
        mark(f"conv{c}", "dv")

        @block.scalar
        def _(sc, c=c, xs=xs):
            waitm(sc, f"conv{c}")
            for gg in range(4):
                a = sc.activation(xs[:, gg, :], xs[:, gg, :], AF.Silu,
                                  bias=cb_s[:, gg:gg + 1])
            inc(a, "sc", 1)
        mark(f"xsl{c}", "sc")

        for half in range(2):
            @block.tensor
            def _(t, c=c, half=half, xs=xs):
                t.wait_ge(sems["sc"], marks[f"xsl{c}"]["sc"])
                if c == 0 and half == 0:
                    waitm(t, "smallT")
                for b in (4, 5, 6):
                    pe_take_bank(t, b)
                sl = slice(half * 512, (half + 1) * 512)
                for gg in range(4):
                    t.matmul(pb[4][:, :], wd1T[:, gg, :], xs[:, gg, sl],
                             start=(gg == 0), stop=(gg == 3))
                for gg in range(4):
                    t.matmul(pb[5][0:NS, :], wbmT[:, gg, :], xs[:, gg, sl],
                             start=(gg == 0), stop=(gg == 3))
                for gg in range(4):
                    last = t.matmul(pb[6][0:NS, :], wcmT[:, gg, :], xs[:, gg, sl],
                                    start=(gg == 0), stop=(gg == 3))
                inc(last, "pe", 1)
            mark(f"prjmm{c}_{half}", "pe")

            @block.vector
            def _(v, c=c, half=half):
                waitm(v, f"prjmm{c}_{half}")
                if c > 0 and half == 0:
                    v.wait_ge(sems["sst"], marks[f"arst{c - 1}"]["sst"])
                sl = slice(half * 512, (half + 1) * 512)
                v.tensor_copy(prj_s[:, sl], pb[4][:, :])
                v.tensor_copy(prj_bc[0:NS, sl], pb[5][0:NS, :])
                a = v.tensor_copy(prj_bc[32:32 + NS, sl], pb[6][0:NS, :])
                inc(a, "dv", 1)
            mark(f"prjcp{c}_{half}", "dv")
            for b in (4, 5, 6):
                bank_release(b, "dv")

        @block.scalar
        def _(sc, c=c):
            sc.wait_ge(sems["dv"], marks[f"prjcp{c}_1"]["dv"])
            if c == 0:
                waitm(sc, "front_free")
            DMA(sc, "sst", ari_p.ap()[c, 0:128, :], prj_s[:, :])
            DMA(sc, "sst", ari_p.ap()[c, 128:144, :], prj_bc[0:NS, :])
            DMA(sc, "sst", ari_p.ap()[c, 144:160, :], prj_bc[32:32 + NS, :])
        mark(f"arst{c}", "sst")

        @block.gpsimd
        def _(g, c=c):
            g.wait_ge(sems["sst"], marks[f"arst{c}"]["sst"])
            c1 = g.collective_compute("AllReduce", AL.add, replica_groups=RG,
                                      ins=[ari_p.ap()[c].opt()],
                                      outs=[aro_p.ap()[c].opt()])
            inc(c1, "cc", 1)
        mark(f"ar{c}", "cc")

    def scan_back(c):
        xpar = c % 2
        xs = xsilu[:, xpar]

        @block.gpsimd
        def _(g, c=c):
            waitm(g, f"ar{c}")
            if c > 0:
                g.wait_ge(sems["pe"], marks[f"dtmm{c - 1}_3_1"]["pe"])
            DMA(g, "gld", dt1f[:, :], aro_p.ap()[c, 0:128, :])
            DMA(g, "gld", bct_d.ap()[c], aro_p.ap()[c, 128:160, :])
        mark(f"arp{c}", "gld")

        for half in range(2):
            for gg in range(4):
                bk = 6 + (gg * 2 + half) % 2

                @block.tensor
                def _(t, c=c, gg=gg, half=half, bk=bk):
                    waitm(t, f"arp{c}")
                    pe_take_bank(t, bk)
                    last = t.matmul(pb[bk][:, :], wd2T[:, gg, :],
                                    dt1f[:, half * 512:(half + 1) * 512],
                                    start=True, stop=True)
                    inc(last, "pe", 1)
                mark(f"dtmm{c}_{gg}_{half}", "pe")

                @block.vector
                def _(v, c=c, gg=gg, half=half, bk=bk):
                    waitm(v, f"dtmm{c}_{gg}_{half}")
                    if gg == 0:
                        pm = marks.get(f"dtsp{c}_{half - 1}") or \
                            marks.get(f"dtsp{c - 1}_1")
                        if pm:
                            v.wait_ge(sems["dv"], pm["dv"])
                    a = v.tensor_scalar(sp0[:, gg, :], pb[bk][:, :],
                                        bd2_s[:, gg:gg + 1], None, AL.add)
                    inc(a, "dv", 1)
                    bank_release(bk, "dv")
            mark(f"dtpre{c}_{half}", "dv")

            @block.scalar
            def _(sc, c=c, half=half):
                waitm(sc, f"dtpre{c}_{half}")
                s0 = sp0.ap().rearrange("p g t -> p (g t)")
                s1 = sp1.ap().rearrange("p g t -> p (g t)")
                a = sc.activation(s1, s0, AF.Abs)
                inc(a, "sc", 1)
                wait_cur(sc, "sc")
                a = sc.activation(s1, s1, AF.Exp, scale=cst[:, 5:6])
                inc(a, "sc", 1)
                wait_cur(sc, "sc")
                a = sc.activation(s1, s1, AF.Ln, bias=cst[:, 6:7])
                inc(a, "sc", 1)
            mark(f"dtln{c}_{half}", "sc")

            @block.vector
            def _(v, c=c, half=half):
                waitm(v, f"dtln{c}_{half}")
                if c > 0 and half == 0:
                    v.wait_ge(sems["dv"], marks[f"scanl{c - 1}"]["dv"])
                sl = slice(half * 512, (half + 1) * 512)
                a = v.scalar_tensor_tensor(
                    dt_t[:, :, sl], sp0[:, :, :], 0.0, sp1[:, :, :],
                    AL.max, AL.add)
                inc(a, "dv", 1)
            mark(f"dtsp{c}_{half}", "dv")
        mark(f"dtsp{c}", "dv")

        for half in range(2):
            sl = slice(half * 512, (half + 1) * 512)

            @block.scalar
            def _(sc, c=c, half=half):
                waitm(sc, f"arp{c}")
                pm = marks.get(f"scanh{c}_{half - 1}") or \
                    marks.get(f"scanh{c - 1}_1")
                if pm:
                    sc.wait_ge(sems["dv"], pm["dv"])
                DMA(sc, "sld", bbc[:, :, :],
                    bct_d.ap()[c, 0:NS, sl][None, :, :]
                    .broadcast_to((128, NS, 512)))
                DMA(sc, "sld", cbc[:, :, :],
                    bct_d.ap()[c, NS:2 * NS, sl][None, :, :]
                    .broadcast_to((128, NS, 512)))
            mark(f"bcld{c}_{half}", "sld")

            for gg in range(4):
                @block.vector
                def _(v, c=c, half=half, gg=gg, sl=sl, xs=xs):
                    if gg == 0:
                        waitm(v, f"bcld{c}_{half}")
                        waitm(v, f"dtsp{c}_{half}")
                        if half == 0:
                            pm = marks.get(f"bwdst{c - 1}")
                            if pm:
                                v.wait_ge(sems["sst"], pm["sst"])
                    a = v.tensor_tensor(dub[:, gg % 2, :], dt_t[:, gg, sl],
                                        xs[:, gg, sl], AL.mult)
                    inc(a, "dv", 1)
                mark(f"du{c}_{half}_{gg}", "dv")

                for nh in range(2):
                    @block.scalar
                    def _(sc, c=c, half=half, gg=gg, sl=sl, nh=nh):
                        if gg == 0 and half == 0 and nh == 0:
                            waitm(sc, f"dtsp{c}")
                            if c == 0:
                                waitm(sc, "abz")
                        pm = marks.get(f"scne{c}_{half}_{gg - 1}") or \
                            marks.get(f"scne{c}_{half - 1}_3") or \
                            marks.get(f"scne{c - 1}_1_3")
                        if pm:
                            sc.wait_ge(sems["dv"], pm["dv"])
                        for n in range(8 * nh, 8 * nh + 8):
                            a = sc.activation(abuf[:, n, 0:512], dt_t[:, gg, sl],
                                              AF.Exp,
                                              scale=a_neg[:, gg * 16 + n:gg * 16 + n + 1])
                        inc(a, "sc", 1)
                    mark(f"aex{c}_{half}_{gg}_{nh}", "sc")

                @block.vector
                def _(v, c=c, half=half, gg=gg, sl=sl):
                    waitm(v, f"aex{c}_{half}_{gg}_0", f"du{c}_{half}_{gg}")
                    a = v.tensor_tensor(bbuf[:, 0:8, 0:512],
                                        dub[:, gg % 2, :][:, None, :]
                                        .broadcast_to((128, 8, 512)),
                                        bbc[:, 0:8, :], AL.mult)
                    inc(a, "dv", 1)
                    waitm(v, f"aex{c}_{half}_{gg}_1")
                    a = v.tensor_tensor(bbuf[:, 8:16, 0:512],
                                        dub[:, gg % 2, :][:, None, :]
                                        .broadcast_to((128, 8, 512)),
                                        bbc[:, 8:16, :], AL.mult)
                    inc(a, "dv", 1)
                    a = v.tensor_copy(bbuf[:, 0:15, 512:513],
                                      carry_h[:, gg * 16 + 1: gg * 16 + 16]
                                      [:, :, None])
                    inc(a, "dv", 1)
                    wait_cur(v, "dv")
                    a = v.tensor_tensor_scan(
                        hprod.ap().rearrange("p n t -> p (n t)"),
                        abuf.ap().rearrange("p n t -> p (n t)"),
                        bbuf.ap().rearrange("p n t -> p (n t)"),
                        carry_h[:, gg * 16: gg * 16 + 1],
                        AL.mult, AL.add)
                    inc(a, "dv", 1)
                    mark(f"scne{c}_{half}_{gg}", "dv")
                    wait_cur(v, "dv")
                    a = v.tensor_copy(carry_h[:, gg * 16: gg * 16 + 16],
                                      hprod[:, :, 511:512]
                                      .rearrange("p n one -> p (n one)"))
                    inc(a, "dv", 1)
                    wait_cur(v, "dv")
                    a = v.tensor_tensor(hprod[:, :, 0:512], hprod[:, :, 0:512],
                                        cbc[:, :, :], AL.mult)
                    inc(a, "dv", 1)
                    mark(f"scnh{c}_{half}_{gg}_0", "dv")
                    mark(f"scnh{c}_{half}_{gg}_1", "dv")
                    wait_cur(v, "dv")
                    a = v.tensor_tensor(hprod[:, 0:8, 0:512], hprod[:, 0:8, 0:512],
                                        hprod[:, 8:16, 0:512], AL.add)
                    inc(a, "dv", 1)
                    wait_cur(v, "dv")
                    a = v.tensor_tensor(hprod[:, 0:4, 0:512], hprod[:, 0:4, 0:512],
                                        hprod[:, 4:8, 0:512], AL.add)
                    inc(a, "dv", 1)
                    wait_cur(v, "dv")
                    a = v.tensor_tensor(hprod[:, 0:2, 0:512], hprod[:, 0:2, 0:512],
                                        hprod[:, 2:4, 0:512], AL.add)
                    inc(a, "dv", 1)
                    wait_cur(v, "dv")
                    a = v.tensor_tensor(y_t[:, gg, sl], hprod[:, 0, 0:512],
                                        hprod[:, 1, 0:512], AL.add)
                    inc(a, "dv", 1)
                mark(f"scang{c}_{half}_{gg}", "dv")
            mark(f"scanh{c}_{half}", "dv")
        mark(f"scanl{c}", "dv")

        @block.vector
        def _(v, c=c, xs=xs):
            wait_cur(v, "dv")
            for gg in range(4):
                a = v.scalar_tensor_tensor(yo_t[:, gg, :], xs[:, gg, :],
                                           wd_s[:, gg:gg + 1], y_t[:, gg, :],
                                           AL.mult, AL.add)
                inc(a, "dv", 1)
        mark(f"ytot{c}", "dv")

        @block.scalar
        def _(sc, c=c):
            sc.wait_ge(sems["dv"], marks[f"ytot{c}"]["dv"])
            for gg in range(4):
                DMA(sc, "sst",
                    a2a_bi.ap()[c].rearrange("j (g p) t -> p g j t", p=128)[:, gg],
                    yo_t[:, gg, :].rearrange("p (j t) -> p j t", t=BLK))
        mark(f"bwdst{c}", "sst")

        @block.gpsimd
        def _(g, c=c):
            g.wait_ge(sems["sst"], marks[f"bwdst{c}"]["sst"])
            with allow_ici_all_to_all():
                c1 = g.collective_compute("AllToAll", AL.bypass, replica_groups=RG,
                                          ins=[a2a_bi.ap()[c].opt()],
                                          outs=[a2a_bo.ap()[c].opt()])
            inc(c1, "cc", 1)
        mark(f"a2ab_{c}", "cc")

    for c in range(NCH):
        scan_front(c)
        if c >= 1:
            scan_back(c - 1)
    scan_back(NCH - 1)

    if STAGE >= 4:
        mark("scan_all", "dv", "sc", "pe", "cc", "gst", "sld", "syn", "gld",
             "gp", "dbgd")
        if STAGE >= 5:
            for _tag, _agi, _ago in (("w3", agi_w3, ago_w3),
                                     ("w1", agi_w1, ago_w1),
                                     ("w2", agi_w2, ago_w2)):
                @block.gpsimd
                def _(g, _tag=_tag, _agi=_agi, _ago=_ago):
                    waitm(g, f"agi_{_tag}")
                    c1 = g.collective_compute(
                        "AllGather", AL.bypass, replica_groups=RG,
                        ins=[_agi.ap().opt()], outs=[_ago.ap().opt()])
                    inc(c1, "cc", 1)
                mark(f"ag_{_tag}", "cc")
        free_group(SCg)
        free_group(FL)
        BK = "back"
        modb = alloc(BK, "modb", [128, 4, D], BF16)
        bscr = alloc(BK, "bscr", [128, 2, D], BF16)
        fscr = alloc(BK, "fscr", [128, D], F32)
        bwk = alloc(BK, "bwk", [128, D], BF16)
        x2b = alloc(BK, "x2b", [128, 4, D], BF16)
        qsb = alloc(BK, "qsb", [128, 4, 16], F32)
        outb = alloc(BK, "outb", [128, 2, 512], F32)
        st2 = alloc(BK, "st2", [128, 32], F32)
        xqT2 = alloc(BK, "xqT2", [128, 16, 4 * BLK], BF16)
        hqT0v = xqT2.ap().rearrange("p a b -> p (a b)") \
            .rearrange("p (f j) -> p f j", j=BLK)

        for s in ("wl0", "wl1", "wl2", "wl3"):
            sem(s)

        @block.gpsimd
        def _(g):
            waitm(g, "scan_all")
            m6 = ago_mod.ap().rearrange("one (s d) -> one s d", s=6)[0]
            DMA(g, "gld", modb[:, :, :],
                m6[2:6][None, :, :].broadcast_to((128, 4, D)))
        mark("modbk", "gld")

        @block.scalar
        def _(sc):
            waitm(sc, "modbk", "scan_all")
            sc.activation(bscr[:, 0, :], modb[:, 0, :], AF.Sign)
            a = sc.activation(bscr[:, 1, :], modb[:, 3, :], AF.Sign)
            inc(a, "sc", 1)
            wait_cur(sc, "sc")
            sc.activation(modb[:, 0, :], modb[:, 0, :], AF.Abs)
            a = sc.activation(modb[:, 3, :], modb[:, 3, :], AF.Abs)
            inc(a, "sc", 1)
        mark("gsign", "sc")

        @block.vector
        def _(v):
            waitm(v, "gsign", "scan_all")
            v.tensor_scalar(modb[:, 0, :], modb[:, 0, :], 0.25, None, AL.max)
            a = v.tensor_scalar(modb[:, 3, :], modb[:, 3, :], 0.25, None, AL.max)
            inc(a, "dv", 1)
            wait_cur(v, "dv")
            v.tensor_tensor(modb[:, 0, :], modb[:, 0, :], bscr[:, 0, :], AL.mult)
            a = v.tensor_tensor(modb[:, 3, :], modb[:, 3, :], bscr[:, 1, :], AL.mult)
            inc(a, "dv", 1)
        mark("gclamp", "dv")

        for half in range(2):
            BA = f"backA{half}"
            yr4 = alloc(BA, f"yr4_{half}", [128, 4, 32, BLK], BF16)
            ssb = alloc(BA, f"ssb_{half}", [128, 32, BLK], BF16)
            mix4 = alloc(BA, f"mix4_{half}", [128, 4, D], BF16)
            wob = alloc(BA, f"wob_{half}", [128, 4, 512], BF16)

            for ci in range(4):
                c = half * 4 + ci

                @block.scalar
                def _(sc, c=c, ci=ci, half=half):
                    waitm(sc, f"a2ab_{c}")
                    if half == 0 and ci == 0:
                        waitm(sc, "gclamp")
                    if half == 1 and ci == 0:
                        waitm(sc, f"bkdone_0")
                    if ci > 0:
                        sc.wait_ge(sems["dv"], marks[f"yf{half}_{ci - 1}"]["dv"])
                    for gg in range(4):
                        DMA(sc, "sld",
                            yr4[:, ci, gg:32:4, :],
                            a2a_bo.ap()[c].rearrange("j (g p) t -> p g j t",
                                                     p=128)[:, gg])
                    DMA(sc, "sld", ssb[:, :, :],
                        skip_d.ap()[:, :, c * BLK:(c + 1) * BLK]
                        .rearrange("e p t -> p e t"))
                mark(f"yrl{half}_{ci}", "sld")

                @block.scalar
                def _(sc, ci=ci, half=half):
                    waitm(sc, f"yrl{half}_{ci}")
                    a = sc.activation(ssb.ap().rearrange("p e t -> p (e t)"),
                                      ssb.ap().rearrange("p e t -> p (e t)"),
                                      AF.Silu)
                    inc(a, "sc", 1)
                mark(f"ss{half}_{ci}", "sc")

                @block.vector
                def _(v, ci=ci, half=half):
                    waitm(v, f"ss{half}_{ci}")
                    a = v.tensor_tensor(
                        yr4.ap()[:, ci].rearrange("p e t -> p (e t)"),
                        yr4.ap()[:, ci].rearrange("p e t -> p (e t)"),
                        ssb.ap().rearrange("p e t -> p (e t)"), AL.mult)
                    inc(a, "dv", 1)
                mark(f"yf{half}_{ci}", "dv")

            for dgrp in range(4):
                for ec in range(32):
                    wl = f"wl{ec % 4}"

                    @block.scalar
                    def _(sc, dgrp=dgrp, ec=ec, wl=wl, half=half):
                        gi = dgrp * 32 + ec - 4
                        if gi >= 0:
                            pm = marks.get(f"womm{half}_{gi // 32}_{gi % 32}")
                            if pm:
                                sc.wait_ge(sems["pe"], pm["pe"])
                        DMA(sc, wl, wob[:, ec % 4, :],
                            ago_wo.ap()[ec * 128:(ec + 1) * 128,
                                        dgrp * 512:(dgrp + 1) * 512])
                        mark(f"wol{half}_{dgrp}_{ec}", wl)

                    @block.tensor
                    def _(t, dgrp=dgrp, ec=ec, half=half):
                        waitm(t, f"wol{half}_{dgrp}_{ec}")
                        if ec == 0:
                            for ci in range(4):
                                waitm(t, f"yf{half}_{ci}")
                                pe_take_bank(t, 4 * (dgrp % 2) + ci)
                        for ci in range(4):
                            last = t.matmul(pb[4 * (dgrp % 2) + ci][:, :],
                                            yr4[:, ci, ec, :],
                                            wob[:, ec % 4, :],
                                            start=(ec == 0), stop=(ec == 31))
                        inc(last, "pe", 1)
                        mark(f"womm{half}_{dgrp}_{ec}", "pe")

                @block.scalar
                def _(sc, dgrp=dgrp, half=half):
                    waitm(sc, f"womm{half}_{dgrp}_31")
                    for ci in range(4):
                        a = sc.activation(mix4[:, ci, dgrp * 512:(dgrp + 1) * 512],
                                          pb[4 * (dgrp % 2) + ci][:, :], AF.Copy)
                        inc(a, "sc", 1)
                        bank_release(4 * (dgrp % 2) + ci, "sc")
                mark(f"woev{half}_{dgrp}", "sc")

            for ci in range(4):
                c = half * 4 + ci

                @block.scalar
                def _(sc, c=c, ci=ci, half=half):
                    for dgrp in range(4):
                        waitm(sc, f"woev{half}_{dgrp}")
                    if ci > 0:
                        sc.wait_ge(sems["syn"], marks[f"xqT{half}_{ci - 1}"]["syn"])
                    DMA(sc, "sld", bwk[:, :], h_sp.ap()[c, :, :])
                    DMA(sc, "sld", fscr[:, :], ti["resid"].ap()[c, :, :])
                mark(f"hrl{half}_{ci}", "sld")

                @block.vector
                def _(v, c=c, ci=ci, half=half):
                    waitm(v, f"hrl{half}_{ci}")
                    a = v.tensor_tensor(bscr[:, 0, :], mix4[:, ci, :], bwk[:, :],
                                        AL.add)
                    inc(a, "dv", 1)
                    wait_cur(v, "dv")
                    a = v.tensor_tensor(bscr[:, 0, :], bscr[:, 0, :],
                                        modb[:, 0, :], AL.mult)
                    inc(a, "dv", 1)
                    wait_cur(v, "dv")
                    a = v.tensor_tensor(x2b[:, ci, :], fscr[:, :], bscr[:, 0, :],
                                        AL.add)
                    inc(a, "dv", 1)
                mark(f"x2{half}_{ci}", "dv")

                if DEBUG and STAGE >= 4:
                    @block.gpsimd
                    def _(g, c=c, ci=ci, half=half):
                        g.wait_ge(sems["dv"], marks[f"x2{half}_{ci}"]["dv"])
                        DMA(g, "dbgd", dbg["x2"].ap()[c], x2b[:, ci, :])
                    mark(f"dbgx2{c}", "dbgd")

                @block.vector
                def _(v, ci=ci, half=half):
                    waitm(v, f"x2{half}_{ci}")
                    x = x2b[:, ci, :]
                    for s4 in range(4):
                        a = v.bn_stats(st2[:, 6 * s4:6 * (s4 + 1)],
                                       x[:, 512 * s4:512 * (s4 + 1)])
                    inc(a, "dv", 1)
                    wait_cur(v, "dv")
                    a = v.bn_aggr(st2[:, 24:26], st2[:, 0:24])
                    inc(a, "dv", 1)
                mark(f"st2{half}_{ci}", "dv")

                @block.scalar
                def _(sc, ci=ci, half=half):
                    waitm(sc, f"st2{half}_{ci}")
                    a = sc.activation(st2[:, 26:27], st2[:, 25:26], AF.Sqrt,
                                      bias=cst[:, 0:1])
                    inc(a, "sc", 1)
                mark(f"sq2{half}_{ci}", "sc")

                @block.vector
                def _(v, ci=ci, half=half):
                    waitm(v, f"sq2{half}_{ci}")
                    a = v.reciprocal(st2[:, 27:28], st2[:, 26:27])
                    inc(a, "dv", 1)
                    wait_cur(v, "dv")
                    a = v.tensor_scalar(bwk[:, :], x2b[:, ci, :], st2[:, 24:25],
                                        st2[:, 27:28], AL.subtract, AL.mult)
                    inc(a, "dv", 1)
                    wait_cur(v, "dv")
                    a = v.tensor_tensor(bscr[:, 0, :], bwk[:, :], modb[:, 2, :],
                                        AL.mult)
                    inc(a, "dv", 1)
                    wait_cur(v, "dv")
                    a = v.tensor_tensor(bwk[:, :], bwk[:, :], bscr[:, 0, :],
                                        AL.add)
                    inc(a, "dv", 1)
                    wait_cur(v, "dv")
                    a = v.tensor_tensor(bwk[:, :], bwk[:, :], modb[:, 1, :],
                                        AL.add)
                    inc(a, "dv", 1)
                    wait_cur(v, "dv")
                    a = v.reduce_max(qsb[:, ci, 0:1], bwk[:, :], axis=AX,
                                     apply_absolute_value=True)
                    inc(a, "dv", 1)
                    wait_cur(v, "dv")
                    a = v.tensor_scalar(qsb[:, ci, 0:1], qsb[:, ci, 0:1],
                                        1e-5, None, AL.max)
                    inc(a, "dv", 1)
                mark(f"h2m{half}_{ci}", "dv")

                @block.scalar
                def _(sc, ci=ci, half=half):
                    waitm(sc, f"h2m{half}_{ci}")
                    a = sc.activation(bscr[:, 1, :], bwk[:, :], AF.Square,
                                      accum_out=qsb[:, ci, 1:2])
                    inc(a, "sc", 1)
                    wait_cur(sc, "sc")
                    a = sc.activation(qsb[:, ci, 2:3], qsb[:, ci, 1:2], AF.Sqrt,
                                      bias=cst[:, 1:2], scale=cst[:, 3:4])
                    inc(a, "sc", 1)
                mark(f"h2s{half}_{ci}", "sc")

                @block.vector
                def _(v, ci=ci, half=half):
                    waitm(v, f"h2s{half}_{ci}")
                    v.reciprocal(qsb[:, ci, 3:4], qsb[:, ci, 2:3])
                    a = v.reciprocal(qsb[:, ci, 4:5], qsb[:, ci, 0:1])
                    inc(a, "dv", 1)
                    wait_cur(v, "dv")
                    v.tensor_scalar(qsb[:, ci, 5:6], qsb[:, ci, 4:5], 127.0,
                                    None, AL.mult)
                    a = v.tensor_tensor(qsb[:, ci, 6:7], qsb[:, ci, 0:1],
                                        qsb[:, ci, 3:4], AL.mult)
                    inc(a, "dv", 1)
                    wait_cur(v, "dv")
                    a = v.tensor_scalar(qsb[:, ci, 7:8], qsb[:, ci, 6:7],
                                        1.0 / 127.0, None, AL.mult)
                    inc(a, "dv", 1)
                    wait_cur(v, "dv")
                    v.tensor_scalar(qsb[:, ci, 8:9], qsb[:, ci, 7:8],
                                    wsc[:, 0:1], None, AL.mult)
                    a = v.tensor_scalar(qsb[:, ci, 9:10], qsb[:, ci, 7:8],
                                        wsc[:, 1:2], None, AL.mult)
                    inc(a, "dv", 1)
                mark(f"qx{half}_{ci}", "dv")

                @block.scalar
                def _(sc, ci=ci, half=half):
                    waitm(sc, f"qx{half}_{ci}")
                    a = sc.activation(fscr[:, :], bwk[:, :], AF.Relu,
                                      bias=cst[:, 2:3], scale=qsb[:, ci, 5:6])
                    inc(a, "sc", 1)
                mark(f"xr{half}_{ci}", "sc")

                @block.vector
                def _(v, ci=ci, half=half):
                    waitm(v, f"xr{half}_{ci}")
                    a = v.tensor_scalar(bwk[:, :], fscr[:, :], -MAGIC, None,
                                        AL.add)
                    inc(a, "dv", 1)
                mark(f"xq{half}_{ci}", "dv")

                @block.sync
                def _(s, ci=ci, half=half):
                    waitm(s, f"xq{half}_{ci}")
                    DMAT(s, "syn", xqT2[:, :, ci * BLK:(ci + 1) * BLK],
                         bwk[:, :])
                mark(f"xqT{half}_{ci}", "syn")

            free_group(BA)
            BB = f"backB{half}"
            hfb = alloc(BB, f"hfb_{half}", [128, 4, DFF], BF16)
            wfb = alloc(BB, f"wfb_{half}", [128, 2, 16, 512], BF16)
            w3b = alloc(BB, f"w3b_{half}", [128, 4, 512], BF16)

            hst = alloc(BB, f"hst_{half}", [128, 4, 192], F32)

            @block.vector
            def _(v, half=half):
                waitm(v, f"xq{half}_3")
                for dgrp in range(4):
                    waitm(v, f"woev{half}_{dgrp}")
                a = v.memset(qsb[:, :, 10:12], 0.0)
                inc(a, "dv", 1)
            mark(f"hacc0{half}", "dv")

            for fg in range(32):
                wl = f"wl{fg % 2}"

                @block.scalar
                def _(sc, fg=fg, wl=wl, half=half):
                    if fg == 0:
                        waitm(sc, f"xq{half}_3")
                        for dgrp in range(4):
                            waitm(sc, f"woev{half}_{dgrp}")
                        waitm(sc, "ag_w1", "ag_w2")
                    pm = marks.get(f"w12mm{half}_{fg - 2}")
                    if pm:
                        sc.wait_ge(sems["pe"], pm["pe"])
                    DMA(sc, wl, wfb[:, fg % 2, :, 0:256],
                        ago_w1.ap().rearrange("(k p) f -> p k f", p=128)
                        [:, :, fg * 256:(fg + 1) * 256])
                    DMA(sc, wl, wfb[:, fg % 2, :, 256:512],
                        ago_w2.ap().rearrange("(k p) f -> p k f", p=128)
                        [:, :, fg * 256:(fg + 1) * 256])
                    mark(f"wfl{half}_{fg}", wl)

                for ci in range(4):
                    it = fg * 4 + ci
                    bk = it % 4
                    bp = it % 2

                    @block.tensor
                    def _(t, fg=fg, ci=ci, bk=bk, half=half):
                        if fg == 0 and ci == 0:
                            for cj in range(4):
                                waitm(t, f"xqT{half}_{cj}")
                        waitm(t, f"wfl{half}_{fg}")
                        pe_take_bank(t, bk)
                        for k in range(16):
                            last = t.matmul(pb[bk][:, :],
                                            xqT2[:, k, ci * BLK:(ci + 1) * BLK],
                                            wfb[:, fg % 2, k, :],
                                            start=(k == 0), stop=(k == 15))
                        inc(last, "pe", 1)
                        mark(f"w12mm{half}_{fg}_{ci}", "pe")
                    if ci == 3:
                        mark(f"w12mm{half}_{fg}", "pe")

                    @block.scalar
                    def _(sc, fg=fg, ci=ci, bk=bk, bp=bp, it=it, half=half):
                        waitm(sc, f"w12mm{half}_{fg}_{ci}")
                        pm = (marks.get(f"hsec{half}_{(it - 2) // 4}_{(it - 2) % 4}")
                              if it >= 2 else
                              marks.get(f"hsec{half - 1}_31_{it + 2}"))
                        if pm:
                            sc.wait_ge(sems["dv"], pm["dv"])
                        a = sc.activation(bscr[:, 0, bp * 256:bp * 256 + 256],
                                          pb[bk][:, 0:256],
                                          AF.Silu, scale=qsb[:, ci, 8:9])
                        inc(a, "sc", 1)
                    mark(f"sil{half}_{fg}_{ci}", "sc")

                    @block.vector
                    def _(v, fg=fg, ci=ci, bk=bk, bp=bp, half=half):
                        waitm(v, f"sil{half}_{fg}_{ci}")
                        a = v.scalar_tensor_tensor(
                            hfb[:, ci, fg * 256:(fg + 1) * 256],
                            pb[bk][:, 256:512], qsb[:, ci, 9:10],
                            bscr[:, 0, bp * 256:bp * 256 + 256],
                            AL.mult, AL.mult)
                        inc(a, "dv", 1)
                        bank_release(bk, "dv")
                        wait_cur(v, "dv")
                        a = v.reduce_max(qsb[:, ci, 12:13],
                                         hfb[:, ci, fg * 256:(fg + 1) * 256],
                                         axis=AX, apply_absolute_value=True)
                        inc(a, "dv", 1)
                        v.bn_stats(hst[:, ci, 6 * fg:6 * (fg + 1)],
                                   hfb[:, ci, fg * 256:(fg + 1) * 256])
                        wait_cur(v, "dv")
                        a = v.tensor_tensor(qsb[:, ci, 10:11], qsb[:, ci, 10:11],
                                            qsb[:, ci, 12:13], AL.max)
                        inc(a, "dv", 1)
                    mark(f"hsec{half}_{fg}_{ci}", "dv")
                    if fg == 31:
                        mark(f"hss{half}_31_{ci}", "dv")

            hqT_loc = {}
            for ci in range(4):
                @block.vector
                def _(v, ci=ci, half=half):
                    v.wait_ge(sems["dv"], marks[f"hss{half}_31_{ci}"]["dv"])
                    a = v.bn_aggr(qsb[:, ci, 14:16], hst[:, ci, :])
                    inc(a, "dv", 1)
                    wait_cur(v, "dv")
                    a = v.tensor_tensor(qsb[:, ci, 11:12], qsb[:, ci, 14:15],
                                        qsb[:, ci, 14:15], AL.mult)
                    inc(a, "dv", 1)
                    wait_cur(v, "dv")
                    a = v.tensor_tensor(qsb[:, ci, 11:12], qsb[:, ci, 11:12],
                                        qsb[:, ci, 15:16], AL.add)
                    inc(a, "dv", 1)
                mark(f"hag{half}_{ci}", "dv")

                @block.scalar
                def _(sc, ci=ci, half=half):
                    sc.wait_ge(sems["dv"], marks[f"hag{half}_{ci}"]["dv"])
                    a = sc.activation(qsb[:, ci, 14:15], qsb[:, ci, 11:12],
                                      AF.Sqrt, bias=cst[:, 1:2], scale=cst[:, 6:7])
                    inc(a, "sc", 1)
                mark(f"hq1{half}_{ci}", "sc")

                @block.vector
                def _(v, ci=ci, half=half):
                    waitm(v, f"hq1{half}_{ci}")
                    a = v.tensor_scalar(qsb[:, ci, 10:11], qsb[:, ci, 10:11],
                                        1e-5, None, AL.max)
                    inc(a, "dv", 1)
                    wait_cur(v, "dv")
                    v.reciprocal(qsb[:, ci, 15:16], qsb[:, ci, 14:15])
                    a = v.reciprocal(qsb[:, ci, 12:13], qsb[:, ci, 10:11])
                    inc(a, "dv", 1)
                    wait_cur(v, "dv")
                    v.tensor_scalar(qsb[:, ci, 12:13], qsb[:, ci, 12:13],
                                    127.0, None, AL.mult)
                    a = v.tensor_tensor(qsb[:, ci, 13:14], qsb[:, ci, 10:11],
                                        qsb[:, ci, 15:16], AL.mult)
                    inc(a, "dv", 1)
                    wait_cur(v, "dv")
                    a = v.tensor_scalar(qsb[:, ci, 13:14], qsb[:, ci, 13:14],
                                        1.0 / 127.0, None, AL.mult)
                    inc(a, "dv", 1)
                    wait_cur(v, "dv")
                    a = v.tensor_scalar(qsb[:, ci, 13:14], qsb[:, ci, 13:14],
                                        wsc[:, 2:3], None, AL.mult)
                    inc(a, "dv", 1)
                mark(f"hq2{half}_{ci}", "dv")

                for piece in range(4):
                    @block.scalar
                    def _(sc, ci=ci, piece=piece, half=half):
                        waitm(sc, f"hq2{half}_{ci}")
                        pm = marks.get(f"hrq{half}_{ci}_{piece - 1}") or \
                            (marks.get(f"hrq{half}_{ci - 1}_3") if piece == 0 else None)
                        if pm:
                            sc.wait_ge(sems["dv"], pm["dv"])
                        a = sc.activation(fscr[:, :],
                                          hfb[:, ci, piece * D:(piece + 1) * D],
                                          AF.Relu, bias=cst[:, 2:3],
                                          scale=qsb[:, ci, 12:13])
                        inc(a, "sc", 1)
                    mark(f"hrr{half}_{ci}_{piece}", "sc")

                    @block.vector
                    def _(v, ci=ci, piece=piece, half=half):
                        waitm(v, f"hrr{half}_{ci}_{piece}")
                        a = v.tensor_scalar(hfb[:, ci, piece * D:(piece + 1) * D],
                                            fscr[:, :], -MAGIC, None, AL.add)
                        inc(a, "dv", 1)
                    mark(f"hrq{half}_{ci}_{piece}", "dv")

                @block.sync
                def _(s, ci=ci, half=half):
                    waitm(s, f"hrq{half}_{ci}_3")
                    if ci > 0:
                        waitm(s, f"hqT{half}_{ci - 1}")
                    if ci == 0:
                        s.wait_ge(sems["pe"], marks[f"w12mm{half}_31"]["pe"])
                        dst = hqT0v
                    else:
                        dst = hfb.ap()[:, ci - 1].rearrange("p (f j) -> p f j",
                                                            j=BLK)
                    DMAT(s, "syn", dst, hfb.ap()[:, ci])
                mark(f"hqT{half}_{ci}", "syn")
                hqT_loc[ci] = ci

            for dgrp in range(4):
                for fk in range(64):
                    wl = f"wl{fk % 4}"

                    @block.scalar
                    def _(sc, dgrp=dgrp, fk=fk, wl=wl, half=half):
                        if dgrp == 0 and fk == 0:
                            waitm(sc, f"xq{half}_3")
                            waitm(sc, "ag_w3")
                        gi = dgrp * 64 + fk - 4
                        if gi >= 0:
                            pm = marks.get(f"w3mm{half}_{gi // 64}_{gi % 64}")
                            if pm:
                                sc.wait_ge(sems["pe"], pm["pe"])
                        DMA(sc, wl, w3b[:, fk % 4, :],
                            ago_w3.ap()[fk * 128:(fk + 1) * 128,
                                        dgrp * 512:(dgrp + 1) * 512])
                        mark(f"w3l{half}_{dgrp}_{fk}", wl)

                    @block.tensor
                    def _(t, dgrp=dgrp, fk=fk, half=half):
                        waitm(t, f"w3l{half}_{dgrp}_{fk}")
                        if fk == 0:
                            if dgrp == 0:
                                for cj in range(4):
                                    waitm(t, f"hqT{half}_{cj}")
                            for ci in range(4):
                                pe_take_bank(t, 4 + ci)
                        for ci in range(4):
                            src = hqT0v if ci == 0 else \
                                hfb.ap()[:, ci - 1].rearrange("p (f j) -> p f j",
                                                              j=BLK)
                            last = t.matmul(pb[4 + ci][:, :],
                                            src[:, fk, :],
                                            w3b[:, fk % 4, :],
                                            start=(fk == 0), stop=(fk == 63))
                        inc(last, "pe", 1)
                        mark(f"w3mm{half}_{dgrp}_{fk}", "pe")

                for ci in range(4):
                    c = half * 4 + ci

                    @block.vector
                    def _(v, dgrp=dgrp, ci=ci, half=half):
                        waitm(v, f"w3mm{half}_{dgrp}_63")
                        pm = marks.get(f"oev{half}_{dgrp}_{ci - 1}") or \
                            (marks.get(f"oev{half}_{dgrp - 1}_3") if ci == 0 else None)
                        if pm:
                            v.wait_ge(sems["outd"], pm["outd"])
                        a = v.tensor_scalar(bscr[:, 0, 0:512], pb[4 + ci][:, :],
                                            qsb[:, ci, 13:14], None, AL.mult)
                        inc(a, "dv", 1)
                        wait_cur(v, "dv")
                        a = v.tensor_tensor(bscr[:, 0, 0:512], bscr[:, 0, 0:512],
                                            modb[:, 3, dgrp * 512:(dgrp + 1) * 512],
                                            AL.mult)
                        inc(a, "dv", 1)
                        wait_cur(v, "dv")
                        a = v.tensor_tensor(outb[:, ci % 2, :],
                                            x2b[:, ci, dgrp * 512:(dgrp + 1) * 512],
                                            bscr[:, 0, 0:512], AL.add)
                        inc(a, "dv", 1)
                    mark(f"ow{half}_{dgrp}_{ci}", "dv")

                    @block.scalar
                    def _(sc, dgrp=dgrp, ci=ci, c=c, half=half):
                        sc.wait_ge(sems["dv"], marks[f"ow{half}_{dgrp}_{ci}"]["dv"])
                        DMA(sc, "outd", out_t.ap()[c, :, dgrp * 512:(dgrp + 1) * 512],
                            outb[:, ci % 2, :])
                    mark(f"oev{half}_{dgrp}_{ci}", "outd")
                for ci in range(4):
                    bank_release(4 + ci, "dv")

            free_group(BB)
            mark(f"bkdone_{half}", "dv", "sc", "pe", "syn", "sld", "outd",
                 "wl0", "wl1", "gld", "dbgd")

    if STAGE < 5:
        @block.scalar
        def _(sc):
            for c in range(NCH):
                DMA(sc, "outd", out_t.ap()[c], ti["resid"].ap()[c])

    @block.sync
    def _(s):
        wait_cur(s, "outd")
        if DEBUG:
            wait_cur(s, "dbgd")
        wait_cur(s, "cc")
        wait_cur(s, "gst")
        wait_cur(s, "sld")
        wait_cur(s, "gld")
    return nc


def kernel(L0_attn_resid, L0_attn_norm_w, L0_attn_W_skip, L0_attn_W_in,
           L0_attn_conv_w, L0_attn_conv_b, L0_attn_W_d1, L0_attn_W_d2,
           L0_attn_b_d2, L0_attn_W_Bm, L0_attn_W_Cm, L0_attn_A_log,
           L0_attn_W_D, L0_attn_W_out, L0_ffn_w1, L0_ffn_w2, L0_ffn_w3,
           L0_c, L0_wmod):
    resid = np.ascontiguousarray(np.asarray(L0_attn_resid, np.float32)[0])
    in_maps = []
    for i in range(NCORES):
        es = slice(ES * i, ES * (i + 1))
        m = {
            "resid": np.ascontiguousarray(resid.reshape(NCH, NCORES, BLK, D)[:, i]),
            "cvec": np.ascontiguousarray(np.asarray(L0_c, np.float32).reshape(16, 128).T),
            "wmod_sl": np.ascontiguousarray(np.asarray(L0_wmod, np.float32)[:, 1536 * i:1536 * (i + 1)]),
            "norm_w": np.asarray(L0_attn_norm_w, np.float32).reshape(1, D),
            "conv_w": np.ascontiguousarray(
                np.asarray(L0_attn_conv_w, np.float32)[es, 0, :]
                .reshape(4, 128, 4).transpose(1, 0, 2).reshape(128, 16)),
            "conv_b": np.ascontiguousarray(
                np.asarray(L0_attn_conv_b, np.float32)[es].reshape(4, 128).T),
            "A_log": np.ascontiguousarray(
                np.asarray(L0_attn_A_log, np.float32)[es]
                .reshape(4, 128, NS).transpose(1, 0, 2).reshape(128, 64)),
            "W_D": np.ascontiguousarray(
                np.asarray(L0_attn_W_D, np.float32)[es].reshape(4, 128).T),
            "b_d2": np.ascontiguousarray(
                np.asarray(L0_attn_b_d2, np.float32)[es].reshape(4, 128).T),
            "W_d1_sl": np.ascontiguousarray(np.asarray(L0_attn_W_d1, np.float32)[:, es]),
            "W_d2_sl": np.ascontiguousarray(np.asarray(L0_attn_W_d2, np.float32)[es, :]),
            "W_Bm_sl": np.ascontiguousarray(np.asarray(L0_attn_W_Bm, np.float32)[:, es]),
            "W_Cm_sl": np.ascontiguousarray(np.asarray(L0_attn_W_Cm, np.float32)[:, es]),
            "W_in_sl": np.ascontiguousarray(np.asarray(L0_attn_W_in, np.float32)[:, DSH * i:DSH * (i + 1)]),
            "W_skip_sl": np.ascontiguousarray(np.asarray(L0_attn_W_skip, np.float32)[:, DSH * i:DSH * (i + 1)]),
            "W_out_sl": np.ascontiguousarray(np.asarray(L0_attn_W_out, np.float32)[:, es]),
            "w1_sl": np.ascontiguousarray(np.asarray(L0_ffn_w1, np.float32)[:, DSH * i:DSH * (i + 1)]),
            "w2_sl": np.ascontiguousarray(np.asarray(L0_ffn_w2, np.float32)[:, DSH * i:DSH * (i + 1)]),
            "w3_sl": np.ascontiguousarray(np.asarray(L0_ffn_w3, np.float32)[:, FSH * i:FSH * (i + 1)]),
        }
        in_maps.append(m)
    nc = build_program()
    res = run_bass_kernel_spmd(nc, in_maps, core_ids=list(range(NCORES)))
    out = np.zeros((L, D), np.float32)
    ov = out.reshape(NCH, NCORES, BLK, D)
    for i in range(NCORES):
        ov[:, i] = res.results[i]["out"]
    if DEBUG:
        kernel.dbg = res.results
    return out.reshape(1, L, D)
